# version 07
# baseline (speedup 1.0000x reference)
import numpy as np
from contextlib import ExitStack

import concourse.bass as bass
import concourse.mybir as mybir
from concourse.bass_utils import run_bass_kernel_spmd


def _enable_ldw_opt():
    try:
        from concourse.compiler_utils import get_compiler_flags, set_compiler_flags
        flags = get_compiler_flags()
        newflags = [f.replace("--enable-ldw-opt=false", "--enable-ldw-opt=true")
                    for f in flags]
        set_compiler_flags(newflags)
    except Exception:
        pass

F32 = mybir.dt.float32
BF16 = mybir.dt.bfloat16
AF = mybir.ActivationFunctionType
ALU = mybir.AluOpType
AXX = mybir.AxisListType.X

C = 2048
D = 128
HPC = 4
NE = 8
NCORES = 8

SEMS = (["pe", "act", "dve", "scst", "sxnw", "sA", "swat", "scmb", "scc",
         "sgin", "gpc"]
        + ["sxb%d" % i for i in range(4)] + ["sxw%d" % i for i in range(2)]
        + ["swf%d" % i for i in range(2)] + ["swd%d" % i for i in range(2)]
        + ["sag%d" % i for i in range(2)] + ["syt%d" % i for i in range(4)]
        + ["sepx%d" % i for i in range(2)] + ["sepa%d" % i for i in range(2)]
        + ["sepy%d" % i for i in range(2)] + ["sout%d" % i for i in range(2)]
        + ["sloc", "swb"])


class Ctx:

    def __init__(self, nc, mode, h, marks):
        self.nc = nc
        self.mode = mode
        self.h = h
        self.marks = marks
        self.cnt = {k: 0 for k in SEMS}
        self.pidval = 0

    def mk(self, key, sem):
        v = self.cnt[sem]
        if self.mode == "count":
            self.marks[key] = v
        else:
            assert self.marks[key] == v, (key, self.marks[key], v, self.mode)
        return v

    def m(self, key):
        return self.marks[key]

    def w(self, eng, sem, val):
        if self.mode != eng:
            return
        if isinstance(val, (tuple, str)):
            val = self.marks[val]
        if val is None or val <= 0:
            return
        self._eng(eng).wait_ge(self.h[sem], val)

    def _eng(self, eng):
        nc = self.nc
        return {"pe": nc.tensor, "act": nc.scalar, "dve": nc.vector,
                "sync": nc.sync, "gp": nc.gpsimd}[eng]

    def _fin(self, inst, sem, amount):
        self.cnt[sem] += amount
        if inst is not None:
            inst.then_inc(self.h[sem], amount)

    def mm(self, out, lhsT, rhs, start=True, stop=True, inc=False):
        inst = None
        if self.mode == "pe":
            inst = self.nc.tensor.matmul(out, lhsT, rhs, start=start, stop=stop)
        if inc:
            self._fin(inst, "pe", 1)

    def tp(self, out, in_, idn, inc=False):
        inst = None
        if self.mode == "pe":
            inst = self.nc.tensor.transpose(out, in_, idn)
        if inc:
            self._fin(inst, "pe", 1)

    def act(self, out, in_, func, bias=0.0, scale=1.0, accum=None, inc=False):
        inst = None
        if self.mode == "act":
            kw = {}
            if accum is not None:
                kw["accum_out"] = accum
            inst = self.nc.scalar.activation(out, in_, func, bias=bias, scale=scale, **kw)
        if inc:
            self._fin(inst, "act", 1)

    def acopy(self, out, in_, inc=False):
        inst = None
        if self.mode == "act":
            inst = self.nc.scalar.copy(out, in_)
        if inc:
            self._fin(inst, "act", 1)

    def amul(self, out, in_, mulv, inc=False):
        inst = None
        if self.mode == "act":
            inst = self.nc.scalar.mul(out, in_, mulv)
        if inc:
            self._fin(inst, "act", 1)

    def dve(self, meth, *args, inc=False, **kw):
        inst = None
        if self.mode == "dve":
            inst = getattr(self.nc.vector, meth)(*args, **kw)
        if inc:
            self._fin(inst, "dve", 1)

    def dma(self, queue, out, in_, sem, accum=None):
        inst = None
        if self.mode == queue:
            if callable(out):
                out = out()
            if callable(in_):
                in_ = in_()
            eng = self._eng(queue)
            try:
                if accum is None:
                    inst = eng.dma_start(out=out, in_=in_)
                else:
                    inst = eng.dma_start(out=out, in_=in_, accum_op=accum)
            except Exception as e:
                import traceback
                print("DMA FAIL: out=", out, " in=", in_)
                tb = e.__traceback__
                while tb is not None:
                    print("  frame:", tb.tb_frame.f_code.co_filename, tb.tb_lineno,
                          tb.tb_frame.f_code.co_name)
                    tb = tb.tb_next
                raise
        self._fin(inst, sem, 16)

    def coll(self, kind, op, groups, ins, outs, sem):
        inst = None
        if self.mode == "gp":
            inst = self.nc.gpsimd.collective_compute(kind, op, replica_groups=groups,
                                                     ins=ins, outs=outs)
        self.cnt[sem] += 1
        if inst is not None:
            inst.then_inc(self.h[sem])

    def gmemset(self, ap, val, sem=None):
        inst = None
        if self.mode == "gp":
            inst = self.nc.gpsimd.memset(ap, val)
        if sem is not None:
            self._fin(inst, sem, 1)


UPTO = 9


def _finish_stub(ctx, T):
    OWN = T // 4
    out_d = ctx.h["out_d"]
    B = ctx.h["B"]
    for e in ["pe", "act", "dve"]:
        for o in ["pe", "act", "dve"]:
            if o != e:
                ctx.w(e, o, ctx.cnt[o])
    for i in range(OWN // 128):
        ctx.dma("sync", out_d[128 * i:128 * i + 128, :],
                B[:, 0:4096].bitcast(mybir.dt.float32), "sout0")
    ctx.w("sync", "sout0", ctx.cnt["sout0"])


def emit_all(ctx, T):
    h = ctx.h
    OWN = T // 4
    NW = T // 512
    OWNW = OWN // 512
    NSUB = OWN // 512
    NQB = T // 512
    NPAIRS = NQB // 2
    NEPI = OWN // 256
    NT_OWN = OWN // 128

    A = h["A"]; B = h["B"]; Cf = h["C"]
    Cb = Cf[:, :].bitcast(BF16)
    pf = h["pf"]; pb = h["pb"]
    idn_f = h["idn_f"]; idn_b = h["idn_b"]
    ones_r = h["ones_r"]; ones_c = h["ones_c"]; ones_cf = h["ones_cf"]
    stats = h["stats"]; rk_all = h["rk_all"]; maskb = h["maskb"]
    natau = h["natau"]; epsln = h["epsln"]
    tau11 = h["tau11"]; nu11 = h["nu11"]
    gw_sb = h["gw_sb"]; wq_sb = h["wq_sb"]; wk_sb = h["wk_sb"]; wv_sb = h["wv_sb"]

    x_d = h["x_d"]; wqT_d = h["wqT_d"]; wkT_d = h["wkT_d"]; wvT_d = h["wvT_d"]
    tau_d = h["tau_d"]; nu_d = h["nu_d"]; gw_d = h["gw_d"]
    wgu_d = h["wgu_d"]; wdn_d = h["wdn_d"]; wsgu_d = h["wsgu_d"]; wsdn_d = h["wsdn_d"]
    cmask_d = h["cmask_d"]; idn_d = h["idn_d"]
    xnt_d = h["xnt_d"]; agin_d = h["agin_d"]; agout_d = h["agout_d"]
    combd_d = h["combd_d"]; ytacc_d = h["ytacc_d"]; out_d = h["out_d"]
    agloc_d = h["agloc_d"]; xloc_d = h["xloc_d"]
    wgu_b = h["wgu_b"]; wdn_b = h["wdn_b"]; wsgu_b = h["wsgu_b"]; wsdn_b = h["wsdn_b"]
    groups = h["groups"]

    ib = lambda off, n: B[:, off:off + n]
    ic = lambda off, n: Cf[:, off:off + n]
    icb = lambda f32off, n: Cb[:, 2 * f32off:2 * f32off + n]

    XB = [0, 2048, 4096, 6144]; TRASH = 8192
    PT = [0, 256, 512, 768]
    CSB = [1024, 1536]
    ASTG = [2048, 2560]
    RQ, RV, RK, CRW = 3072, 3584, 4096, 4608
    Q2, K2, V2, VTMP = 5632, 5888, 6144, 6400
    RQS, RVS = 6656, 7168
    AST2 = [7680, 8192]
    SG = [0, 512]
    STG = [1024, 1536, 2048, 2560]
    CBW = 3072
    LTOK, STOK, CMBT = 5120, 5248, 5376
    LSB = 5504
    CROW2 = 5504
    CMBS = 0
    EX = [0, 2048]; EO = [4096, 6144]
    EAG = [8192, 8448]; EYT = [8704, 8960]; ES1 = [9216, 9472]

    QT, KT, VT = 0, T, 2 * T
    XW = [3 * T, 3 * T + 8192]
    STAGE1 = 0
    XN1 = [8192, 10240, 12288, 14336]
    WGU = [0, 4096]
    WDN = [8192, 8192 + 5632]
    ACT4 = 8192 + 2 * 5632

    ctx.dma("sync", h["idn_f"], idn_d[:, :], "scst")
    ctx.dma("sync", h["tau11"], tau_d[:].unsqueeze(0), "scst")
    ctx.dma("sync", h["nu11"], nu_d[:].unsqueeze(0), "scst")
    ctx.mk(("p0sx"), "scst")
    ctx.dma("gp", h["idn_b"], idn_d[:, :], "sgin")
    ctx.dma("gp", maskb[:, :].rearrange("p (d c) -> p d c", d=4), cmask_d[:, :, :], "sgin")
    ctx.dma("gp", gw_sb[:, :].rearrange("p (kc e) -> p kc e", kc=16),
            gw_d[:, :, :].rearrange("kc p e -> p kc e"), "sgin")
    ctx.mk("p0gp", "sgin")
    ctx.gmemset(ones_r, 1.0, sem="gpc")
    ctx.gmemset(ones_c, 1.0, sem="gpc")
    ctx.gmemset(ones_cf, 1.0, sem="gpc")
    ctx.gmemset(epsln, 1.0e-5, sem="gpc")
    ctx.mk("p0gpc", "gpc")

    ctx.w("act", "scst", ("p0sx"))
    ctx.act(stats[0:1, 60:61], tau11, AF.Abs, inc=True)
    ctx.mk("a_abs", "act")
    ctx.w("act", "act", ("a_abs"))
    ctx.amul(stats[0:1, 61:62], stats[0:1, 60:61], -1.0, inc=True)
    ctx.mk("a_neg", "act")
    ctx.w("pe", "act", ("a_neg"))
    ctx.w("pe", "sgin", ("p0gp"))
    ctx.w("pe", "gpc", ("p0gpc"))
    ctx.mm(pf[0][:, 0:1], ones_r, stats[0:1, 61:62], inc=True)
    ctx.mk("p_nat", "pe")
    ctx.w("act", "pe", ("p_nat"))
    ctx.acopy(natau, pf[0][:, 0:1], inc=True)
    ctx.mk("a_nat", "act")

    if UPTO <= 0:
        _finish_stub(ctx, T)
        return

    tpg = [0]

    for w in range(NW):
        for tt in range(4):
            gi = 4 * w + tt
            xb = ic(XB[tt], 2048)
            if gi >= 4:
                ctx.w("sync", "dve", ("xn", gi - 4))
            ctx.dma("sync", xb, x_d[512 * w + 128 * tt:512 * w + 128 * tt + 128, :],
                    "sxb%d" % tt)
            ctx.mk(("xld", gi), "sxb%d" % tt)
        for tt in range(4):
            gi = 4 * w + tt
            xb = ic(XB[tt], 2048)
            st = stats[:, 64 + 8 * tt:64 + 8 * tt + 8]
            ctx.w("dve", "sxb%d" % tt, ("xld", gi))
            if gi >= 4:
                ctx.w("dve", "dve", ("xn", gi - 4))
            ctx.dve("reduce_sum", st[:, 0:1], xb, axis=AXX, inc=True)
            ctx.mk(("sum", gi), "dve")
            ctx.w("act", "sxb%d" % tt, ("xld", gi))
            if gi >= 4:
                ctx.w("act", "dve", ("var", gi - 4))
            if gi >= 1:
                ctx.w("act", "act", ("ssq", gi - 1))
            ctx.act(ic(TRASH, 2048), xb, AF.Square, accum=st[:, 1:2], inc=True)
            ctx.mk(("ssq", gi), "act")
        for tt in range(4):
            gi = 4 * w + tt
            st = stats[:, 64 + 8 * tt:64 + 8 * tt + 8]
            ctx.w("dve", "dve", ("sum", gi))
            ctx.dve("tensor_scalar", st[:, 2:3], st[:, 0:1], 1.0 / C, None, ALU.mult, inc=True)
            ctx.mk(("mean", gi), "dve")
        for tt in range(4):
            gi = 4 * w + tt
            st = stats[:, 64 + 8 * tt:64 + 8 * tt + 8]
            ctx.w("dve", "dve", ("mean", gi))
            ctx.dve("tensor_mul", st[:, 3:4], st[:, 2:3], st[:, 2:3], inc=True)
            ctx.mk(("mm2", gi), "dve")
        for tt in range(4):
            gi = 4 * w + tt
            st = stats[:, 64 + 8 * tt:64 + 8 * tt + 8]
            ctx.w("dve", "act", ("ssq", gi))
            ctx.w("dve", "dve", ("mm2", gi))
            ctx.dve("scalar_tensor_tensor", st[:, 4:5], st[:, 1:2], 1.0 / C, st[:, 3:4],
                    ALU.mult, ALU.subtract, inc=True)
            ctx.mk(("var", gi), "dve")
        for tt in range(4):
            gi = 4 * w + tt
            st = stats[:, 64 + 8 * tt:64 + 8 * tt + 8]
            ctx.w("act", "dve", ("var", gi))
            if gi == 0:
                ctx.w("act", "gpc", ("p0gpc"))
            ctx.act(st[:, 5:6], st[:, 4:5], AF.Sqrt, bias=epsln, inc=True)
            ctx.mk(("sqv", gi), "act")
        for tt in range(4):
            gi = 4 * w + tt
            st = stats[:, 64 + 8 * tt:64 + 8 * tt + 8]
            ctx.w("dve", "act", ("sqv", gi))
            ctx.dve("reciprocal", st[:, 6:7], st[:, 5:6], inc=True)
            ctx.mk(("rstd", gi), "dve")
        for tt in range(4):
            gi = 4 * w + tt
            st = stats[:, 64 + 8 * tt:64 + 8 * tt + 8]
            xb = ic(XB[tt], 2048)
            xnb = ib(XN1[tt], 2048)
            ctx.w("dve", "dve", ("rstd", gi))
            if w > 0 and UPTO != 21:
                ctx.w("dve", "pe", ("tpk", w - 1, 15, tt))
            ctx.dve("tensor_scalar", xnb, xb, st[:, 2:3], st[:, 6:7],
                    ALU.subtract, ALU.mult, inc=True)
            ctx.mk(("xn", gi), "dve")
        pbF6 = pf[6][:, :].bitcast(BF16)
        for kc in range(16 if UPTO != 21 else 0):
            for tt in range(4):
                gi = 4 * w + tt
                g = tpg[0]; tpg[0] += 1
                grp = g // 8
                if UPTO == 23:
                    bankap = pb
                else:
                    bankap = pb if grp % 2 == 0 else pbF6
                sl = bankap[:, 128 * (g % 8):128 * (g % 8) + 128]
                ctx.w("pe", "dve", ("xn", gi))
                if UPTO == 23:
                    if grp >= 1 and g % 8 == 0:
                        ctx.w("pe", "dve", ("tpgv", grp - 1))
                elif grp >= 2 and g % 8 == 0:
                    ctx.w("pe", "dve", ("tpgv", grp - 2))
                ctx.tp(sl, ib(XN1[tt], 2048)[:, 128 * kc:128 * kc + 128], idn_b, inc=True)
                ctx.mk(("tpk", w, kc, tt), "pe")
                if g % 8 == 7:
                    ctx.mk(("tpgp", grp), "pe")
                dst = ib(STAGE1 + kc * 512 + tt * 128, 128)
                ctx.w("dve", "pe", ("tpgp", grp))
                if w > 0 and g % 64 < 1 and UPTO not in (21, 22, 23):
                    ctx.w("dve", "sxnw", ("xnwm", w - 1))
                ctx.dve("tensor_copy", dst, sl, inc=True)
                ctx.mk(("tpev", g), "dve")
                if g % 8 == 7:
                    ctx.mk(("tpgv", grp), "dve")
                ctx.mk(("ev", w, kc, tt), "dve")
        for kc in range(16 if UPTO not in (21, 22, 23) else 0):
            ctx.w("sync", "dve", ("ev", w, kc, 3))
            ctx.dma("sync", xnt_d[w * 16 + kc, :, :], ib(STAGE1 + kc * 512, 512), "sxnw")
        ctx.mk(("xnwm", w), "sxnw")
    ctx.mk("xnw_all", "sxnw")
    for i in range(4):
        ctx.mk(("xbt", i), "sxb%d" % i)

    ctx.w("sync", "sxnw", ("xnw_all"))
    for wl in range(OWNW if UPTO not in (11, 21, 22, 23) else 0):
        ctx.dma("sync",
                (lambda wl=wl: A[:, :].rearrange("p (kc t) -> p kc t", kc=16)
                 [:, :, wl * 512:wl * 512 + 512]),
                (lambda wl=wl: ctx.h["xnt_flat"][16 * wl:16 * NW + 1][
                    bass.ts((ctx.pidval % 4) * OWNW, 16), :, :]
                 .rearrange("k p c -> p k c")),
                "sA")
    ctx.mk("sA_all", "sA")

    if UPTO <= 1 or UPTO in (11, 21, 22, 23):
        _finish_stub(ctx, T)
        return
    for e in ["pe", "act", "dve"]:
        ctx.mk(("b12", e), e)
    for e in ["pe", "act", "dve"]:
        for o in ["pe", "act", "dve"]:
            if o != e:
                ctx.w(e, o, ("b12", o))
        for i in range(4):
            ctx.w(e, "sxb%d" % i, ("xbt", i))
        ctx.w(e, "sgin", ("p0gp"))
        ctx.w(e, "gpc", ("p0gpc"))

    for hh in range(HPC):
        for wi, (wsb, wd) in enumerate([(wq_sb, wqT_d), (wk_sb, wkT_d), (wv_sb, wvT_d)]):
            if hh > 0:
                ctx.w("gp", "pe", ("projdone", hh - 1))
            for half in range(2):
                ctx.dma("gp",
                        wsb[:, :].rearrange("p (kc d) -> p kc d", kc=16)[:, 8 * half:8 * half + 8],
                        wd[:, 128 * hh:128 * hh + 128]
                        .rearrange("(kc p) d -> p kc d", p=128)[:, 8 * half:8 * half + 8],
                        "swat")
        ctx.mk(("wat", hh), "swat")

        if hh == 0:
            ctx.dma("gp", wsgu_b[:, :, :, :], wsgu_d[:, :, :, :], "swb")
            ctx.dma("gp", wsdn_b[:, :, :, :, :], wsdn_d[:, :, :, :, :], "swb")
            ctx.mk("swb_sh", "swb")
        for e_m in (range(3 * hh - 3, min(3 * hh, NE)) if hh > 0 else []):
            ctx.dma("gp", wgu_b[e_m], wgu_d[e_m], "swb")
            ctx.dma("gp", wdn_b[e_m], wdn_d[e_m], "swb")
        if hh == HPC - 1:
            ctx.mk("swb_all", "swb")

        def chain(w):
            qb, kb, vb = pf[3 * (w % 2)], pf[3 * (w % 2) + 1], pf[3 * (w % 2) + 2]
            ctx.w("act", "pe", ("projmm", hh, w))
            if (hh, w) != (0, 0):
                pw = (hh, w - 1) if w > 0 else (hh - 1, NW - 1)
                ctx.w("act", "pe", ("ssmm",) + pw)
            ctx.act(icb(Q2, 512), qb[:, :], AF.Square, inc=True)
            ctx.act(icb(K2, 512), kb[:, :], AF.Square, inc=True)
            ctx.act(icb(V2, 512), vb[:, :], AF.Square, inc=True)
            ctx.mk(("sq", hh, w), "act")

            ctx.w("pe", "act", ("sq", hh, w))
            if (hh, w) != (0, 0):
                pw = (hh, w - 1) if w > 0 else (hh - 1, NW - 1)
                ctx.w("pe", "act", ("Rve",) + pw)
            ctx.mm(pf[6][0:1, :], ones_c, icb(Q2, 512), inc=True)
            ctx.mm(pf[6][32:33, :], ones_c, icb(K2, 512), inc=True)
            ctx.mm(pf[6][64:65, :], ones_c, icb(V2, 512), inc=True)
            ctx.mk(("ssmm", hh, w), "pe")

            ctx.w("act", "pe", ("ssmm", hh, w))
            if (hh, w) != (0, 0):
                pw = (hh, w - 1) if w > 0 else (hh - 1, NW - 1)
                ctx.w("act", "dve", ("rows",) + pw)
            ctx.act(Cf[0:1, RQ:RQ + 512], pf[6][0:1, :], AF.Sqrt, inc=True)
            ctx.act(Cf[0:1, RK:RK + 512], pf[6][32:33, :], AF.Sqrt, inc=True)
            ctx.act(Cf[0:1, RV:RV + 512], pf[6][64:65, :], AF.Sqrt, inc=True)
            ctx.mk(("rowsq", hh, w), "act")

            ctx.w("dve", "act", ("rowsq", hh, w))
            for off in (RQ, RK, RV):
                ctx.dve("tensor_scalar", Cf[0:1, off:off + 512], Cf[0:1, off:off + 512],
                        1.0e-8, None, ALU.add, inc=True)
            ctx.mk(("radd", hh, w), "dve")
            ctx.w("dve", "dve", ("radd", hh, w))
            for off in (RQ, RK, RV):
                ctx.dve("reciprocal", Cf[0:1, off:off + 512], Cf[0:1, off:off + 512], inc=True)
            ctx.mk(("rrec", hh, w), "dve")
            ctx.w("dve", "dve", ("rrec", hh, w))
            ctx.w("dve", "scst", ("p0sx"))
            ctx.dve("tensor_scalar", Cf[0:1, RQ:RQ + 512], Cf[0:1, RQ:RQ + 512],
                    tau11, None, ALU.mult, inc=True)
            ctx.mk(("rows", hh, w), "dve")

            ctx.w("pe", "dve", ("rows", hh, w))
            ctx.mm(pf[6][:, :], ones_r, Cf[0:1, RQ:RQ + 512], inc=True)
            ctx.mk(("Rq", hh, w), "pe")
            ctx.w("act", "pe", ("Rq", hh, w))
            if (hh, w) != (0, 0):
                pw = (hh, w - 1) if w > 0 else (hh - 1, NW - 1)
                ctx.w("act", "dve", ("qTev",) + pw)
            ctx.acopy(ic(RQS, 512), pf[6][:, :], inc=True)
            ctx.mk(("Rqe", hh, w), "act")
            ctx.w("pe", "act", ("Rqe", hh, w))
            ctx.mm(pf[6][:, :], ones_r, Cf[0:1, RV:RV + 512], inc=True)
            ctx.mk(("Rv", hh, w), "pe")
            ctx.w("act", "pe", ("Rv", hh, w))
            if (hh, w) != (0, 0):
                pw = (hh, w - 1) if w > 0 else (hh - 1, NW - 1)
                ctx.w("act", "dve", ("vsc",) + pw)
            ctx.acopy(ic(RVS, 512), pf[6][:, :], inc=True)
            ctx.mk(("Rve", hh, w), "act")

            ctx.w("dve", "act", ("Rqe", hh, w))
            if hh > 0 and w == 0:
                ctx.w("dve", "pe", ("flashdone", hh - 1))
            ctx.dve("tensor_mul", B[:, QT + 512 * w:QT + 512 * w + 512], qb[:, :],
                    ic(RQS, 512), inc=True)
            ctx.mk(("qTev", hh, w), "dve")
            ctx.w("act", "pe", ("ssmm", hh, w))
            if hh > 0 and w == 0:
                ctx.w("act", "pe", ("flashdone", hh - 1))
            ctx.acopy(B[:, KT + 512 * w:KT + 512 * w + 512], kb[:, :], inc=True)
            ctx.mk(("kTev", hh, w), "act")
            ctx.w("dve", "act", ("Rve", hh, w))
            ctx.dve("tensor_mul", icb(VTMP, 512), vb[:, :], ic(RVS, 512), inc=True)
            ctx.mk(("vsc", hh, w), "dve")

            for dd in range(4):
                ctx.w("pe", "dve", ("vsc", hh, w))
                if dd == 0:
                    if (hh, w) != (0, 0):
                        pw = (hh, w - 1) if w > 0 else (hh - 1, NW - 1)
                        ctx.w("pe", "act", ("rke",) + pw)
                        ctx.w("pe", "dve", ("vte",) + pw + (3,))
                    if hh > 0 and w == 0:
                        ctx.w("pe", "dve", ("crec0", hh - 1, NPAIRS - 1, 1))
                ctx.tp(pb[:, 128 * dd:128 * dd + 128], icb(VTMP + 64 * dd, 128), idn_b,
                       inc=True)
                ctx.mk(("vt", hh, w, dd), "pe")
            for dd in range(4):
                ctx.w("dve", "pe", ("vt", hh, w, 3))
                if hh > 0 and w == 0:
                    ctx.w("dve", "pe", ("flashdone", hh - 1))
                ctx.dve("tensor_copy",
                        B[:, VT + 512 * w + 128 * dd:VT + 512 * w + 128 * dd + 128],
                        pb[:, 128 * dd:128 * dd + 128], inc=True)
                ctx.mk(("vte", hh, w, dd), "dve")
            pbF2 = pb[:, :].bitcast(F32)
            for dd in range(4):
                ctx.w("pe", "dve", ("rows", hh, w))
                if dd == 0:
                    ctx.w("pe", "dve", ("vte", hh, w, 3))
                ctx.tp(pbF2[:, 384 + dd:384 + dd + 1],
                       Cf[0:1, RK + 128 * dd:RK + 128 * dd + 128],
                       ones_cf[0:1, 0:1], inc=True)
                ctx.mk(("rkt", hh, w, dd), "pe")
            ctx.w("act", "pe", ("rkt", hh, w, 3))
            if hh > 0 and w == 0:
                ctx.w("act", "pe", ("flashdone", hh - 1))
            for dd in range(4):
                ctx.acopy(rk_all[:, 4 * w + dd:4 * w + dd + 1],
                          pbF2[:, 384 + dd:384 + dd + 1], inc=(dd == 3))
            ctx.mk(("rke", hh, w), "act")

        for w in range(NW):
            ctx.w("sync", "sxnw", ("xnw_all"))
            if w >= 2:
                ctx.w("sync", "pe", ("projmm", hh, w - 2))
            elif hh > 0:
                ctx.w("sync", "pe", ("projmm", hh - 1, NW - 2 + w))
            ctx.dma("sync", ib(XW[w % 2], 8192).rearrange("p (kc c) -> p kc c", kc=16),
                    xnt_d[16 * w:16 * w + 16].rearrange("kc p c -> p kc c"),
                    "sxw%d" % (w % 2))
            ctx.mk(("xwl", hh, w), "sxw%d" % (w % 2))

            xw = ib(XW[w % 2], 8192)
            ctx.w("pe", "sxw%d" % (w % 2), ("xwl", hh, w))
            ctx.w("pe", "swat", ("wat", hh))
            if w >= 2:
                ctx.w("pe", "dve", ("qTev", hh, w - 2))
                ctx.w("pe", "act", ("kTev", hh, w - 2))
                ctx.w("pe", "dve", ("vsc", hh, w - 2))
            elif hh > 0:
                ctx.w("pe", "dve", ("qTev", hh - 1, NW - 2 + w))
                ctx.w("pe", "act", ("kTev", hh - 1, NW - 2 + w))
                ctx.w("pe", "dve", ("vsc", hh - 1, NW - 2 + w))
                for bk in range(2):
                    ctx.w("pe", "act", ("Cev", hh - 1, NPAIRS - 1, bk))
                    ctx.w("pe", "dve", ("attn", hh - 1, NPAIRS - 1, bk))
                    ctx.w("pe", "dve", ("crec", hh - 1, NPAIRS - 1, bk))
                    ctx.w("pe", "act", ("attnTe", hh - 1, NPAIRS - 1, bk))
            for (bi, wsb) in [(0, wq_sb), (1, wk_sb), (2, wv_sb)]:
                bank = pf[3 * (w % 2) + bi]
                for kc in range(16):
                    ctx.mm(bank[:, :], wsb[:, 128 * kc:128 * kc + 128],
                           xw[:, 512 * kc:512 * kc + 512],
                           start=(kc == 0), stop=(kc == 15), inc=(kc == 15))
            ctx.mk(("projmm", hh, w), "pe")
            if w > 0:
                chain(w - 1)
        chain(NW - 1)
        ctx.mk(("projdone", hh), "pe")

        st_banks = [[pf[0], pf[1]], [pf[2], pf[3]]]
        out_banks = [pf[4], pf[5]]
        pbF = pb[:, :].bitcast(F32)
        lrow = [pf[6][0:1, :], pbF[0:1, 0:512]]

        for P in range(NPAIRS):
            B0, B1 = 2 * P, 2 * P + 1
            JA = 4 * B0 + 4
            JB = 4 * B1 + 4
            blocks = [B0, B1]

            if P == 0:
                ctx.w("pe", "dve", ("vte", hh, NW - 1, 3))
                ctx.w("pe", "act", ("rke", hh, NW - 1))
                ctx.w("pe", "act", ("kTev", hh, NW - 1))
                ctx.w("pe", "dve", ("qTev", hh, NW - 1))
                ctx.w("pe", "act", ("a_nat"))
                if hh > 0:
                    for bk in range(2):
                        ctx.w("pe", "act", ("Cev", hh - 1, NPAIRS - 1, bk))
            else:
                for bk in range(2):
                    ctx.w("pe", "act", ("Cev", hh, P - 1, bk))
            prevfin = (hh, P - 1) if P > 0 else ((hh - 1, NPAIRS - 1) if hh > 0 else None)

            def active(bk, j):
                return j < (JA if bk == 0 else JB)

            def pt_ap(bk, j):
                idx = 2 * bk + (j % 2)
                return icb(PT[idx], 512)

            def emit_st(j):
                for bk in range(2):
                    if not active(bk, j):
                        continue
                    ctx.mm(st_banks[bk][j % 2][:, :],
                           B[:, KT + 128 * j:KT + 128 * j + 128],
                           B[:, QT + 512 * blocks[bk]:QT + 512 * blocks[bk] + 512],
                           inc=(bk == 1 or not active(1, j)))
                ctx.mk(("st", hh, P, j), "pe")

            def emit_exp(j):
                ctx.w("act", "pe", ("st", hh, P, j))
                if j >= 2:
                    ctx.w("act", "pe", ("pv", hh, P, j - 2))
                for bk in range(2):
                    if not active(bk, j):
                        continue
                    ctx.act(pt_ap(bk, j), st_banks[bk][j % 2][:, :], AF.Exp,
                            bias=natau, scale=rk_all[:, j:j + 1], inc=True)
                ctx.mk(("exp", hh, P, j), "act")
                for bk in range(2):
                    if not active(bk, j):
                        continue
                    dlt = j - 4 * blocks[bk]
                    if 0 <= dlt < 4:
                        ctx.w("dve", "act", ("exp", hh, P, j))
                        ctx.dve("tensor_mul", pt_ap(bk, j), pt_ap(bk, j),
                                maskb[:, 512 * dlt:512 * dlt + 512], inc=True)
                        ctx.mk(("msk", hh, P, j, bk), "dve")

            def emit_fin(bk):
                blkq = blocks[bk]
                crw = CRW + 512 * bk
                last = (JA - 1) if bk == 0 else (JB - 1)
                ctx.w("dve", "pe", ("pv", hh, P, last))
                if P > 0:
                    ctx.w("dve", "dve", ("crec", hh, P - 1, bk))
                elif hh > 0:
                    ctx.w("dve", "dve", ("crec", hh - 1, NPAIRS - 1, bk))
                ctx.dve("reciprocal", Cf[0:1, crw:crw + 512], lrow[bk], inc=True)
                ctx.mk(("crec0", hh, P, bk), "dve")
                ctx.w("dve", "dve", ("crec0", hh, P, bk))
                ctx.dve("tensor_scalar", Cf[0:1, crw:crw + 512], Cf[0:1, crw:crw + 512],
                        nu11, None, ALU.mult, inc=True)
                ctx.mk(("crec", hh, P, bk), "dve")
                ctx.w("pe", "dve", ("crec", hh, P, bk))
                ctx.mm(st_banks[bk][0][:, :], ones_r, Cf[0:1, crw:crw + 512], inc=True)
                ctx.mk(("Cbc", hh, P, bk), "pe")
                ctx.w("act", "pe", ("Cbc", hh, P, bk))
                if P > 0:
                    ctx.w("act", "dve", ("attn", hh, P - 1, bk))
                elif hh > 0:
                    ctx.w("act", "dve", ("attn", hh - 1, NPAIRS - 1, bk))
                ctx.acopy(ic(CSB[bk], 512), st_banks[bk][0][:, :], inc=True)
                ctx.mk(("Cev", hh, P, bk), "act")
                ctx.w("dve", "act", ("Cev", hh, P, bk))
                prevtp = ("attnT", hh, P - 1, bk) if P > 0 else (
                    ("attnT", hh - 1, NPAIRS - 1, bk) if hh > 0 else None)
                if prevtp is not None:
                    ctx.w("dve", "pe", prevtp)
                ctx.dve("tensor_mul", ic(ASTG[bk], 512), out_banks[bk][:, :],
                        ic(CSB[bk], 512), inc=True)
                ctx.mk(("attn", hh, P, bk), "dve")
                pfT = out_banks[bk]
                for cidx in range(4):
                    ctx.w("pe", "dve", ("attn", hh, P, bk))
                    ctx.tp(pfT[:, 128 * cidx:128 * cidx + 128],
                           ic(ASTG[bk] + 128 * cidx, 128), idn_f, inc=(cidx == 3))
                ctx.mk(("attnT", hh, P, bk), "pe")
                ctx.w("act", "pe", ("attnT", hh, P, bk))
                prevdma2 = ("agdma", hh, P - 1, bk) if P > 0 else (
                    ("agdma", hh - 1, NPAIRS - 1, bk) if hh > 0 else None)
                if prevdma2 is not None:
                    ctx.w("act", "sag%d" % bk, prevdma2)
                for cidx in range(4):
                    ctx.acopy(ic(AST2[bk] + 128 * cidx, 128),
                              pfT[:, 128 * cidx:128 * cidx + 128], inc=(cidx == 3))
                ctx.mk(("attnTe", hh, P, bk), "act")
                for cidx in range(4):
                    ctx.w("sync", "act", ("attnTe", hh, P, bk))
                    ctx.dma("sync",
                            agin_d[512 * blkq + 128 * cidx:512 * blkq + 128 * cidx + 128,
                                   128 * hh:128 * hh + 128],
                            ic(AST2[bk] + 128 * cidx, 128), "sag%d" % bk)
                ctx.mk(("agdma", hh, P, bk), "sag%d" % bk)

            def emit_pv(j):
                if j == 0 and prevfin is not None:
                    for bk in range(2):
                        ctx.w("pe", "dve", ("attn",) + prevfin + (bk,))
                        ctx.w("pe", "dve", ("crec",) + prevfin + (bk,))
                        ctx.w("pe", "act", ("attnTe",) + prevfin + (bk,))
                ctx.w("pe", "act", ("exp", hh, P, j))
                for bk in range(2):
                    if not active(bk, j):
                        continue
                    dlt = j - 4 * blocks[bk]
                    if 0 <= dlt < 4:
                        ctx.w("pe", "dve", ("msk", hh, P, j, bk))
                    last = (JA - 1) if bk == 0 else (JB - 1)
                    ctx.mm(out_banks[bk][:, :], B[:, VT + 128 * j:VT + 128 * j + 128],
                           pt_ap(bk, j), start=(j == 0), stop=(j == last))
                    ctx.mm(lrow[bk], ones_c, pt_ap(bk, j),
                           start=(j == 0), stop=(j == last), inc=True)
                ctx.mk(("pv", hh, P, j), "pe")
                for bk in range(2):
                    last = (JA - 1) if bk == 0 else (JB - 1)
                    if j == last:
                        emit_fin(bk)

            emit_st(0)
            if JB > 1:
                emit_st(1)
            for j in range(JB):
                emit_exp(j)
                emit_pv(j)
                if j + 2 < JB:
                    emit_st(j + 2)
        ctx.mk(("flashdone", hh), "pe")

    ctx.mk(("agT", 0), "sag0")
    ctx.mk(("agT", 1), "sag1")
    if UPTO <= 2:
        for e in ["pe", "act", "dve"]:
            ctx.mk(("b24", e), e)
        _finish_stub(ctx, T)
        return
    ctx.w("gp", "sag0", ("agT", 0))
    ctx.w("gp", "sag1", ("agT", 1))
    ctx.coll("AllGather", ALU.bypass, groups, [agin_d[:, :]], [agout_d[:, :]], "scc")

    for e in ["pe", "act", "dve"]:
        ctx.mk(("b24", e), e)
    if UPTO <= 3:
        ctx.w("sync", "scc", 1)
        _finish_stub(ctx, T)
        return
    for e in ["pe", "act", "dve", "gp", "sync"]:
        for o in ["pe", "act", "dve"]:
            if o != e:
                ctx.w(e, o, ("b24", o))

    for sub in range(NSUB):
        ctx.w("pe", "sgin", ("p0gp"))
        ctx.w("pe", "gpc", ("p0gpc"))
        ctx.w("pe", "sA", ("sA_all"))
        if sub > 0:
            ctx.w("pe", "act", ("lgev", sub - 1))
        for kc in range(16):
            ctx.mm(pf[4][0:8, :], gw_sb[:, 8 * kc:8 * kc + 8],
                   A[:, kc * OWN + 512 * sub:kc * OWN + 512 * sub + 512],
                   start=(kc == 0), stop=(kc == 15), inc=(kc == 15))
        ctx.mk(("lgmm", sub), "pe")
        ctx.w("act", "pe", ("lgmm", sub))
        ctx.acopy(Cf[0:8, LSB + 512 * sub:LSB + 512 * sub + 512], pf[4][0:8, :], inc=True)
        ctx.mk(("lgev", sub), "act")
    for tt in range(NT_OWN):
        ctx.w("pe", "act", ("lgev", (tt * 128) // 512))
        ctx.tp(pf[5][:, 8 * tt:8 * tt + 8], Cf[0:8, LSB + 128 * tt:LSB + 128 * tt + 128],
               idn_f[0:8, 0:8], inc=True)
        ctx.mk(("lgt", tt), "pe")
    ctx.w("dve", "pe", ("lgt", NT_OWN - 1))
    ctx.dve("tensor_copy", ic(LTOK, 8 * NT_OWN), pf[5][:, 0:8 * NT_OWN], inc=True)
    ctx.mk("ltok", "dve")
    sc = stats
    for tt in range(NT_OWN):
        ctx.w("dve", "dve", ("ltok"))
        ctx.dve("reduce_max", sc[:, 16 + tt:17 + tt], ic(LTOK + 8 * tt, 8),
                axis=AXX, negate=True, inc=True)
        ctx.mk(("nmx", tt), "dve")
    for tt in range(NT_OWN):
        ctx.w("act", "dve", ("nmx", tt))
        ctx.act(ic(STOK + 8 * tt, 8), ic(LTOK + 8 * tt, 8), AF.Exp,
                bias=sc[:, 16 + tt:17 + tt], accum=sc[:, 32 + tt:33 + tt], inc=True)
        ctx.mk(("sfe", tt), "act")
    for tt in range(NT_OWN):
        ctx.w("dve", "act", ("sfe", tt))
        ctx.dve("reciprocal", sc[:, 48 + tt:49 + tt], sc[:, 32 + tt:33 + tt], inc=True)
        ctx.mk(("srec", tt), "dve")
    for tt in range(NT_OWN):
        ctx.w("dve", "dve", ("srec", tt))
        ctx.dve("tensor_scalar", ic(STOK + 8 * tt, 8), ic(STOK + 8 * tt, 8),
                sc[:, 48 + tt:49 + tt], None, ALU.mult, inc=True)
        ctx.mk(("snrm", tt), "dve")
    for nm, args in [
        ("t2a", None), ("t2b", None), ("t2c", None), ("t2d", None),
        ("t2e", None), ("t2f", None), ("t2g", None)]:
        for tt in range(NT_OWN):
            if nm == "t2a":
                ctx.w("dve", "dve", ("snrm", tt))
                ctx.dve("reduce_max", sc[:, 16 + tt:17 + tt], ic(STOK + 8 * tt, 8),
                        axis=AXX, inc=True)
            elif nm == "t2b":
                ctx.w("dve", "dve", ("t2a", tt))
                ctx.dve("tensor_scalar", ic(CMBT + 8 * tt, 8), ic(STOK + 8 * tt, 8),
                        sc[:, 16 + tt:17 + tt], None, ALU.subtract, inc=True)
            elif nm == "t2c":
                ctx.w("dve", "dve", ("t2b", tt))
                ctx.dve("tensor_single_scalar", ic(CMBT + 8 * tt, 8), ic(CMBT + 8 * tt, 8),
                        0.0, ALU.is_equal, inc=True)
            elif nm == "t2d":
                ctx.w("dve", "dve", ("t2c", tt))
                ctx.dve("scalar_tensor_tensor", ic(CMBT + 8 * tt, 8), ic(CMBT + 8 * tt, 8),
                        -1.0e30, ic(STOK + 8 * tt, 8), ALU.mult, ALU.add, inc=True)
            elif nm == "t2e":
                ctx.w("dve", "dve", ("t2d", tt))
                ctx.dve("reduce_max", sc[:, 16 + tt:17 + tt], ic(CMBT + 8 * tt, 8),
                        axis=AXX, inc=True)
            elif nm == "t2f":
                ctx.w("dve", "dve", ("t2e", tt))
                ctx.dve("tensor_scalar", ic(CMBT + 8 * tt, 8), ic(STOK + 8 * tt, 8),
                        sc[:, 16 + tt:17 + tt], None, ALU.is_ge, inc=True)
            elif nm == "t2g":
                ctx.w("dve", "dve", ("t2f", tt))
                ctx.dve("tensor_mul", ic(CMBT + 8 * tt, 8), ic(CMBT + 8 * tt, 8),
                        ic(STOK + 8 * tt, 8), inc=True)
            ctx.mk((nm, tt), "dve")
    for ch in range(NT_OWN // 4):
        for tt4 in range(4):
            tt = 4 * ch + tt4
            ctx.w("pe", "dve", ("t2g", tt))
            if ch > 0 and tt4 == 0:
                ctx.w("pe", "dve", ("cmbev", ch - 1))
            ctx.tp(pf[6][0:8, 128 * tt4:128 * tt4 + 128], ic(CMBT + 8 * tt, 8),
                   idn_f, inc=True)
            ctx.mk(("cbt", tt), "pe")
        ctx.w("dve", "pe", ("cbt", 4 * ch + 3))
        ctx.dve("tensor_copy", Cf[0:8, CMBS + 512 * ch:CMBS + 512 * ch + 512],
                pf[6][0:8, 0:512], inc=True)
        ctx.mk(("cmbev", ch), "dve")
    ctx.mk("cmbev", "dve")
    ctx.w("sync", "dve", ("cmbev"))
    ctx.dma("sync", combd_d[:, :], Cf[0:8, CMBS:CMBS + OWN], "scmb")
    ctx.mk("cmbw", "scmb")

    ydma = [0]
    for u in range(10):
        is_sh = u < 2
        e = u - 2
        if not is_sh:
            ctx.w("sync", "scmb", ("cmbw"))
            if u > 2:
                ctx.w("sync", "pe", ("cbmm", u - 1))
            ctx.dma("sync", Cf[0:1, CROW2:CROW2 + OWN], combd_d[e:e + 1, :], "scmb")
            ctx.mk(("crl", u), "scmb")
            for sub in range(NSUB):
                ctx.w("pe", "scmb", ("crl", u))
                if sub > 0:
                    ctx.w("pe", "act", ("cbe", u, sub - 1))
                elif u > 2:
                    ctx.w("pe", "dve", ("dnall", u - 1))
                    ctx.w("pe", "act", ("cbe", u - 1, NSUB - 1))
                ctx.mm(pf[6][:, :], ones_r,
                       Cf[0:1, CROW2 + 512 * sub:CROW2 + 512 * sub + 512], inc=True)
                ctx.mk(("cbm", u, sub), "pe")
                ctx.w("act", "pe", ("cbm", u, sub))
                if u > 2:
                    ctx.w("act", "dve", ("dnall", u - 1))
                ctx.acopy(ic(CBW + 512 * sub, 512), pf[6][:, :], inc=True)
                ctx.mk(("cbe", u, sub), "act")
            ctx.mk(("cbmm", u), "pe")

        for p in range(11):
            buf = p % 2
            wg = ib(WGU[buf], 4096)
            if p >= 2:
                ctx.w("sync", "pe", ("guc", u, p - 2))
            elif u > 0:
                ctx.w("sync", "pe", ("guc", u - 1, 9 + p))
            else:
                ctx.w("sync", "pe", ("flashdone", HPC - 1))
            ctx.w("sync", "swb", ("swb_all"))
            src = wsgu_b[11 * u + p] if is_sh else wgu_b[e, p]
            for half in range(2):
                ctx.dma("sync",
                        wg[:, :].rearrange("p (kc c) -> p kc c", kc=16)[:, 8 * half:8 * half + 8],
                        src[8 * half:8 * half + 8].rearrange("kc p c -> p kc c"),
                        "swf%d" % (p % 2))
            ctx.mk(("gul", u, p), "swf%d" % (p % 2))

            for sub in range(NSUB):
                gb, ub_ = pf[0 + 2 * (p % 2)], pf[1 + 2 * (p % 2)]
                ctx.w("pe", "swf%d" % (p % 2), ("gul", u, p))
                ctx.w("pe", "sA", ("sA_all"))
                if sub > 0:
                    ctx.w("pe", "act", ("silu", u, p, sub - 1))
                    ctx.w("pe", "dve", ("actm", u, p, sub - 1))
                elif p >= 2:
                    ctx.w("pe", "act", ("silu", u, p - 2, NSUB - 1))
                    ctx.w("pe", "dve", ("actm", u, p - 2, NSUB - 1))
                elif u > 0:
                    ctx.w("pe", "act", ("silu", u - 1, 9 + p, NSUB - 1))
                    ctx.w("pe", "dve", ("actm", u - 1, 9 + p, NSUB - 1))
                for kc in range(16):
                    ctx.mm(gb[:, :], wg[:, 256 * kc:256 * kc + 128],
                           A[:, kc * OWN + 512 * sub:kc * OWN + 512 * sub + 512],
                           start=(kc == 0), stop=(kc == 15), inc=(kc == 15))
                ctx.mk(("gumm", u, p, sub), "pe")
                for kc in range(16):
                    ctx.mm(ub_[:, :], wg[:, 256 * kc + 128:256 * kc + 256],
                           A[:, kc * OWN + 512 * sub:kc * OWN + 512 * sub + 512],
                           start=(kc == 0), stop=(kc == 15), inc=(kc == 15))
                ctx.mk(("upmm", u, p, sub), "pe")
                ctx.w("act", "pe", ("gumm", u, p, sub))
                if (u, p, sub) == (0, 0, 0):
                    ctx.w("act", "scmb", ("cmbw"))
                if sub > 0:
                    ctx.w("act", "dve", ("actm", u, p, sub - 1))
                elif p >= 2:
                    ctx.w("act", "dve", ("actm", u, p - 2, NSUB - 1))
                elif u > 0:
                    ctx.w("act", "dve", ("actm", u - 1, 9 + p, NSUB - 1))
                sgo = ic(SG[p % 2], 512)
                ctx.act(sgo, gb[:, :], AF.Silu, inc=True)
                ctx.mk(("silu", u, p, sub), "act")
                ctx.w("dve", "act", ("silu", u, p, sub))
                ctx.w("dve", "pe", ("upmm", u, p, sub))
                if u == 0 and p < 2:
                    ctx.w("dve", "pe", ("flashdone", HPC - 1))
                ctx.dve("tensor_mul",
                        B[:, ACT4 + p * OWN + 512 * sub:ACT4 + p * OWN + 512 * sub + 512],
                        sgo, ub_[:, :], inc=True)
                ctx.mk(("actm", u, p, sub), "dve")
            ctx.mk(("guc", u, p), "pe")
        ctx.mk(("actall", u), "dve")

        for q in range(4):
            buf = q % 2
            wd = ib(WDN[buf], 5632)
            if q >= 2:
                ctx.w("sync", "pe", ("dnc", u, q - 2))
            elif u > 0:
                ctx.w("sync", "pe", ("dnc", u - 1, 2 + q))
            else:
                ctx.w("sync", "pe", ("flashdone", HPC - 1))
            ctx.w("sync", "swb", ("swb_all"))
            src = wsdn_b[u, q] if is_sh else wdn_b[e, q]
            ctx.dma("sync", wd[:, :].rearrange("p (fc c) -> p fc c", fc=11)[:, 0:6],
                    src[0:6].rearrange("fc p c -> p fc c"), "swd%d" % (q % 2))
            ctx.dma("sync", wd[:, :].rearrange("p (fc c) -> p fc c", fc=11)[:, 6:11],
                    src[6:11].rearrange("fc p c -> p fc c"), "swd%d" % (q % 2))
            ctx.mk(("dnl", u, q), "swd%d" % (q % 2))

            for hcl in range(4):
                hc = 4 * q + hcl
                for sub in range(NSUB):
                    pidx = hcl * NSUB + sub
                    bank = pf[4 + pidx % 2]
                    ctx.w("pe", "swd%d" % (q % 2), ("dnl", u, q))
                    ctx.w("pe", "dve", ("actall", u))
                    pv = None
                    if pidx >= 2:
                        pv = ("dnev", u, q, pidx - 2)
                    elif q > 0:
                        pv = ("dnev", u, q - 1, 4 * NSUB - 2 + pidx)
                    elif u > 0:
                        pv = ("dnev", u - 1, 3, 4 * NSUB - 2 + pidx)
                    if pv is not None:
                        ctx.w("pe", "dve", pv)
                    for fc in range(11):
                        ctx.mm(bank[:, :],
                               wd[:, 512 * fc + 128 * hcl:512 * fc + 128 * hcl + 128],
                               B[:, ACT4 + fc * OWN + 512 * sub:ACT4 + fc * OWN + 512 * sub + 512],
                               start=(fc == 0), stop=(fc == 10), inc=(fc == 10))
                    ctx.mk(("dnmm", u, q, pidx), "pe")
                    if pidx == 4 * NSUB - 1:
                        ctx.mk(("dnc", u, q), "pe")
                    stg = ic(STG[pidx % 4], 512)
                    di = ydma[0]
                    ssem = "syt%d" % (di % 4)
                    ctx.w("dve", "pe", ("dnmm", u, q, pidx))
                    if not is_sh:
                        ctx.w("dve", "act", ("cbe", u, sub))
                    if di >= 4:
                        ctx.w("dve", ssem, ("ytc", di - 4))
                    if di == 0:
                        ctx.w("dve", "scmb", ("cmbw"))
                    if is_sh:
                        ctx.dve("tensor_copy", stg, bank[:, :], inc=True)
                    else:
                        ctx.dve("tensor_mul", stg, bank[:, :], ic(CBW + 512 * sub, 512),
                                inc=True)
                    ctx.mk(("dnev", u, q, pidx), "dve")
                    ctx.w("gp", "dve", ("dnev", u, q, pidx))
                    if u > 0 and di == ctx.marks.get(("ytfu", u), -1):
                        for i in range(4):
                            ctx.w("gp", "syt%d" % i, ("ytallS", u - 1, i))
                    ctx.dma("gp", ytacc_d[128 * hc:128 * hc + 128, 512 * sub:512 * sub + 512],
                            stg, ssem, accum=None if u == 0 else ALU.add)
                    ctx.mk(("ytc", di), ssem)
                    ydma[0] += 1
        for i in range(4):
            ctx.mk(("ytallS", u, i), "syt%d" % i)
        if ctx.mode == "count":
            ctx.marks[("ytfu", u + 1)] = ydma[0]
        ctx.mk(("dnall", u), "dve")

    for i in range(4):
        ctx.mk(("ytT", i), "syt%d" % i)
    ctx.mk(("swfT", 0), "swf0")
    ctx.mk(("swfT", 1), "swf1")
    ctx.mk(("swdT", 0), "swd0")
    ctx.mk(("swdT", 1), "swd1")

    for e in ["pe", "act", "dve"]:
        ctx.mk(("b45", e), e)
    for e in ["pe", "act", "dve", "sync"]:
        for o in ["pe", "act", "dve"]:
            if o != e:
                ctx.w(e, o, ("b45", o))

    if UPTO <= 4:
        for i in range(4):
            ctx.w("sync", "syt%d" % i, ("ytT", i))
        _finish_stub(ctx, T)
        return
    Bf = B[:, :].bitcast(F32)
    EXB = [0, 2048]
    ATKB = [4096, 6144]
    EOB = [8192, 10240]
    YTB = [8192, 8704]
    TMPB = [9216, 9344]

    NTI = OWN // 128
    ctx.w("sync", "scc", 1)
    ctx.dma("sync", xloc_d[0:OWN, :],
            (lambda: x_d[:, :][bass.ts(ctx.pidval % 4, OWN), :]), "sloc")
    for rp in range(4):
        ctx.dma("sync", agloc_d[rp, 0:OWN, :],
                (lambda rp=rp: agout_d[rp * T:4 * T, :][
                    bass.ts(ctx.pidval % 4, OWN), :]), "sloc")
    ctx.mk("sloc_all", "sloc")
    ctx.w("sync", "sloc", ("sloc_all"))
    for i in range(4):
        ctx.w("sync", "syt%d" % i, ("ytT", i))
    ctx.w("sync", "swf0", ("swfT", 0))
    ctx.w("sync", "swf1", ("swfT", 1))
    ctx.w("sync", "swd0", ("swdT", 0))
    ctx.w("sync", "swd1", ("swdT", 1))
    for gi in range(NTI):
        if gi >= 2:
            ctx.w("sync", "dve", ("o2", gi - 2, 15))
        ctx.dma("sync", Bf[:, EXB[gi % 2]:EXB[gi % 2] + 2048],
                xloc_d[128 * gi:128 * gi + 128, :], "sepx%d" % (gi % 2))
        ctx.mk(("exl", gi), "sepx%d" % (gi % 2))
        for rp in range(4):
            ctx.dma("sync",
                    Bf[:, ATKB[gi % 2] + 512 * rp:ATKB[gi % 2] + 512 * rp + 512],
                    agloc_d[rp, 128 * gi:128 * gi + 128, :], "sepa%d" % (gi % 2))
        ctx.mk(("atk", gi), "sepa%d" % (gi % 2))
        for k4 in range(4):
            ki = 4 * gi + k4
            ybuf = ic(YTB[ki % 2], 512)
            if ki >= 2:
                ctx.w("sync", "pe", ("ytpl", ki - 2))
            ctx.dma("sync", ybuf.rearrange("p (k t) -> p k t", k=4),
                    ytacc_d[512 * k4:512 * k4 + 512, 128 * gi:128 * gi + 128]
                    .rearrange("(k p) t -> p k t", p=128), "sepy%d" % (ki % 2))
            ctx.mk(("ytl", ki), "sepy%d" % (ki % 2))
        def etp(kc):
            tg = 16 * gi + kc
            pslot = pf[tg % 2][:, 128 * ((tg // 2) % 4):128 * ((tg // 2) % 4) + 128]
            ctx.w("pe", "sepy%d" % ((4 * gi + kc // 4) % 2), ("ytl", 4 * gi + kc // 4))
            if tg >= 2:
                ctx.w("pe", "dve", ("o1", (tg - 2) // 16, (tg - 2) % 16))
            ctx.tp(pslot, ic(YTB[(4 * gi + kc // 4) % 2] + 128 * (kc % 4), 128),
                   idn_f, inc=True)
            ctx.mk(("ytp", tg), "pe")
            if kc % 4 == 3:
                ctx.mk(("ytpl", 4 * gi + kc // 4), "pe")
            return pslot

        slots = {}
        def eo1(kc):
            tg = 16 * gi + kc
            pslot = pf[tg % 2][:, 128 * ((tg // 2) % 4):128 * ((tg // 2) % 4) + 128]
            ctx.w("dve", "pe", ("ytp", tg))
            ctx.w("dve", "sepa%d" % (gi % 2), ("atk", gi))
            if kc >= 2:
                ctx.w("dve", "dve", ("o2", gi, kc - 2))
            ctx.dve("tensor_add", ic(TMPB[kc % 2], 128), pslot,
                    Bf[:, ATKB[gi % 2] + 128 * kc:ATKB[gi % 2] + 128 * kc + 128], inc=True)
            ctx.mk(("o1", gi, kc), "dve")

        def eo2(kc):
            ctx.w("dve", "dve", ("o1", gi, kc))
            ctx.w("dve", "sepx%d" % (gi % 2), ("exl", gi))
            if kc == 0 and gi >= 2:
                ctx.w("dve", "sout%d" % (gi % 2), ("outd", gi - 2))
            ctx.dve("tensor_add", Bf[:, EOB[gi % 2] + 128 * kc:EOB[gi % 2] + 128 * kc + 128],
                    ic(TMPB[kc % 2], 128),
                    Bf[:, EXB[gi % 2] + 128 * kc:EXB[gi % 2] + 128 * kc + 128], inc=True)
            ctx.mk(("o2", gi, kc), "dve")

        for kc in range(16):
            etp(kc)
            if kc >= 1:
                eo1(kc - 1)
            if kc >= 2:
                eo2(kc - 2)
        eo1(15)
        eo2(14)
        eo2(15)
        ctx.w("sync", "dve", ("o2", gi, 15))
        ctx.dma("sync", out_d[128 * gi:128 * gi + 128, :],
                Bf[:, EOB[gi % 2]:EOB[gi % 2] + 2048], "sout%d" % (gi % 2))
        ctx.mk(("outd", gi), "sout%d" % (gi % 2))
    ctx.w("sync", "sout0", ctx.cnt["sout0"])
    ctx.w("sync", "sout1", ctx.cnt["sout1"])


def build_program(T):
    OWN = T // 4
    NW = T // 512
    nc = bass.Bass("TRN2", target_bir_lowering=False, debug=False)

    h = {"groups": [[0, 1, 2, 3], [4, 5, 6, 7]]}

    def din(name, shp):
        return nc.dram_tensor(name, shp, F32, kind="ExternalInput").ap()

    h["x_d"] = din("x", [T, C])
    h["wqT_d"] = din("wqT", [C, 512])
    h["wkT_d"] = din("wkT", [C, 512])
    h["wvT_d"] = din("wvT", [C, 512])
    h["tau_d"] = din("tau", [1])
    h["nu_d"] = din("nu", [1])
    h["gw_d"] = din("gw", [16, 128, 8])
    h["wgu_d"] = din("wgu", [NE, 11, 16, 128, 256])
    h["wdn_d"] = din("wdn", [NE, 4, 11, 128, 512])
    h["wsgu_d"] = din("wsgu", [22, 16, 128, 256])
    h["wsdn_d"] = din("wsdn", [2, 4, 11, 128, 512])
    h["cmask_d"] = din("cmask", [128, 4, 512])
    h["idn_d"] = din("idn", [128, 128])
    h["out_d"] = nc.dram_tensor("out", [OWN, C], F32, kind="ExternalOutput").ap()

    xnt = nc.dram_tensor("xnt", [NW * 16 + 1, 128, 512], BF16, kind="Internal")
    h["xnt_d"] = xnt.ap()
    h["xnt_flat"] = xnt.ap()
    h["agin_d"] = nc.dram_tensor("agin", [T, 512], F32, kind="Internal").ap()
    h["agout_d"] = nc.dram_tensor("agout", [4 * T, 512], F32, kind="Internal").ap()
    h["combd_d"] = nc.dram_tensor("combd", [NE, OWN], F32, kind="Internal").ap()
    h["ytacc_d"] = nc.dram_tensor("ytacc", [C, OWN], F32, kind="Internal").ap()
    h["agloc_d"] = nc.dram_tensor("agloc", [4, OWN + 128, 512], F32, kind="Internal").ap()
    h["wgu_b"] = nc.dram_tensor("wgu_b", [NE, 11, 16, 128, 256], BF16, kind="Internal").ap()
    h["wdn_b"] = nc.dram_tensor("wdn_b", [NE, 4, 11, 128, 512], BF16, kind="Internal").ap()
    h["wsgu_b"] = nc.dram_tensor("wsgu_b", [22, 16, 128, 256], BF16, kind="Internal").ap()
    h["wsdn_b"] = nc.dram_tensor("wsdn_b", [2, 4, 11, 128, 512], BF16, kind="Internal").ap()
    h["xloc_d"] = nc.dram_tensor("xloc", [OWN + 128, C], F32, kind="Internal").ap()

    BCOLS = max(3 * T + 16384, 8192 + 2 * 5632 + 11 * OWN)
    CCOLS = 10240

    es = ExitStack()
    with es:
        h["A"] = es.enter_context(nc.sbuf_tensor("arnA", [128, 16 * OWN], BF16))[:, :]
        h["B"] = es.enter_context(nc.sbuf_tensor("arnB", [128, BCOLS], BF16))[:, :]
        h["C"] = es.enter_context(nc.sbuf_tensor("arnC", [128, CCOLS], F32))[:, :]
        for nm, shp, dt in [("idn_f", [128, 128], F32), ("idn_b", [128, 128], BF16),
                            ("ones_r", [1, 128], F32), ("ones_c", [128, 1], BF16),
                            ("ones_cf", [128, 1], F32),
                            ("stats", [128, 96], F32), ("rk_all", [128, 64], F32),
                            ("maskb", [128, 2048], BF16), ("natau", [128, 1], F32),
                            ("epsln", [128, 1], F32), ("tau11", [1, 1], F32),
                            ("nu11", [1, 1], F32), ("gw_sb", [128, 128], BF16),
                            ("wq_sb", [128, 2048], BF16), ("wk_sb", [128, 2048], BF16),
                            ("wv_sb", [128, 2048], BF16)]:
            h[nm] = es.enter_context(nc.sbuf_tensor(nm, shp, dt))[:, :]
        pf = []
        for i in range(7):
            pf.append(es.enter_context(nc.psum_tensor("pf%d" % i, [128, 512], F32))[:, :])
        h["pf"] = pf
        h["pb"] = es.enter_context(nc.psum_tensor("pb", [128, 1024], BF16))[:, :]
        for s in SEMS:
            h[s] = es.enter_context(nc.semaphore(s))

        marks = {}
        cc = Ctx(nc, "count", h, marks)
        emit_all(cc, T)
        final = dict(cc.cnt)

        blk = es.enter_context(nc.Block())

        def run(mode, pid=None):
            c2 = Ctx(nc, mode, h, marks)
            if pid is not None:
                c2.pidval = pid
            emit_all(c2, T)
            for k in SEMS:
                assert c2.cnt[k] == final[k], (mode, k, c2.cnt[k], final[k])

        @blk.tensor
        def _(t):
            run("pe")

        @blk.scalar
        def _(s):
            run("act")

        @blk.vector
        def _(v):
            run("dve")

        @blk.sync
        def _(s):
            pid = nc.partition_id(engines=[mybir.EngineType.SP])
            run("sync", pid=pid)

        @blk.gpsimd
        def _(g):
            run("gp")

    return nc


def kernel(L0_attn_x, L0_attn_wq, L0_attn_wk, L0_attn_wv, L0_attn_tau, L0_attn_nu,
           L0_ffn_gate_weight, L0_ffn_gate_up_weights, L0_ffn_down_weights,
           L0_ffn_shared_gate_up, L0_ffn_shared_down):
    x = np.asarray(L0_attn_x)
    Bn, T, _ = x.shape
    OWN = T // 4

    wq = np.asarray(L0_attn_wq)
    wk = np.asarray(L0_attn_wk)
    wv = np.asarray(L0_attn_wv)
    tau = np.asarray(L0_attn_tau).reshape(1).astype(np.float32)
    nu = np.asarray(L0_attn_nu).reshape(1).astype(np.float32)
    gwt = np.ascontiguousarray(np.asarray(L0_ffn_gate_weight).T).reshape(16, 128, 8)
    guw = np.asarray(L0_ffn_gate_up_weights)
    dw = np.asarray(L0_ffn_down_weights)
    sgu = np.asarray(L0_ffn_shared_gate_up)
    sdn = np.asarray(L0_ffn_shared_down)

    Ag = guw.transpose(0, 2, 1).reshape(NE, 16, 128, 22, 128)
    wgu = np.ascontiguousarray(
        np.stack([Ag[:, :, :, 0:11, :], Ag[:, :, :, 11:22, :]], axis=4)
        .transpose(0, 3, 1, 2, 4, 5).reshape(NE, 11, 16, 128, 256))
    Dd = dw.transpose(0, 2, 1).reshape(NE, 11, 128, 4, 512)
    wdn = np.ascontiguousarray(Dd.transpose(0, 3, 1, 2, 4))
    Sg = np.ascontiguousarray(sgu.T).reshape(16, 128, 44, 128)
    wsgu = np.ascontiguousarray(
        np.stack([Sg[:, :, 0:22, :], Sg[:, :, 22:44, :]], axis=3)
        .transpose(2, 0, 1, 3, 4).reshape(22, 16, 128, 256))
    Sd = np.ascontiguousarray(sdn.T).reshape(2, 11, 128, 4, 512)
    wsdn = np.ascontiguousarray(Sd.transpose(0, 3, 1, 2, 4))

    p = np.arange(128)[:, None]
    f = np.arange(512)[None, :]
    cmask = np.stack([(128 * d + p <= f).astype(np.float32) for d in range(4)], axis=1)
    cmask = np.ascontiguousarray(cmask)
    idn = np.eye(128, dtype=np.float32)

    _enable_ldw_opt()
    nc = build_program(T)

    in_maps = []
    for c in range(NCORES):
        b = c // 4
        r = c % 4
        in_maps.append(dict(
            x=np.ascontiguousarray(x[b]),
            wqT=np.ascontiguousarray(wq.T[:, 512 * r:512 * r + 512]),
            wkT=np.ascontiguousarray(wk.T[:, 512 * r:512 * r + 512]),
            wvT=np.ascontiguousarray(wv.T[:, 512 * r:512 * r + 512]),
            tau=tau, nu=nu, gw=gwt, wgu=wgu, wdn=wdn, wsgu=wsgu, wsdn=wsdn,
            cmask=cmask, idn=idn,
        ))
    res = run_bass_kernel_spmd(nc, in_maps, core_ids=list(range(NCORES)))
    out = np.empty((Bn, T, C), dtype=np.float32)
    for c in range(NCORES):
        b, r = c // 4, c % 4
        out[b, r * OWN:(r + 1) * OWN, :] = res.results[c]["out"]
    return out


# version 08
# speedup vs baseline: 1.0176x; 1.0176x over previous
import numpy as np
from contextlib import ExitStack

import concourse.bass as bass
import concourse.mybir as mybir
from concourse.bass_utils import run_bass_kernel_spmd


def _enable_ldw_opt():
    try:
        from concourse.compiler_utils import get_compiler_flags, set_compiler_flags
        flags = get_compiler_flags()
        newflags = [f.replace("--enable-ldw-opt=false", "--enable-ldw-opt=true")
                    for f in flags]
        set_compiler_flags(newflags)
    except Exception:
        pass

F32 = mybir.dt.float32
BF16 = mybir.dt.bfloat16
AF = mybir.ActivationFunctionType
ALU = mybir.AluOpType
AXX = mybir.AxisListType.X

C = 2048
D = 128
HPC = 4
NE = 8
NCORES = 8

SEMS = (["pe", "act", "dve", "scst", "sxnw", "sA", "swat", "scmb", "scc",
         "sgin", "gpc"]
        + ["sxb%d" % i for i in range(4)] + ["sxw%d" % i for i in range(2)]
        + ["swf%d" % i for i in range(2)] + ["swd%d" % i for i in range(2)]
        + ["sag%d" % i for i in range(2)] + ["syt%d" % i for i in range(4)]
        + ["sepx%d" % i for i in range(2)] + ["sepa%d" % i for i in range(2)]
        + ["sepy%d" % i for i in range(2)] + ["sout%d" % i for i in range(2)]
        + ["sloc", "swbs"] + ["swb%d" % i for i in range(4)])


class Ctx:

    def __init__(self, nc, mode, h, marks):
        self.nc = nc
        self.mode = mode
        self.h = h
        self.marks = marks
        self.cnt = {k: 0 for k in SEMS}
        self.pidval = 0

    def mk(self, key, sem):
        v = self.cnt[sem]
        if self.mode == "count":
            self.marks[key] = v
        else:
            assert self.marks[key] == v, (key, self.marks[key], v, self.mode)
        return v

    def m(self, key):
        return self.marks[key]

    def w(self, eng, sem, val):
        if self.mode != eng:
            return
        if isinstance(val, (tuple, str)):
            val = self.marks[val]
        if val is None or val <= 0:
            return
        self._eng(eng).wait_ge(self.h[sem], val)

    def _eng(self, eng):
        nc = self.nc
        return {"pe": nc.tensor, "act": nc.scalar, "dve": nc.vector,
                "sync": nc.sync, "gp": nc.gpsimd}[eng]

    def _fin(self, inst, sem, amount):
        self.cnt[sem] += amount
        if inst is not None:
            inst.then_inc(self.h[sem], amount)

    def mm(self, out, lhsT, rhs, start=True, stop=True, inc=False):
        inst = None
        if self.mode == "pe":
            inst = self.nc.tensor.matmul(out, lhsT, rhs, start=start, stop=stop)
        if inc:
            self._fin(inst, "pe", 1)

    def tp(self, out, in_, idn, inc=False):
        inst = None
        if self.mode == "pe":
            inst = self.nc.tensor.transpose(out, in_, idn)
        if inc:
            self._fin(inst, "pe", 1)

    def act(self, out, in_, func, bias=0.0, scale=1.0, accum=None, inc=False):
        inst = None
        if self.mode == "act":
            kw = {}
            if accum is not None:
                kw["accum_out"] = accum
            inst = self.nc.scalar.activation(out, in_, func, bias=bias, scale=scale, **kw)
        if inc:
            self._fin(inst, "act", 1)

    def acopy(self, out, in_, inc=False):
        inst = None
        if self.mode == "act":
            inst = self.nc.scalar.copy(out, in_)
        if inc:
            self._fin(inst, "act", 1)

    def amul(self, out, in_, mulv, inc=False):
        inst = None
        if self.mode == "act":
            inst = self.nc.scalar.mul(out, in_, mulv)
        if inc:
            self._fin(inst, "act", 1)

    def dve(self, meth, *args, inc=False, **kw):
        inst = None
        if self.mode == "dve":
            inst = getattr(self.nc.vector, meth)(*args, **kw)
        if inc:
            self._fin(inst, "dve", 1)

    def dma(self, queue, out, in_, sem, accum=None):
        inst = None
        if self.mode == queue:
            if callable(out):
                out = out()
            if callable(in_):
                in_ = in_()
            eng = self._eng(queue)
            try:
                if accum is None:
                    inst = eng.dma_start(out=out, in_=in_)
                else:
                    inst = eng.dma_start(out=out, in_=in_, accum_op=accum)
            except Exception as e:
                import traceback
                print("DMA FAIL: out=", out, " in=", in_)
                tb = e.__traceback__
                while tb is not None:
                    print("  frame:", tb.tb_frame.f_code.co_filename, tb.tb_lineno,
                          tb.tb_frame.f_code.co_name)
                    tb = tb.tb_next
                raise
        self._fin(inst, sem, 16)

    def coll(self, kind, op, groups, ins, outs, sem):
        inst = None
        if self.mode == "gp":
            inst = self.nc.gpsimd.collective_compute(kind, op, replica_groups=groups,
                                                     ins=ins, outs=outs)
        self.cnt[sem] += 1
        if inst is not None:
            inst.then_inc(self.h[sem])

    def gmemset(self, ap, val, sem=None):
        inst = None
        if self.mode == "gp":
            inst = self.nc.gpsimd.memset(ap, val)
        if sem is not None:
            self._fin(inst, sem, 1)


UPTO = 9


def _finish_stub(ctx, T):
    OWN = T // 4
    out_d = ctx.h["out_d"]
    B = ctx.h["B"]
    for e in ["pe", "act", "dve"]:
        for o in ["pe", "act", "dve"]:
            if o != e:
                ctx.w(e, o, ctx.cnt[o])
    for i in range(OWN // 128):
        ctx.dma("sync", out_d[128 * i:128 * i + 128, :],
                B[:, 0:4096].bitcast(mybir.dt.float32), "sout0")
    ctx.w("sync", "sout0", ctx.cnt["sout0"])


def emit_all(ctx, T):
    h = ctx.h
    OWN = T // 4
    NW = T // 512
    OWNW = OWN // 512
    NSUB = OWN // 512
    NQB = T // 512
    NPAIRS = NQB // 2
    NEPI = OWN // 256
    NT_OWN = OWN // 128

    A = h["A"]; B = h["B"]; Cf = h["C"]
    Cb = Cf[:, :].bitcast(BF16)
    pf = h["pf"]; pb = h["pb"]
    idn_f = h["idn_f"]; idn_b = h["idn_b"]
    ones_r = h["ones_r"]; ones_c = h["ones_c"]; ones_cf = h["ones_cf"]
    stats = h["stats"]; rk_all = h["rk_all"]; maskb = h["maskb"]
    natau = h["natau"]; epsln = h["epsln"]
    tau11 = h["tau11"]; nu11 = h["nu11"]
    gw_sb = h["gw_sb"]; wq_sb = h["wq_sb"]; wk_sb = h["wk_sb"]; wv_sb = h["wv_sb"]

    x_d = h["x_d"]; wqT_d = h["wqT_d"]; wkT_d = h["wkT_d"]; wvT_d = h["wvT_d"]
    tau_d = h["tau_d"]; nu_d = h["nu_d"]; gw_d = h["gw_d"]
    wgu_d = h["wgu_d"]; wdn_d = h["wdn_d"]; wsgu_d = h["wsgu_d"]; wsdn_d = h["wsdn_d"]
    cmask_d = h["cmask_d"]; idn_d = h["idn_d"]
    xnt_d = h["xnt_d"]; agin_d = h["agin_d"]; agout_d = h["agout_d"]
    combd_d = h["combd_d"]; ytacc_d = h["ytacc_d"]; out_d = h["out_d"]
    agloc_d = h["agloc_d"]; xloc_d = h["xloc_d"]
    wgu_b = h["wgu_b"]; wdn_b = h["wdn_b"]; wsgu_b = h["wsgu_b"]; wsdn_b = h["wsdn_b"]
    groups = h["groups"]

    ib = lambda off, n: B[:, off:off + n]
    ic = lambda off, n: Cf[:, off:off + n]
    icb = lambda f32off, n: Cb[:, 2 * f32off:2 * f32off + n]

    XB = [0, 2048, 4096, 6144]; TRASH = 8192
    PT = [0, 256, 512, 768]
    CSB = [1024, 1536]
    ASTG = [2048, 2560]
    RQ, RV, RK, CRW = 3072, 3584, 4096, 4608
    Q2, K2, V2, VTMP = 5632, 5888, 6144, 6400
    RQS, RVS = 6656, 7168
    AST2 = [7680, 8192]
    SG = [0, 512]
    STG = [1024, 1536, 2048, 2560]
    CBW = 3072
    LTOK, STOK, CMBT = 5120, 5248, 5376
    LSB = 5504
    CROW2 = 5504
    CMBS = 0
    EX = [0, 2048]; EO = [4096, 6144]
    EAG = [8192, 8448]; EYT = [8704, 8960]; ES1 = [9216, 9472]

    QT, KT, VT = 0, T, 2 * T
    XW = [3 * T, 3 * T + 8192]
    STAGE1 = 0
    XN1 = [8192, 10240, 12288, 14336]
    WGU = [0, 4096]
    WDN = [8192, 8192 + 5632]
    ACT4 = 8192 + 2 * 5632

    ctx.dma("sync", h["idn_f"], idn_d[:, :], "scst")
    ctx.dma("sync", h["tau11"], tau_d[:].unsqueeze(0), "scst")
    ctx.dma("sync", h["nu11"], nu_d[:].unsqueeze(0), "scst")
    ctx.mk(("p0sx"), "scst")
    ctx.dma("gp", h["idn_b"], idn_d[:, :], "sgin")
    ctx.dma("gp", maskb[:, :].rearrange("p (d c) -> p d c", d=4), cmask_d[:, :, :], "sgin")
    ctx.dma("gp", gw_sb[:, :].rearrange("p (kc e) -> p kc e", kc=16),
            gw_d[:, :, :].rearrange("kc p e -> p kc e"), "sgin")
    ctx.mk("p0gp", "sgin")
    ctx.gmemset(ones_r, 1.0, sem="gpc")
    ctx.gmemset(ones_c, 1.0, sem="gpc")
    ctx.gmemset(ones_cf, 1.0, sem="gpc")
    ctx.gmemset(epsln, 1.0e-5, sem="gpc")
    ctx.mk("p0gpc", "gpc")

    ctx.w("act", "scst", ("p0sx"))
    ctx.act(stats[0:1, 60:61], tau11, AF.Abs, inc=True)
    ctx.mk("a_abs", "act")
    ctx.w("act", "act", ("a_abs"))
    ctx.amul(stats[0:1, 61:62], stats[0:1, 60:61], -1.0, inc=True)
    ctx.mk("a_neg", "act")
    ctx.w("pe", "act", ("a_neg"))
    ctx.w("pe", "sgin", ("p0gp"))
    ctx.w("pe", "gpc", ("p0gpc"))
    ctx.mm(pf[0][:, 0:1], ones_r, stats[0:1, 61:62], inc=True)
    ctx.mk("p_nat", "pe")
    ctx.w("act", "pe", ("p_nat"))
    ctx.acopy(natau, pf[0][:, 0:1], inc=True)
    ctx.mk("a_nat", "act")

    if UPTO <= 0:
        _finish_stub(ctx, T)
        return

    tpg = [0]

    for w in range(NW):
        for tt in range(4):
            gi = 4 * w + tt
            xb = ic(XB[tt], 2048)
            if gi >= 4:
                ctx.w("sync", "dve", ("xn", gi - 4))
            ctx.dma("sync", xb, x_d[512 * w + 128 * tt:512 * w + 128 * tt + 128, :],
                    "sxb%d" % tt)
            ctx.mk(("xld", gi), "sxb%d" % tt)
        for tt in range(4):
            gi = 4 * w + tt
            xb = ic(XB[tt], 2048)
            st = stats[:, 64 + 8 * tt:64 + 8 * tt + 8]
            ctx.w("dve", "sxb%d" % tt, ("xld", gi))
            if gi >= 4:
                ctx.w("dve", "dve", ("xn", gi - 4))
            ctx.dve("reduce_sum", st[:, 0:1], xb, axis=AXX, inc=True)
            ctx.mk(("sum", gi), "dve")
            ctx.w("act", "sxb%d" % tt, ("xld", gi))
            if gi >= 4:
                ctx.w("act", "dve", ("var", gi - 4))
            if gi >= 1:
                ctx.w("act", "act", ("ssq", gi - 1))
            ctx.act(ic(TRASH, 2048), xb, AF.Square, accum=st[:, 1:2], inc=True)
            ctx.mk(("ssq", gi), "act")
        for tt in range(4):
            gi = 4 * w + tt
            st = stats[:, 64 + 8 * tt:64 + 8 * tt + 8]
            ctx.w("dve", "dve", ("sum", gi))
            ctx.dve("tensor_scalar", st[:, 2:3], st[:, 0:1], 1.0 / C, None, ALU.mult, inc=True)
            ctx.mk(("mean", gi), "dve")
        for tt in range(4):
            gi = 4 * w + tt
            st = stats[:, 64 + 8 * tt:64 + 8 * tt + 8]
            ctx.w("dve", "dve", ("mean", gi))
            ctx.dve("tensor_mul", st[:, 3:4], st[:, 2:3], st[:, 2:3], inc=True)
            ctx.mk(("mm2", gi), "dve")
        for tt in range(4):
            gi = 4 * w + tt
            st = stats[:, 64 + 8 * tt:64 + 8 * tt + 8]
            ctx.w("dve", "act", ("ssq", gi))
            ctx.w("dve", "dve", ("mm2", gi))
            ctx.dve("scalar_tensor_tensor", st[:, 4:5], st[:, 1:2], 1.0 / C, st[:, 3:4],
                    ALU.mult, ALU.subtract, inc=True)
            ctx.mk(("var", gi), "dve")
        for tt in range(4):
            gi = 4 * w + tt
            st = stats[:, 64 + 8 * tt:64 + 8 * tt + 8]
            ctx.w("act", "dve", ("var", gi))
            if gi == 0:
                ctx.w("act", "gpc", ("p0gpc"))
            ctx.act(st[:, 5:6], st[:, 4:5], AF.Sqrt, bias=epsln, inc=True)
            ctx.mk(("sqv", gi), "act")
        for tt in range(4):
            gi = 4 * w + tt
            st = stats[:, 64 + 8 * tt:64 + 8 * tt + 8]
            ctx.w("dve", "act", ("sqv", gi))
            ctx.dve("reciprocal", st[:, 6:7], st[:, 5:6], inc=True)
            ctx.mk(("rstd", gi), "dve")
        for tt in range(4):
            gi = 4 * w + tt
            st = stats[:, 64 + 8 * tt:64 + 8 * tt + 8]
            xb = ic(XB[tt], 2048)
            xnb = ib(XN1[tt], 2048)
            ctx.w("dve", "dve", ("rstd", gi))
            if w > 0 and UPTO != 21:
                ctx.w("dve", "pe", ("tpk", w - 1, 15, tt))
            ctx.dve("tensor_scalar", xnb, xb, st[:, 2:3], st[:, 6:7],
                    ALU.subtract, ALU.mult, inc=True)
            ctx.mk(("xn", gi), "dve")
        pbF6 = pf[6][:, :].bitcast(BF16)
        for kc in range(16 if UPTO != 21 else 0):
            for tt in range(4):
                gi = 4 * w + tt
                g = tpg[0]; tpg[0] += 1
                grp = g // 8
                if UPTO == 23:
                    bankap = pb
                else:
                    bankap = pb if grp % 2 == 0 else pbF6
                sl = bankap[:, 128 * (g % 8):128 * (g % 8) + 128]
                ctx.w("pe", "dve", ("xn", gi))
                if UPTO == 23:
                    if grp >= 1 and g % 8 == 0:
                        ctx.w("pe", "dve", ("tpgv", grp - 1))
                elif grp >= 2 and g % 8 == 0:
                    ctx.w("pe", "dve", ("tpgv", grp - 2))
                ctx.tp(sl, ib(XN1[tt], 2048)[:, 128 * kc:128 * kc + 128], idn_b, inc=True)
                ctx.mk(("tpk", w, kc, tt), "pe")
                if g % 8 == 7:
                    ctx.mk(("tpgp", grp), "pe")
                dst = ib(STAGE1 + kc * 512 + tt * 128, 128)
                ctx.w("dve", "pe", ("tpgp", grp))
                if w > 0 and g % 64 < 1 and UPTO not in (21, 22, 23):
                    ctx.w("dve", "sxnw", ("xnwm", w - 1))
                ctx.dve("tensor_copy", dst, sl, inc=True)
                ctx.mk(("tpev", g), "dve")
                if g % 8 == 7:
                    ctx.mk(("tpgv", grp), "dve")
                ctx.mk(("ev", w, kc, tt), "dve")
        for kc in range(16 if UPTO not in (21, 22, 23) else 0):
            ctx.w("sync", "dve", ("ev", w, kc, 3))
            ctx.dma("sync", xnt_d[w * 16 + kc, :, :], ib(STAGE1 + kc * 512, 512), "sxnw")
        ctx.mk(("xnwm", w), "sxnw")
    ctx.mk("xnw_all", "sxnw")
    for i in range(4):
        ctx.mk(("xbt", i), "sxb%d" % i)

    ctx.w("sync", "sxnw", ("xnw_all"))
    for wl in range(OWNW if UPTO not in (11, 21, 22, 23) else 0):
        ctx.dma("sync",
                (lambda wl=wl: A[:, :].rearrange("p (kc t) -> p kc t", kc=16)
                 [:, :, wl * 512:wl * 512 + 512]),
                (lambda wl=wl: ctx.h["xnt_flat"][16 * wl:16 * NW + 1][
                    bass.ts((ctx.pidval % 4) * OWNW, 16), :, :]
                 .rearrange("k p c -> p k c")),
                "sA")
    ctx.mk("sA_all", "sA")

    if UPTO <= 1 or UPTO in (11, 21, 22, 23):
        _finish_stub(ctx, T)
        return
    for e in ["pe", "act", "dve"]:
        ctx.mk(("b12", e), e)
    for e in ["pe", "act", "dve"]:
        for o in ["pe", "act", "dve"]:
            if o != e:
                ctx.w(e, o, ("b12", o))
        for i in range(4):
            ctx.w(e, "sxb%d" % i, ("xbt", i))
        ctx.w(e, "sgin", ("p0gp"))
        ctx.w(e, "gpc", ("p0gpc"))

    for hh in range(HPC):
        for wi, (wsb, wd) in enumerate([(wq_sb, wqT_d), (wk_sb, wkT_d), (wv_sb, wvT_d)]):
            if hh > 0:
                ctx.w("gp", "pe", ("projdone", hh - 1))
            for half in range(2):
                ctx.dma("gp",
                        wsb[:, :].rearrange("p (kc d) -> p kc d", kc=16)[:, 8 * half:8 * half + 8],
                        wd[:, 128 * hh:128 * hh + 128]
                        .rearrange("(kc p) d -> p kc d", p=128)[:, 8 * half:8 * half + 8],
                        "swat")
        ctx.mk(("wat", hh), "swat")

        if hh == HPC - 1:
            ctx.dma("gp", wsgu_b[:, :, :, :], wsgu_d[:, :, :, :], "swbs")
            ctx.dma("gp", wsdn_b[:, :, :, :, :], wsdn_d[:, :, :, :, :], "swbs")
            ctx.mk("swb_sh", "swbs")
            for e_m in (0, 1):
                ctx.dma("gp", wgu_b[e_m], wgu_d[e_m], "swb%d" % (e_m % 4))
                ctx.dma("gp", wdn_b[e_m], wdn_d[e_m], "swb%d" % (e_m % 4))
                ctx.mk(("swb_e", e_m), "swb%d" % (e_m % 4))

        def chain(w):
            qb, kb, vb = pf[3 * (w % 2)], pf[3 * (w % 2) + 1], pf[3 * (w % 2) + 2]
            ctx.w("act", "pe", ("projmm", hh, w))
            if (hh, w) != (0, 0):
                pw = (hh, w - 1) if w > 0 else (hh - 1, NW - 1)
                ctx.w("act", "pe", ("ssmm",) + pw)
            ctx.act(icb(Q2, 512), qb[:, :], AF.Square, inc=True)
            ctx.act(icb(K2, 512), kb[:, :], AF.Square, inc=True)
            ctx.act(icb(V2, 512), vb[:, :], AF.Square, inc=True)
            ctx.mk(("sq", hh, w), "act")

            ctx.w("pe", "act", ("sq", hh, w))
            if (hh, w) != (0, 0):
                pw = (hh, w - 1) if w > 0 else (hh - 1, NW - 1)
                ctx.w("pe", "act", ("Rve",) + pw)
            ctx.mm(pf[6][0:1, :], ones_c, icb(Q2, 512), inc=True)
            ctx.mm(pf[6][32:33, :], ones_c, icb(K2, 512), inc=True)
            ctx.mm(pf[6][64:65, :], ones_c, icb(V2, 512), inc=True)
            ctx.mk(("ssmm", hh, w), "pe")

            ctx.w("act", "pe", ("ssmm", hh, w))
            if (hh, w) != (0, 0):
                pw = (hh, w - 1) if w > 0 else (hh - 1, NW - 1)
                ctx.w("act", "dve", ("rows",) + pw)
            ctx.act(Cf[0:1, RQ:RQ + 512], pf[6][0:1, :], AF.Sqrt, inc=True)
            ctx.act(Cf[0:1, RK:RK + 512], pf[6][32:33, :], AF.Sqrt, inc=True)
            ctx.act(Cf[0:1, RV:RV + 512], pf[6][64:65, :], AF.Sqrt, inc=True)
            ctx.mk(("rowsq", hh, w), "act")

            ctx.w("dve", "act", ("rowsq", hh, w))
            for off in (RQ, RK, RV):
                ctx.dve("tensor_scalar", Cf[0:1, off:off + 512], Cf[0:1, off:off + 512],
                        1.0e-8, None, ALU.add, inc=True)
            ctx.mk(("radd", hh, w), "dve")
            ctx.w("dve", "dve", ("radd", hh, w))
            for off in (RQ, RK, RV):
                ctx.dve("reciprocal", Cf[0:1, off:off + 512], Cf[0:1, off:off + 512], inc=True)
            ctx.mk(("rrec", hh, w), "dve")
            ctx.w("dve", "dve", ("rrec", hh, w))
            ctx.w("dve", "scst", ("p0sx"))
            ctx.dve("tensor_scalar", Cf[0:1, RQ:RQ + 512], Cf[0:1, RQ:RQ + 512],
                    tau11, None, ALU.mult, inc=True)
            ctx.mk(("rows", hh, w), "dve")

            ctx.w("pe", "dve", ("rows", hh, w))
            ctx.mm(pf[6][:, :], ones_r, Cf[0:1, RQ:RQ + 512], inc=True)
            ctx.mk(("Rq", hh, w), "pe")
            ctx.w("act", "pe", ("Rq", hh, w))
            if (hh, w) != (0, 0):
                pw = (hh, w - 1) if w > 0 else (hh - 1, NW - 1)
                ctx.w("act", "dve", ("qTev",) + pw)
            ctx.acopy(ic(RQS, 512), pf[6][:, :], inc=True)
            ctx.mk(("Rqe", hh, w), "act")
            ctx.w("pe", "act", ("Rqe", hh, w))
            ctx.mm(pf[6][:, :], ones_r, Cf[0:1, RV:RV + 512], inc=True)
            ctx.mk(("Rv", hh, w), "pe")
            ctx.w("act", "pe", ("Rv", hh, w))
            if (hh, w) != (0, 0):
                pw = (hh, w - 1) if w > 0 else (hh - 1, NW - 1)
                ctx.w("act", "dve", ("vsc",) + pw)
            ctx.acopy(ic(RVS, 512), pf[6][:, :], inc=True)
            ctx.mk(("Rve", hh, w), "act")

            ctx.w("dve", "act", ("Rqe", hh, w))
            if hh > 0 and w == 0:
                ctx.w("dve", "pe", ("flashdone", hh - 1))
            ctx.dve("tensor_mul", B[:, QT + 512 * w:QT + 512 * w + 512], qb[:, :],
                    ic(RQS, 512), inc=True)
            ctx.mk(("qTev", hh, w), "dve")
            ctx.w("act", "pe", ("ssmm", hh, w))
            if hh > 0 and w == 0:
                ctx.w("act", "pe", ("flashdone", hh - 1))
            ctx.acopy(B[:, KT + 512 * w:KT + 512 * w + 512], kb[:, :], inc=True)
            ctx.mk(("kTev", hh, w), "act")
            ctx.w("dve", "act", ("Rve", hh, w))
            ctx.dve("tensor_mul", icb(VTMP, 512), vb[:, :], ic(RVS, 512), inc=True)
            ctx.mk(("vsc", hh, w), "dve")

            for dd in range(4):
                ctx.w("pe", "dve", ("vsc", hh, w))
                if dd == 0:
                    if (hh, w) != (0, 0):
                        pw = (hh, w - 1) if w > 0 else (hh - 1, NW - 1)
                        ctx.w("pe", "act", ("rke",) + pw)
                        ctx.w("pe", "dve", ("vte",) + pw + (3,))
                    if hh > 0 and w == 0:
                        ctx.w("pe", "dve", ("crec0", hh - 1, NPAIRS - 1, 1))
                ctx.tp(pb[:, 128 * dd:128 * dd + 128], icb(VTMP + 64 * dd, 128), idn_b,
                       inc=True)
                ctx.mk(("vt", hh, w, dd), "pe")
            for dd in range(4):
                ctx.w("dve", "pe", ("vt", hh, w, 3))
                if hh > 0 and w == 0:
                    ctx.w("dve", "pe", ("flashdone", hh - 1))
                ctx.dve("tensor_copy",
                        B[:, VT + 512 * w + 128 * dd:VT + 512 * w + 128 * dd + 128],
                        pb[:, 128 * dd:128 * dd + 128], inc=True)
                ctx.mk(("vte", hh, w, dd), "dve")
            pbF2 = pb[:, :].bitcast(F32)
            for dd in range(4):
                ctx.w("pe", "dve", ("rows", hh, w))
                if dd == 0:
                    ctx.w("pe", "dve", ("vte", hh, w, 3))
                ctx.tp(pbF2[:, 384 + dd:384 + dd + 1],
                       Cf[0:1, RK + 128 * dd:RK + 128 * dd + 128],
                       ones_cf[0:1, 0:1], inc=True)
                ctx.mk(("rkt", hh, w, dd), "pe")
            ctx.w("act", "pe", ("rkt", hh, w, 3))
            if hh > 0 and w == 0:
                ctx.w("act", "pe", ("flashdone", hh - 1))
            for dd in range(4):
                ctx.acopy(rk_all[:, 4 * w + dd:4 * w + dd + 1],
                          pbF2[:, 384 + dd:384 + dd + 1], inc=(dd == 3))
            ctx.mk(("rke", hh, w), "act")

        for w in range(NW):
            ctx.w("sync", "sxnw", ("xnw_all"))
            if w >= 2:
                ctx.w("sync", "pe", ("projmm", hh, w - 2))
            elif hh > 0:
                ctx.w("sync", "pe", ("projmm", hh - 1, NW - 2 + w))
            ctx.dma("sync", ib(XW[w % 2], 8192).rearrange("p (kc c) -> p kc c", kc=16),
                    xnt_d[16 * w:16 * w + 16].rearrange("kc p c -> p kc c"),
                    "sxw%d" % (w % 2))
            ctx.mk(("xwl", hh, w), "sxw%d" % (w % 2))

            xw = ib(XW[w % 2], 8192)
            ctx.w("pe", "sxw%d" % (w % 2), ("xwl", hh, w))
            ctx.w("pe", "swat", ("wat", hh))
            if w >= 2:
                ctx.w("pe", "dve", ("qTev", hh, w - 2))
                ctx.w("pe", "act", ("kTev", hh, w - 2))
                ctx.w("pe", "dve", ("vsc", hh, w - 2))
            elif hh > 0:
                ctx.w("pe", "dve", ("qTev", hh - 1, NW - 2 + w))
                ctx.w("pe", "act", ("kTev", hh - 1, NW - 2 + w))
                ctx.w("pe", "dve", ("vsc", hh - 1, NW - 2 + w))
                for bk in range(2):
                    ctx.w("pe", "act", ("Cev", hh - 1, NPAIRS - 1, bk))
                    ctx.w("pe", "dve", ("attn", hh - 1, NPAIRS - 1, bk))
                    ctx.w("pe", "dve", ("crec", hh - 1, NPAIRS - 1, bk))
                    ctx.w("pe", "act", ("attnTe", hh - 1, NPAIRS - 1, bk))
            for (bi, wsb) in [(0, wq_sb), (1, wk_sb), (2, wv_sb)]:
                bank = pf[3 * (w % 2) + bi]
                for kc in range(16):
                    ctx.mm(bank[:, :], wsb[:, 128 * kc:128 * kc + 128],
                           xw[:, 512 * kc:512 * kc + 512],
                           start=(kc == 0), stop=(kc == 15), inc=(kc == 15))
            ctx.mk(("projmm", hh, w), "pe")
            if w > 0:
                chain(w - 1)
        chain(NW - 1)
        ctx.mk(("projdone", hh), "pe")

        st_banks = [[pf[0], pf[1]], [pf[2], pf[3]]]
        out_banks = [pf[4], pf[5]]
        pbF = pb[:, :].bitcast(F32)
        lrow = [pf[6][0:1, :], pbF[0:1, 0:512]]

        for P in range(NPAIRS):
            B0, B1 = 2 * P, 2 * P + 1
            JA = 4 * B0 + 4
            JB = 4 * B1 + 4
            blocks = [B0, B1]

            if P == 0:
                ctx.w("pe", "dve", ("vte", hh, NW - 1, 3))
                ctx.w("pe", "act", ("rke", hh, NW - 1))
                ctx.w("pe", "act", ("kTev", hh, NW - 1))
                ctx.w("pe", "dve", ("qTev", hh, NW - 1))
                ctx.w("pe", "act", ("a_nat"))
                if hh > 0:
                    for bk in range(2):
                        ctx.w("pe", "act", ("Cev", hh - 1, NPAIRS - 1, bk))
            else:
                for bk in range(2):
                    ctx.w("pe", "act", ("Cev", hh, P - 1, bk))
            prevfin = (hh, P - 1) if P > 0 else ((hh - 1, NPAIRS - 1) if hh > 0 else None)

            def active(bk, j):
                return j < (JA if bk == 0 else JB)

            def pt_ap(bk, j):
                idx = 2 * bk + (j % 2)
                return icb(PT[idx], 512)

            def emit_st(j):
                for bk in range(2):
                    if not active(bk, j):
                        continue
                    ctx.mm(st_banks[bk][j % 2][:, :],
                           B[:, KT + 128 * j:KT + 128 * j + 128],
                           B[:, QT + 512 * blocks[bk]:QT + 512 * blocks[bk] + 512],
                           inc=(bk == 1 or not active(1, j)))
                ctx.mk(("st", hh, P, j), "pe")

            def emit_exp(j):
                ctx.w("act", "pe", ("st", hh, P, j))
                if j >= 2:
                    ctx.w("act", "pe", ("pv", hh, P, j - 2))
                for bk in range(2):
                    if not active(bk, j):
                        continue
                    ctx.act(pt_ap(bk, j), st_banks[bk][j % 2][:, :], AF.Exp,
                            bias=natau, scale=rk_all[:, j:j + 1], inc=True)
                ctx.mk(("exp", hh, P, j), "act")
                for bk in range(2):
                    if not active(bk, j):
                        continue
                    dlt = j - 4 * blocks[bk]
                    if 0 <= dlt < 4:
                        ctx.w("dve", "act", ("exp", hh, P, j))
                        ctx.dve("tensor_mul", pt_ap(bk, j), pt_ap(bk, j),
                                maskb[:, 512 * dlt:512 * dlt + 512], inc=True)
                        ctx.mk(("msk", hh, P, j, bk), "dve")

            def emit_fin(bk):
                blkq = blocks[bk]
                crw = CRW + 512 * bk
                last = (JA - 1) if bk == 0 else (JB - 1)
                ctx.w("dve", "pe", ("pv", hh, P, last))
                if P > 0:
                    ctx.w("dve", "dve", ("crec", hh, P - 1, bk))
                elif hh > 0:
                    ctx.w("dve", "dve", ("crec", hh - 1, NPAIRS - 1, bk))
                ctx.dve("reciprocal", Cf[0:1, crw:crw + 512], lrow[bk], inc=True)
                ctx.mk(("crec0", hh, P, bk), "dve")
                ctx.w("dve", "dve", ("crec0", hh, P, bk))
                ctx.dve("tensor_scalar", Cf[0:1, crw:crw + 512], Cf[0:1, crw:crw + 512],
                        nu11, None, ALU.mult, inc=True)
                ctx.mk(("crec", hh, P, bk), "dve")
                ctx.w("pe", "dve", ("crec", hh, P, bk))
                ctx.mm(st_banks[bk][0][:, :], ones_r, Cf[0:1, crw:crw + 512], inc=True)
                ctx.mk(("Cbc", hh, P, bk), "pe")
                ctx.w("act", "pe", ("Cbc", hh, P, bk))
                if P > 0:
                    ctx.w("act", "dve", ("attn", hh, P - 1, bk))
                elif hh > 0:
                    ctx.w("act", "dve", ("attn", hh - 1, NPAIRS - 1, bk))
                ctx.acopy(ic(CSB[bk], 512), st_banks[bk][0][:, :], inc=True)
                ctx.mk(("Cev", hh, P, bk), "act")
                ctx.w("dve", "act", ("Cev", hh, P, bk))
                prevtp = ("attnT", hh, P - 1, bk) if P > 0 else (
                    ("attnT", hh - 1, NPAIRS - 1, bk) if hh > 0 else None)
                if prevtp is not None:
                    ctx.w("dve", "pe", prevtp)
                ctx.dve("tensor_mul", ic(ASTG[bk], 512), out_banks[bk][:, :],
                        ic(CSB[bk], 512), inc=True)
                ctx.mk(("attn", hh, P, bk), "dve")
                pfT = out_banks[bk]
                for cidx in range(4):
                    ctx.w("pe", "dve", ("attn", hh, P, bk))
                    ctx.tp(pfT[:, 128 * cidx:128 * cidx + 128],
                           ic(ASTG[bk] + 128 * cidx, 128), idn_f, inc=(cidx == 3))
                ctx.mk(("attnT", hh, P, bk), "pe")
                ctx.w("act", "pe", ("attnT", hh, P, bk))
                prevdma2 = ("agdma", hh, P - 1, bk) if P > 0 else (
                    ("agdma", hh - 1, NPAIRS - 1, bk) if hh > 0 else None)
                if prevdma2 is not None:
                    ctx.w("act", "sag%d" % bk, prevdma2)
                for cidx in range(4):
                    ctx.acopy(ic(AST2[bk] + 128 * cidx, 128),
                              pfT[:, 128 * cidx:128 * cidx + 128], inc=(cidx == 3))
                ctx.mk(("attnTe", hh, P, bk), "act")
                for cidx in range(4):
                    ctx.w("sync", "act", ("attnTe", hh, P, bk))
                    ctx.dma("sync",
                            agin_d[512 * blkq + 128 * cidx:512 * blkq + 128 * cidx + 128,
                                   128 * hh:128 * hh + 128],
                            ic(AST2[bk] + 128 * cidx, 128), "sag%d" % bk)
                ctx.mk(("agdma", hh, P, bk), "sag%d" % bk)

            def emit_pv(j):
                if j == 0 and prevfin is not None:
                    for bk in range(2):
                        ctx.w("pe", "dve", ("attn",) + prevfin + (bk,))
                        ctx.w("pe", "dve", ("crec",) + prevfin + (bk,))
                        ctx.w("pe", "act", ("attnTe",) + prevfin + (bk,))
                ctx.w("pe", "act", ("exp", hh, P, j))
                for bk in range(2):
                    if not active(bk, j):
                        continue
                    dlt = j - 4 * blocks[bk]
                    if 0 <= dlt < 4:
                        ctx.w("pe", "dve", ("msk", hh, P, j, bk))
                    last = (JA - 1) if bk == 0 else (JB - 1)
                    ctx.mm(out_banks[bk][:, :], B[:, VT + 128 * j:VT + 128 * j + 128],
                           pt_ap(bk, j), start=(j == 0), stop=(j == last))
                    ctx.mm(lrow[bk], ones_c, pt_ap(bk, j),
                           start=(j == 0), stop=(j == last), inc=True)
                ctx.mk(("pv", hh, P, j), "pe")
                for bk in range(2):
                    last = (JA - 1) if bk == 0 else (JB - 1)
                    if j == last:
                        emit_fin(bk)

            emit_st(0)
            if JB > 1:
                emit_st(1)
            for j in range(JB):
                emit_exp(j)
                emit_pv(j)
                if j + 2 < JB:
                    emit_st(j + 2)
        ctx.mk(("flashdone", hh), "pe")

    ctx.mk(("agT", 0), "sag0")
    ctx.mk(("agT", 1), "sag1")
    if UPTO <= 2:
        for e in ["pe", "act", "dve"]:
            ctx.mk(("b24", e), e)
        _finish_stub(ctx, T)
        return
    ctx.w("gp", "sag0", ("agT", 0))
    ctx.w("gp", "sag1", ("agT", 1))
    ctx.coll("AllGather", ALU.bypass, groups, [agin_d[:, :]], [agout_d[:, :]], "scc")

    for e in ["pe", "act", "dve"]:
        ctx.mk(("b24", e), e)
    if UPTO <= 3:
        ctx.w("sync", "scc", 1)
        _finish_stub(ctx, T)
        return
    for e in ["pe", "act", "dve", "gp", "sync"]:
        for o in ["pe", "act", "dve"]:
            if o != e:
                ctx.w(e, o, ("b24", o))

    for sub in range(NSUB):
        ctx.w("pe", "sgin", ("p0gp"))
        ctx.w("pe", "gpc", ("p0gpc"))
        ctx.w("pe", "sA", ("sA_all"))
        if sub > 0:
            ctx.w("pe", "act", ("lgev", sub - 1))
        for kc in range(16):
            ctx.mm(pf[4][0:8, :], gw_sb[:, 8 * kc:8 * kc + 8],
                   A[:, kc * OWN + 512 * sub:kc * OWN + 512 * sub + 512],
                   start=(kc == 0), stop=(kc == 15), inc=(kc == 15))
        ctx.mk(("lgmm", sub), "pe")
        ctx.w("act", "pe", ("lgmm", sub))
        ctx.acopy(Cf[0:8, LSB + 512 * sub:LSB + 512 * sub + 512], pf[4][0:8, :], inc=True)
        ctx.mk(("lgev", sub), "act")
    for tt in range(NT_OWN):
        ctx.w("pe", "act", ("lgev", (tt * 128) // 512))
        ctx.tp(pf[5][:, 8 * tt:8 * tt + 8], Cf[0:8, LSB + 128 * tt:LSB + 128 * tt + 128],
               idn_f[0:8, 0:8], inc=True)
        ctx.mk(("lgt", tt), "pe")
    ctx.w("dve", "pe", ("lgt", NT_OWN - 1))
    ctx.dve("tensor_copy", ic(LTOK, 8 * NT_OWN), pf[5][:, 0:8 * NT_OWN], inc=True)
    ctx.mk("ltok", "dve")
    sc = stats
    for tt in range(NT_OWN):
        ctx.w("dve", "dve", ("ltok"))
        ctx.dve("reduce_max", sc[:, 16 + tt:17 + tt], ic(LTOK + 8 * tt, 8),
                axis=AXX, negate=True, inc=True)
        ctx.mk(("nmx", tt), "dve")
    for tt in range(NT_OWN):
        ctx.w("act", "dve", ("nmx", tt))
        ctx.act(ic(STOK + 8 * tt, 8), ic(LTOK + 8 * tt, 8), AF.Exp,
                bias=sc[:, 16 + tt:17 + tt], accum=sc[:, 32 + tt:33 + tt], inc=True)
        ctx.mk(("sfe", tt), "act")
    for tt in range(NT_OWN):
        ctx.w("dve", "act", ("sfe", tt))
        ctx.dve("reciprocal", sc[:, 48 + tt:49 + tt], sc[:, 32 + tt:33 + tt], inc=True)
        ctx.mk(("srec", tt), "dve")
    for tt in range(NT_OWN):
        ctx.w("dve", "dve", ("srec", tt))
        ctx.dve("tensor_scalar", ic(STOK + 8 * tt, 8), ic(STOK + 8 * tt, 8),
                sc[:, 48 + tt:49 + tt], None, ALU.mult, inc=True)
        ctx.mk(("snrm", tt), "dve")
    for nm, args in [
        ("t2a", None), ("t2b", None), ("t2c", None), ("t2d", None),
        ("t2e", None), ("t2f", None), ("t2g", None)]:
        for tt in range(NT_OWN):
            if nm == "t2a":
                ctx.w("dve", "dve", ("snrm", tt))
                ctx.dve("reduce_max", sc[:, 16 + tt:17 + tt], ic(STOK + 8 * tt, 8),
                        axis=AXX, inc=True)
            elif nm == "t2b":
                ctx.w("dve", "dve", ("t2a", tt))
                ctx.dve("tensor_scalar", ic(CMBT + 8 * tt, 8), ic(STOK + 8 * tt, 8),
                        sc[:, 16 + tt:17 + tt], None, ALU.subtract, inc=True)
            elif nm == "t2c":
                ctx.w("dve", "dve", ("t2b", tt))
                ctx.dve("tensor_single_scalar", ic(CMBT + 8 * tt, 8), ic(CMBT + 8 * tt, 8),
                        0.0, ALU.is_equal, inc=True)
            elif nm == "t2d":
                ctx.w("dve", "dve", ("t2c", tt))
                ctx.dve("scalar_tensor_tensor", ic(CMBT + 8 * tt, 8), ic(CMBT + 8 * tt, 8),
                        -1.0e30, ic(STOK + 8 * tt, 8), ALU.mult, ALU.add, inc=True)
            elif nm == "t2e":
                ctx.w("dve", "dve", ("t2d", tt))
                ctx.dve("reduce_max", sc[:, 16 + tt:17 + tt], ic(CMBT + 8 * tt, 8),
                        axis=AXX, inc=True)
            elif nm == "t2f":
                ctx.w("dve", "dve", ("t2e", tt))
                ctx.dve("tensor_scalar", ic(CMBT + 8 * tt, 8), ic(STOK + 8 * tt, 8),
                        sc[:, 16 + tt:17 + tt], None, ALU.is_ge, inc=True)
            elif nm == "t2g":
                ctx.w("dve", "dve", ("t2f", tt))
                ctx.dve("tensor_mul", ic(CMBT + 8 * tt, 8), ic(CMBT + 8 * tt, 8),
                        ic(STOK + 8 * tt, 8), inc=True)
            ctx.mk((nm, tt), "dve")
    for ch in range(NT_OWN // 4):
        for tt4 in range(4):
            tt = 4 * ch + tt4
            ctx.w("pe", "dve", ("t2g", tt))
            if ch > 0 and tt4 == 0:
                ctx.w("pe", "dve", ("cmbev", ch - 1))
            ctx.tp(pf[6][0:8, 128 * tt4:128 * tt4 + 128], ic(CMBT + 8 * tt, 8),
                   idn_f, inc=True)
            ctx.mk(("cbt", tt), "pe")
        ctx.w("dve", "pe", ("cbt", 4 * ch + 3))
        ctx.dve("tensor_copy", Cf[0:8, CMBS + 512 * ch:CMBS + 512 * ch + 512],
                pf[6][0:8, 0:512], inc=True)
        ctx.mk(("cmbev", ch), "dve")
    ctx.mk("cmbev", "dve")
    ctx.w("sync", "dve", ("cmbev"))
    ctx.dma("sync", combd_d[:, :], Cf[0:8, CMBS:CMBS + OWN], "scmb")
    ctx.mk("cmbw", "scmb")

    ydma = [0]
    for u in range(10):
        is_sh = u < 2
        e = u - 2
        if 2 <= u <= 7:
            ctx.dma("gp", wgu_b[u], wgu_d[u], "swb%d" % (u % 4))
            ctx.dma("gp", wdn_b[u], wdn_d[u], "swb%d" % (u % 4))
            ctx.mk(("swb_e", u), "swb%d" % (u % 4))
        if not is_sh:
            ctx.w("sync", "scmb", ("cmbw"))
            if u > 2:
                ctx.w("sync", "pe", ("cbmm", u - 1))
            ctx.dma("sync", Cf[0:1, CROW2:CROW2 + OWN], combd_d[e:e + 1, :], "scmb")
            ctx.mk(("crl", u), "scmb")
            for sub in range(NSUB):
                ctx.w("pe", "scmb", ("crl", u))
                if sub > 0:
                    ctx.w("pe", "act", ("cbe", u, sub - 1))
                elif u > 2:
                    ctx.w("pe", "dve", ("dnall", u - 1))
                    ctx.w("pe", "act", ("cbe", u - 1, NSUB - 1))
                ctx.mm(pf[6][:, :], ones_r,
                       Cf[0:1, CROW2 + 512 * sub:CROW2 + 512 * sub + 512], inc=True)
                ctx.mk(("cbm", u, sub), "pe")
                ctx.w("act", "pe", ("cbm", u, sub))
                if u > 2:
                    ctx.w("act", "dve", ("dnall", u - 1))
                ctx.acopy(ic(CBW + 512 * sub, 512), pf[6][:, :], inc=True)
                ctx.mk(("cbe", u, sub), "act")
            ctx.mk(("cbmm", u), "pe")

        for p in range(11):
            buf = p % 2
            wg = ib(WGU[buf], 4096)
            if p >= 2:
                ctx.w("sync", "pe", ("guc", u, p - 2))
            elif u > 0:
                ctx.w("sync", "pe", ("guc", u - 1, 9 + p))
            else:
                ctx.w("sync", "pe", ("flashdone", HPC - 1))
            if is_sh:
                ctx.w("sync", "swbs", ("swb_sh"))
            else:
                ctx.w("sync", "swb%d" % (e % 4), ("swb_e", e))
            src = wsgu_b[11 * u + p] if is_sh else wgu_b[e, p]
            for half in range(2):
                ctx.dma("sync",
                        wg[:, :].rearrange("p (kc c) -> p kc c", kc=16)[:, 8 * half:8 * half + 8],
                        src[8 * half:8 * half + 8].rearrange("kc p c -> p kc c"),
                        "swf%d" % (p % 2))
            ctx.mk(("gul", u, p), "swf%d" % (p % 2))

            for sub in range(NSUB):
                gb, ub_ = pf[0 + 2 * (p % 2)], pf[1 + 2 * (p % 2)]
                ctx.w("pe", "swf%d" % (p % 2), ("gul", u, p))
                ctx.w("pe", "sA", ("sA_all"))
                if sub > 0:
                    ctx.w("pe", "act", ("silu", u, p, sub - 1))
                    ctx.w("pe", "dve", ("actm", u, p, sub - 1))
                elif p >= 2:
                    ctx.w("pe", "act", ("silu", u, p - 2, NSUB - 1))
                    ctx.w("pe", "dve", ("actm", u, p - 2, NSUB - 1))
                elif u > 0:
                    ctx.w("pe", "act", ("silu", u - 1, 9 + p, NSUB - 1))
                    ctx.w("pe", "dve", ("actm", u - 1, 9 + p, NSUB - 1))
                for kc in range(16):
                    ctx.mm(gb[:, :], wg[:, 256 * kc:256 * kc + 128],
                           A[:, kc * OWN + 512 * sub:kc * OWN + 512 * sub + 512],
                           start=(kc == 0), stop=(kc == 15), inc=(kc == 15))
                ctx.mk(("gumm", u, p, sub), "pe")
                for kc in range(16):
                    ctx.mm(ub_[:, :], wg[:, 256 * kc + 128:256 * kc + 256],
                           A[:, kc * OWN + 512 * sub:kc * OWN + 512 * sub + 512],
                           start=(kc == 0), stop=(kc == 15), inc=(kc == 15))
                ctx.mk(("upmm", u, p, sub), "pe")
                ctx.w("act", "pe", ("gumm", u, p, sub))
                if (u, p, sub) == (0, 0, 0):
                    ctx.w("act", "scmb", ("cmbw"))
                if sub > 0:
                    ctx.w("act", "dve", ("actm", u, p, sub - 1))
                elif p >= 2:
                    ctx.w("act", "dve", ("actm", u, p - 2, NSUB - 1))
                elif u > 0:
                    ctx.w("act", "dve", ("actm", u - 1, 9 + p, NSUB - 1))
                sgo = ic(SG[p % 2], 512)
                ctx.act(sgo, gb[:, :], AF.Silu, inc=True)
                ctx.mk(("silu", u, p, sub), "act")
                ctx.w("dve", "act", ("silu", u, p, sub))
                ctx.w("dve", "pe", ("upmm", u, p, sub))
                if u == 0 and p < 2:
                    ctx.w("dve", "pe", ("flashdone", HPC - 1))
                ctx.dve("tensor_mul",
                        B[:, ACT4 + p * OWN + 512 * sub:ACT4 + p * OWN + 512 * sub + 512],
                        sgo, ub_[:, :], inc=True)
                ctx.mk(("actm", u, p, sub), "dve")
            ctx.mk(("guc", u, p), "pe")
        ctx.mk(("actall", u), "dve")

        for q in range(4):
            buf = q % 2
            wd = ib(WDN[buf], 5632)
            if q >= 2:
                ctx.w("sync", "pe", ("dnc", u, q - 2))
            elif u > 0:
                ctx.w("sync", "pe", ("dnc", u - 1, 2 + q))
            else:
                ctx.w("sync", "pe", ("flashdone", HPC - 1))
            if is_sh:
                ctx.w("sync", "swbs", ("swb_sh"))
            else:
                ctx.w("sync", "swb%d" % (e % 4), ("swb_e", e))
            src = wsdn_b[u, q] if is_sh else wdn_b[e, q]
            ctx.dma("sync", wd[:, :].rearrange("p (fc c) -> p fc c", fc=11)[:, 0:6],
                    src[0:6].rearrange("fc p c -> p fc c"), "swd%d" % (q % 2))
            ctx.dma("sync", wd[:, :].rearrange("p (fc c) -> p fc c", fc=11)[:, 6:11],
                    src[6:11].rearrange("fc p c -> p fc c"), "swd%d" % (q % 2))
            ctx.mk(("dnl", u, q), "swd%d" % (q % 2))

            for hcl in range(4):
                hc = 4 * q + hcl
                for sub in range(NSUB):
                    pidx = hcl * NSUB + sub
                    bank = pf[4 + pidx % 2]
                    ctx.w("pe", "swd%d" % (q % 2), ("dnl", u, q))
                    ctx.w("pe", "dve", ("actall", u))
                    pv = None
                    if pidx >= 2:
                        pv = ("dnev", u, q, pidx - 2)
                    elif q > 0:
                        pv = ("dnev", u, q - 1, 4 * NSUB - 2 + pidx)
                    elif u > 0:
                        pv = ("dnev", u - 1, 3, 4 * NSUB - 2 + pidx)
                    if pv is not None:
                        ctx.w("pe", "dve", pv)
                    for fc in range(11):
                        ctx.mm(bank[:, :],
                               wd[:, 512 * fc + 128 * hcl:512 * fc + 128 * hcl + 128],
                               B[:, ACT4 + fc * OWN + 512 * sub:ACT4 + fc * OWN + 512 * sub + 512],
                               start=(fc == 0), stop=(fc == 10), inc=(fc == 10))
                    ctx.mk(("dnmm", u, q, pidx), "pe")
                    if pidx == 4 * NSUB - 1:
                        ctx.mk(("dnc", u, q), "pe")
                    stg = ic(STG[pidx % 4], 512)
                    di = ydma[0]
                    ssem = "syt%d" % (di % 4)
                    ctx.w("dve", "pe", ("dnmm", u, q, pidx))
                    if not is_sh:
                        ctx.w("dve", "act", ("cbe", u, sub))
                    if di >= 4:
                        ctx.w("dve", ssem, ("ytc", di - 4))
                    if di == 0:
                        ctx.w("dve", "scmb", ("cmbw"))
                    if is_sh:
                        ctx.dve("tensor_copy", stg, bank[:, :], inc=True)
                    else:
                        ctx.dve("tensor_mul", stg, bank[:, :], ic(CBW + 512 * sub, 512),
                                inc=True)
                    ctx.mk(("dnev", u, q, pidx), "dve")
                    ctx.w("gp", "dve", ("dnev", u, q, pidx))
                    if u > 0 and di == ctx.marks.get(("ytfu", u), -1):
                        for i in range(4):
                            ctx.w("gp", "syt%d" % i, ("ytallS", u - 1, i))
                    ctx.dma("gp", ytacc_d[128 * hc:128 * hc + 128, 512 * sub:512 * sub + 512],
                            stg, ssem, accum=None if u == 0 else ALU.add)
                    ctx.mk(("ytc", di), ssem)
                    ydma[0] += 1
        for i in range(4):
            ctx.mk(("ytallS", u, i), "syt%d" % i)
        if ctx.mode == "count":
            ctx.marks[("ytfu", u + 1)] = ydma[0]
        ctx.mk(("dnall", u), "dve")

    for i in range(4):
        ctx.mk(("ytT", i), "syt%d" % i)
    ctx.mk(("swfT", 0), "swf0")
    ctx.mk(("swfT", 1), "swf1")
    ctx.mk(("swdT", 0), "swd0")
    ctx.mk(("swdT", 1), "swd1")

    for e in ["pe", "act", "dve"]:
        ctx.mk(("b45", e), e)
    for e in ["pe", "act", "dve", "sync"]:
        for o in ["pe", "act", "dve"]:
            if o != e:
                ctx.w(e, o, ("b45", o))

    if UPTO <= 4:
        for i in range(4):
            ctx.w("sync", "syt%d" % i, ("ytT", i))
        _finish_stub(ctx, T)
        return
    Bf = B[:, :].bitcast(F32)
    EXB = [0, 2048]
    ATKB = [4096, 6144]
    EOB = [8192, 10240]
    YTB = [8192, 8704]
    TMPB = [9216, 9344]

    NTI = OWN // 128
    ctx.w("sync", "scc", 1)
    ctx.dma("sync", xloc_d[0:OWN, :],
            (lambda: x_d[:, :][bass.ts(ctx.pidval % 4, OWN), :]), "sloc")
    for rp in range(4):
        ctx.dma("sync", agloc_d[rp, 0:OWN, :],
                (lambda rp=rp: agout_d[rp * T:4 * T, :][
                    bass.ts(ctx.pidval % 4, OWN), :]), "sloc")
    ctx.mk("sloc_all", "sloc")
    ctx.w("sync", "sloc", ("sloc_all"))
    for i in range(4):
        ctx.w("sync", "syt%d" % i, ("ytT", i))
    ctx.w("sync", "swf0", ("swfT", 0))
    ctx.w("sync", "swf1", ("swfT", 1))
    ctx.w("sync", "swd0", ("swdT", 0))
    ctx.w("sync", "swd1", ("swdT", 1))
    for gi in range(NTI):
        if gi >= 2:
            ctx.w("sync", "dve", ("o2", gi - 2, 15))
        ctx.dma("sync", Bf[:, EXB[gi % 2]:EXB[gi % 2] + 2048],
                xloc_d[128 * gi:128 * gi + 128, :], "sepx%d" % (gi % 2))
        ctx.mk(("exl", gi), "sepx%d" % (gi % 2))
        for rp in range(4):
            ctx.dma("sync",
                    Bf[:, ATKB[gi % 2] + 512 * rp:ATKB[gi % 2] + 512 * rp + 512],
                    agloc_d[rp, 128 * gi:128 * gi + 128, :], "sepa%d" % (gi % 2))
        ctx.mk(("atk", gi), "sepa%d" % (gi % 2))
        for k4 in range(4):
            ki = 4 * gi + k4
            ybuf = ic(YTB[ki % 2], 512)
            if ki >= 2:
                ctx.w("sync", "pe", ("ytpl", ki - 2))
            ctx.dma("sync", ybuf.rearrange("p (k t) -> p k t", k=4),
                    ytacc_d[512 * k4:512 * k4 + 512, 128 * gi:128 * gi + 128]
                    .rearrange("(k p) t -> p k t", p=128), "sepy%d" % (ki % 2))
            ctx.mk(("ytl", ki), "sepy%d" % (ki % 2))
        def etp(kc):
            tg = 16 * gi + kc
            pslot = pf[tg % 2][:, 128 * ((tg // 2) % 4):128 * ((tg // 2) % 4) + 128]
            ctx.w("pe", "sepy%d" % ((4 * gi + kc // 4) % 2), ("ytl", 4 * gi + kc // 4))
            if tg >= 2:
                ctx.w("pe", "dve", ("o1", (tg - 2) // 16, (tg - 2) % 16))
            ctx.tp(pslot, ic(YTB[(4 * gi + kc // 4) % 2] + 128 * (kc % 4), 128),
                   idn_f, inc=True)
            ctx.mk(("ytp", tg), "pe")
            if kc % 4 == 3:
                ctx.mk(("ytpl", 4 * gi + kc // 4), "pe")
            return pslot

        slots = {}
        def eo1(kc):
            tg = 16 * gi + kc
            pslot = pf[tg % 2][:, 128 * ((tg // 2) % 4):128 * ((tg // 2) % 4) + 128]
            ctx.w("dve", "pe", ("ytp", tg))
            ctx.w("dve", "sepa%d" % (gi % 2), ("atk", gi))
            if kc >= 2:
                ctx.w("dve", "dve", ("o2", gi, kc - 2))
            ctx.dve("tensor_add", ic(TMPB[kc % 2], 128), pslot,
                    Bf[:, ATKB[gi % 2] + 128 * kc:ATKB[gi % 2] + 128 * kc + 128], inc=True)
            ctx.mk(("o1", gi, kc), "dve")

        def eo2(kc):
            ctx.w("dve", "dve", ("o1", gi, kc))
            ctx.w("dve", "sepx%d" % (gi % 2), ("exl", gi))
            if kc == 0 and gi >= 2:
                ctx.w("dve", "sout%d" % (gi % 2), ("outd", gi - 2))
            ctx.dve("tensor_add", Bf[:, EOB[gi % 2] + 128 * kc:EOB[gi % 2] + 128 * kc + 128],
                    ic(TMPB[kc % 2], 128),
                    Bf[:, EXB[gi % 2] + 128 * kc:EXB[gi % 2] + 128 * kc + 128], inc=True)
            ctx.mk(("o2", gi, kc), "dve")

        for kc in range(16):
            etp(kc)
            if kc >= 1:
                eo1(kc - 1)
            if kc >= 2:
                eo2(kc - 2)
        eo1(15)
        eo2(14)
        eo2(15)
        ctx.w("sync", "dve", ("o2", gi, 15))
        ctx.dma("sync", out_d[128 * gi:128 * gi + 128, :],
                Bf[:, EOB[gi % 2]:EOB[gi % 2] + 2048], "sout%d" % (gi % 2))
        ctx.mk(("outd", gi), "sout%d" % (gi % 2))
    ctx.w("sync", "sout0", ctx.cnt["sout0"])
    ctx.w("sync", "sout1", ctx.cnt["sout1"])


def build_program(T):
    OWN = T // 4
    NW = T // 512
    nc = bass.Bass("TRN2", target_bir_lowering=False, debug=False)

    h = {"groups": [[0, 1, 2, 3], [4, 5, 6, 7]]}

    def din(name, shp):
        return nc.dram_tensor(name, shp, F32, kind="ExternalInput").ap()

    h["x_d"] = din("x", [T, C])
    h["wqT_d"] = din("wqT", [C, 512])
    h["wkT_d"] = din("wkT", [C, 512])
    h["wvT_d"] = din("wvT", [C, 512])
    h["tau_d"] = din("tau", [1])
    h["nu_d"] = din("nu", [1])
    h["gw_d"] = din("gw", [16, 128, 8])
    h["wgu_d"] = din("wgu", [NE, 11, 16, 128, 256])
    h["wdn_d"] = din("wdn", [NE, 4, 11, 128, 512])
    h["wsgu_d"] = din("wsgu", [22, 16, 128, 256])
    h["wsdn_d"] = din("wsdn", [2, 4, 11, 128, 512])
    h["cmask_d"] = din("cmask", [128, 4, 512])
    h["idn_d"] = din("idn", [128, 128])
    h["out_d"] = nc.dram_tensor("out", [OWN, C], F32, kind="ExternalOutput").ap()

    xnt = nc.dram_tensor("xnt", [NW * 16 + 1, 128, 512], BF16, kind="Internal")
    h["xnt_d"] = xnt.ap()
    h["xnt_flat"] = xnt.ap()
    h["agin_d"] = nc.dram_tensor("agin", [T, 512], F32, kind="Internal").ap()
    h["agout_d"] = nc.dram_tensor("agout", [4 * T, 512], F32, kind="Internal").ap()
    h["combd_d"] = nc.dram_tensor("combd", [NE, OWN], F32, kind="Internal").ap()
    h["ytacc_d"] = nc.dram_tensor("ytacc", [C, OWN], F32, kind="Internal").ap()
    h["agloc_d"] = nc.dram_tensor("agloc", [4, OWN + 128, 512], F32, kind="Internal").ap()
    h["wgu_b"] = nc.dram_tensor("wgu_b", [NE, 11, 16, 128, 256], BF16, kind="Internal").ap()
    h["wdn_b"] = nc.dram_tensor("wdn_b", [NE, 4, 11, 128, 512], BF16, kind="Internal").ap()
    h["wsgu_b"] = nc.dram_tensor("wsgu_b", [22, 16, 128, 256], BF16, kind="Internal").ap()
    h["wsdn_b"] = nc.dram_tensor("wsdn_b", [2, 4, 11, 128, 512], BF16, kind="Internal").ap()
    h["xloc_d"] = nc.dram_tensor("xloc", [OWN + 128, C], F32, kind="Internal").ap()

    BCOLS = max(3 * T + 16384, 8192 + 2 * 5632 + 11 * OWN)
    CCOLS = 10240

    es = ExitStack()
    with es:
        h["A"] = es.enter_context(nc.sbuf_tensor("arnA", [128, 16 * OWN], BF16))[:, :]
        h["B"] = es.enter_context(nc.sbuf_tensor("arnB", [128, BCOLS], BF16))[:, :]
        h["C"] = es.enter_context(nc.sbuf_tensor("arnC", [128, CCOLS], F32))[:, :]
        for nm, shp, dt in [("idn_f", [128, 128], F32), ("idn_b", [128, 128], BF16),
                            ("ones_r", [1, 128], F32), ("ones_c", [128, 1], BF16),
                            ("ones_cf", [128, 1], F32),
                            ("stats", [128, 96], F32), ("rk_all", [128, 64], F32),
                            ("maskb", [128, 2048], BF16), ("natau", [128, 1], F32),
                            ("epsln", [128, 1], F32), ("tau11", [1, 1], F32),
                            ("nu11", [1, 1], F32), ("gw_sb", [128, 128], BF16),
                            ("wq_sb", [128, 2048], BF16), ("wk_sb", [128, 2048], BF16),
                            ("wv_sb", [128, 2048], BF16)]:
            h[nm] = es.enter_context(nc.sbuf_tensor(nm, shp, dt))[:, :]
        pf = []
        for i in range(7):
            pf.append(es.enter_context(nc.psum_tensor("pf%d" % i, [128, 512], F32))[:, :])
        h["pf"] = pf
        h["pb"] = es.enter_context(nc.psum_tensor("pb", [128, 1024], BF16))[:, :]
        for s in SEMS:
            h[s] = es.enter_context(nc.semaphore(s))

        marks = {}
        cc = Ctx(nc, "count", h, marks)
        emit_all(cc, T)
        final = dict(cc.cnt)

        blk = es.enter_context(nc.Block())

        def run(mode, pid=None):
            c2 = Ctx(nc, mode, h, marks)
            if pid is not None:
                c2.pidval = pid
            emit_all(c2, T)
            for k in SEMS:
                assert c2.cnt[k] == final[k], (mode, k, c2.cnt[k], final[k])

        @blk.tensor
        def _(t):
            run("pe")

        @blk.scalar
        def _(s):
            run("act")

        @blk.vector
        def _(v):
            run("dve")

        @blk.sync
        def _(s):
            pid = nc.partition_id(engines=[mybir.EngineType.SP])
            run("sync", pid=pid)

        @blk.gpsimd
        def _(g):
            run("gp")

    return nc


def kernel(L0_attn_x, L0_attn_wq, L0_attn_wk, L0_attn_wv, L0_attn_tau, L0_attn_nu,
           L0_ffn_gate_weight, L0_ffn_gate_up_weights, L0_ffn_down_weights,
           L0_ffn_shared_gate_up, L0_ffn_shared_down):
    x = np.asarray(L0_attn_x)
    Bn, T, _ = x.shape
    OWN = T // 4

    wq = np.asarray(L0_attn_wq)
    wk = np.asarray(L0_attn_wk)
    wv = np.asarray(L0_attn_wv)
    tau = np.asarray(L0_attn_tau).reshape(1).astype(np.float32)
    nu = np.asarray(L0_attn_nu).reshape(1).astype(np.float32)
    gwt = np.ascontiguousarray(np.asarray(L0_ffn_gate_weight).T).reshape(16, 128, 8)
    guw = np.asarray(L0_ffn_gate_up_weights)
    dw = np.asarray(L0_ffn_down_weights)
    sgu = np.asarray(L0_ffn_shared_gate_up)
    sdn = np.asarray(L0_ffn_shared_down)

    Ag = guw.transpose(0, 2, 1).reshape(NE, 16, 128, 22, 128)
    wgu = np.ascontiguousarray(
        np.stack([Ag[:, :, :, 0:11, :], Ag[:, :, :, 11:22, :]], axis=4)
        .transpose(0, 3, 1, 2, 4, 5).reshape(NE, 11, 16, 128, 256))
    Dd = dw.transpose(0, 2, 1).reshape(NE, 11, 128, 4, 512)
    wdn = np.ascontiguousarray(Dd.transpose(0, 3, 1, 2, 4))
    Sg = np.ascontiguousarray(sgu.T).reshape(16, 128, 44, 128)
    wsgu = np.ascontiguousarray(
        np.stack([Sg[:, :, 0:22, :], Sg[:, :, 22:44, :]], axis=3)
        .transpose(2, 0, 1, 3, 4).reshape(22, 16, 128, 256))
    Sd = np.ascontiguousarray(sdn.T).reshape(2, 11, 128, 4, 512)
    wsdn = np.ascontiguousarray(Sd.transpose(0, 3, 1, 2, 4))

    p = np.arange(128)[:, None]
    f = np.arange(512)[None, :]
    cmask = np.stack([(128 * d + p <= f).astype(np.float32) for d in range(4)], axis=1)
    cmask = np.ascontiguousarray(cmask)
    idn = np.eye(128, dtype=np.float32)

    _enable_ldw_opt()
    nc = build_program(T)

    in_maps = []
    for c in range(NCORES):
        b = c // 4
        r = c % 4
        in_maps.append(dict(
            x=np.ascontiguousarray(x[b]),
            wqT=np.ascontiguousarray(wq.T[:, 512 * r:512 * r + 512]),
            wkT=np.ascontiguousarray(wk.T[:, 512 * r:512 * r + 512]),
            wvT=np.ascontiguousarray(wv.T[:, 512 * r:512 * r + 512]),
            tau=tau, nu=nu, gw=gwt, wgu=wgu, wdn=wdn, wsgu=wsgu, wsdn=wsdn,
            cmask=cmask, idn=idn,
        ))
    res = run_bass_kernel_spmd(nc, in_maps, core_ids=list(range(NCORES)))
    out = np.empty((Bn, T, C), dtype=np.float32)
    for c in range(NCORES):
        b, r = c // 4, c % 4
        out[b, r * OWN:(r + 1) * OWN, :] = res.results[c]["out"]
    return out


# version 09
# speedup vs baseline: 1.0198x; 1.0021x over previous
import numpy as np
from contextlib import ExitStack

import concourse.bass as bass
import concourse.mybir as mybir
from concourse.bass_utils import run_bass_kernel_spmd


def _enable_ldw_opt():
    try:
        from concourse.compiler_utils import get_compiler_flags, set_compiler_flags
        flags = get_compiler_flags()
        newflags = [f.replace("--enable-ldw-opt=false", "--enable-ldw-opt=true")
                    for f in flags]
        set_compiler_flags(newflags)
    except Exception:
        pass

F32 = mybir.dt.float32
BF16 = mybir.dt.bfloat16
AF = mybir.ActivationFunctionType
ALU = mybir.AluOpType
AXX = mybir.AxisListType.X

C = 2048
D = 128
HPC = 4
NE = 8
NCORES = 8

SEMS = (["pe", "act", "dve", "scst", "sxnw", "sA", "swat", "scmb", "scc",
         "sgin", "gpc"]
        + ["sxb%d" % i for i in range(4)] + ["sxw%d" % i for i in range(2)]
        + ["swf%d" % i for i in range(2)] + ["swd%d" % i for i in range(2)]
        + ["sag%d" % i for i in range(2)] + ["syt%d" % i for i in range(4)]
        + ["sepx%d" % i for i in range(2)] + ["sepa%d" % i for i in range(2)]
        + ["sepy%d" % i for i in range(2)] + ["sout%d" % i for i in range(2)]
        + ["sloc", "swbs"] + ["swb%d" % i for i in range(8)])


class Ctx:

    def __init__(self, nc, mode, h, marks):
        self.nc = nc
        self.mode = mode
        self.h = h
        self.marks = marks
        self.cnt = {k: 0 for k in SEMS}
        self.pidval = 0

    def mk(self, key, sem):
        v = self.cnt[sem]
        if self.mode == "count":
            self.marks[key] = v
        else:
            assert self.marks[key] == v, (key, self.marks[key], v, self.mode)
        return v

    def m(self, key):
        return self.marks[key]

    def w(self, eng, sem, val):
        if self.mode != eng:
            return
        if isinstance(val, (tuple, str)):
            val = self.marks[val]
        if val is None or val <= 0:
            return
        self._eng(eng).wait_ge(self.h[sem], val)

    def _eng(self, eng):
        nc = self.nc
        return {"pe": nc.tensor, "act": nc.scalar, "dve": nc.vector,
                "sync": nc.sync, "gp": nc.gpsimd}[eng]

    def _fin(self, inst, sem, amount):
        self.cnt[sem] += amount
        if inst is not None:
            inst.then_inc(self.h[sem], amount)

    def mm(self, out, lhsT, rhs, start=True, stop=True, inc=False):
        inst = None
        if self.mode == "pe":
            inst = self.nc.tensor.matmul(out, lhsT, rhs, start=start, stop=stop)
        if inc:
            self._fin(inst, "pe", 1)

    def tp(self, out, in_, idn, inc=False):
        inst = None
        if self.mode == "pe":
            inst = self.nc.tensor.transpose(out, in_, idn)
        if inc:
            self._fin(inst, "pe", 1)

    def act(self, out, in_, func, bias=0.0, scale=1.0, accum=None, inc=False):
        inst = None
        if self.mode == "act":
            kw = {}
            if accum is not None:
                kw["accum_out"] = accum
            inst = self.nc.scalar.activation(out, in_, func, bias=bias, scale=scale, **kw)
        if inc:
            self._fin(inst, "act", 1)

    def acopy(self, out, in_, inc=False):
        inst = None
        if self.mode == "act":
            inst = self.nc.scalar.copy(out, in_)
        if inc:
            self._fin(inst, "act", 1)

    def amul(self, out, in_, mulv, inc=False):
        inst = None
        if self.mode == "act":
            inst = self.nc.scalar.mul(out, in_, mulv)
        if inc:
            self._fin(inst, "act", 1)

    def dve(self, meth, *args, inc=False, **kw):
        inst = None
        if self.mode == "dve":
            inst = getattr(self.nc.vector, meth)(*args, **kw)
        if inc:
            self._fin(inst, "dve", 1)

    def dma(self, queue, out, in_, sem, accum=None):
        inst = None
        if self.mode == queue:
            if callable(out):
                out = out()
            if callable(in_):
                in_ = in_()
            eng = self._eng(queue)
            try:
                if accum is None:
                    inst = eng.dma_start(out=out, in_=in_)
                else:
                    inst = eng.dma_start(out=out, in_=in_, accum_op=accum)
            except Exception as e:
                import traceback
                print("DMA FAIL: out=", out, " in=", in_)
                tb = e.__traceback__
                while tb is not None:
                    print("  frame:", tb.tb_frame.f_code.co_filename, tb.tb_lineno,
                          tb.tb_frame.f_code.co_name)
                    tb = tb.tb_next
                raise
        self._fin(inst, sem, 16)

    def coll(self, kind, op, groups, ins, outs, sem):
        inst = None
        if self.mode == "gp":
            inst = self.nc.gpsimd.collective_compute(kind, op, replica_groups=groups,
                                                     ins=ins, outs=outs)
        self.cnt[sem] += 1
        if inst is not None:
            inst.then_inc(self.h[sem])

    def gmemset(self, ap, val, sem=None):
        inst = None
        if self.mode == "gp":
            inst = self.nc.gpsimd.memset(ap, val)
        if sem is not None:
            self._fin(inst, sem, 1)


UPTO = 9


def _finish_stub(ctx, T):
    OWN = T // 4
    out_d = ctx.h["out_d"]
    B = ctx.h["B"]
    for e in ["pe", "act", "dve"]:
        for o in ["pe", "act", "dve"]:
            if o != e:
                ctx.w(e, o, ctx.cnt[o])
    for i in range(OWN // 128):
        ctx.dma("sync", out_d[128 * i:128 * i + 128, :],
                B[:, 0:4096].bitcast(mybir.dt.float32), "sout0")
    ctx.w("sync", "sout0", ctx.cnt["sout0"])


def emit_all(ctx, T):
    h = ctx.h
    OWN = T // 4
    NW = T // 512
    OWNW = OWN // 512
    NSUB = OWN // 512
    NQB = T // 512
    NPAIRS = NQB // 2
    NEPI = OWN // 256
    NT_OWN = OWN // 128

    A = h["A"]; B = h["B"]; Cf = h["C"]
    Cb = Cf[:, :].bitcast(BF16)
    pf = h["pf"]; pb = h["pb"]
    idn_f = h["idn_f"]; idn_b = h["idn_b"]
    ones_r = h["ones_r"]; ones_c = h["ones_c"]; ones_cf = h["ones_cf"]
    stats = h["stats"]; rk_all = h["rk_all"]; maskb = h["maskb"]
    natau = h["natau"]; epsln = h["epsln"]
    tau11 = h["tau11"]; nu11 = h["nu11"]
    gw_sb = h["gw_sb"]; wq_sb = h["wq_sb"]; wk_sb = h["wk_sb"]; wv_sb = h["wv_sb"]

    x_d = h["x_d"]; wqT_d = h["wqT_d"]; wkT_d = h["wkT_d"]; wvT_d = h["wvT_d"]
    tau_d = h["tau_d"]; nu_d = h["nu_d"]; gw_d = h["gw_d"]
    wgu_d = h["wgu_d"]; wdn_d = h["wdn_d"]; wsgu_d = h["wsgu_d"]; wsdn_d = h["wsdn_d"]
    cmask_d = h["cmask_d"]; idn_d = h["idn_d"]
    xnt_d = h["xnt_d"]; agin_d = h["agin_d"]; agout_d = h["agout_d"]
    combd_d = h["combd_d"]; ytacc_d = h["ytacc_d"]; out_d = h["out_d"]
    agloc_d = h["agloc_d"]; xloc_d = h["xloc_d"]
    wgu_b = h["wgu_b"]; wdn_b = h["wdn_b"]; wsgu_b = h["wsgu_b"]; wsdn_b = h["wsdn_b"]
    groups = h["groups"]

    ib = lambda off, n: B[:, off:off + n]
    ic = lambda off, n: Cf[:, off:off + n]
    icb = lambda f32off, n: Cb[:, 2 * f32off:2 * f32off + n]

    XB = [0, 2048, 4096, 6144]; TRASH = 8192
    PT = [0, 256, 512, 768]
    CSB = [1024, 1536]
    ASTG = [2048, 2560]
    RQ, RV, RK, CRW = 3072, 3584, 4096, 4608
    Q2, K2, V2, VTMP = 5632, 5888, 6144, 6400
    RQS, RVS = 6656, 7168
    AST2 = [7680, 8192]
    SG = [0, 512]
    STG = [1024, 1536, 2048, 2560]
    CBW = 3072
    LTOK, STOK, CMBT = 5120, 5248, 5376
    LSB = 5504
    CROW2 = 5504
    CMBS = 0
    EX = [0, 2048]; EO = [4096, 6144]
    EAG = [8192, 8448]; EYT = [8704, 8960]; ES1 = [9216, 9472]

    QT, KT, VT = 0, T, 2 * T
    XW = [3 * T, 3 * T + 8192]
    STAGE1 = 0
    XN1 = [8192, 10240, 12288, 14336]
    WGU = [0, 4096]
    WDN = [8192, 8192 + 5632]
    ACT4 = 8192 + 2 * 5632

    ctx.dma("sync", h["idn_f"], idn_d[:, :], "scst")
    ctx.dma("sync", h["tau11"], tau_d[:].unsqueeze(0), "scst")
    ctx.dma("sync", h["nu11"], nu_d[:].unsqueeze(0), "scst")
    ctx.mk(("p0sx"), "scst")
    ctx.dma("gp", h["idn_b"], idn_d[:, :], "sgin")
    ctx.dma("gp", maskb[:, :].rearrange("p (d c) -> p d c", d=4), cmask_d[:, :, :], "sgin")
    ctx.dma("gp", gw_sb[:, :].rearrange("p (kc e) -> p kc e", kc=16),
            gw_d[:, :, :].rearrange("kc p e -> p kc e"), "sgin")
    ctx.mk("p0gp", "sgin")
    ctx.gmemset(ones_r, 1.0, sem="gpc")
    ctx.gmemset(ones_c, 1.0, sem="gpc")
    ctx.gmemset(ones_cf, 1.0, sem="gpc")
    ctx.gmemset(epsln, 1.0e-5, sem="gpc")
    ctx.mk("p0gpc", "gpc")

    ctx.w("act", "scst", ("p0sx"))
    ctx.act(stats[0:1, 60:61], tau11, AF.Abs, inc=True)
    ctx.mk("a_abs", "act")
    ctx.w("act", "act", ("a_abs"))
    ctx.amul(stats[0:1, 61:62], stats[0:1, 60:61], -1.0, inc=True)
    ctx.mk("a_neg", "act")
    ctx.w("pe", "act", ("a_neg"))
    ctx.w("pe", "sgin", ("p0gp"))
    ctx.w("pe", "gpc", ("p0gpc"))
    ctx.mm(pf[0][:, 0:1], ones_r, stats[0:1, 61:62], inc=True)
    ctx.mk("p_nat", "pe")
    ctx.w("act", "pe", ("p_nat"))
    ctx.acopy(natau, pf[0][:, 0:1], inc=True)
    ctx.mk("a_nat", "act")

    if UPTO <= 0:
        _finish_stub(ctx, T)
        return

    tpg = [0]

    for w in range(NW):
        for tt in range(4):
            gi = 4 * w + tt
            xb = ic(XB[tt], 2048)
            if gi >= 4:
                ctx.w("sync", "dve", ("xn", gi - 4))
            ctx.dma("sync", xb, x_d[512 * w + 128 * tt:512 * w + 128 * tt + 128, :],
                    "sxb%d" % tt)
            ctx.mk(("xld", gi), "sxb%d" % tt)
        for tt in range(4):
            gi = 4 * w + tt
            xb = ic(XB[tt], 2048)
            st = stats[:, 64 + 8 * tt:64 + 8 * tt + 8]
            ctx.w("dve", "sxb%d" % tt, ("xld", gi))
            if gi >= 4:
                ctx.w("dve", "dve", ("xn", gi - 4))
            ctx.dve("reduce_sum", st[:, 0:1], xb, axis=AXX, inc=True)
            ctx.mk(("sum", gi), "dve")
            ctx.w("act", "sxb%d" % tt, ("xld", gi))
            if gi >= 4:
                ctx.w("act", "dve", ("var", gi - 4))
            if gi >= 1:
                ctx.w("act", "act", ("ssq", gi - 1))
            ctx.act(ic(TRASH, 2048), xb, AF.Square, accum=st[:, 1:2], inc=True)
            ctx.mk(("ssq", gi), "act")
        for tt in range(4):
            gi = 4 * w + tt
            st = stats[:, 64 + 8 * tt:64 + 8 * tt + 8]
            ctx.w("dve", "dve", ("sum", gi))
            ctx.dve("tensor_scalar", st[:, 2:3], st[:, 0:1], 1.0 / C, None, ALU.mult, inc=True)
            ctx.mk(("mean", gi), "dve")
        for tt in range(4):
            gi = 4 * w + tt
            st = stats[:, 64 + 8 * tt:64 + 8 * tt + 8]
            ctx.w("dve", "dve", ("mean", gi))
            ctx.dve("tensor_mul", st[:, 3:4], st[:, 2:3], st[:, 2:3], inc=True)
            ctx.mk(("mm2", gi), "dve")
        for tt in range(4):
            gi = 4 * w + tt
            st = stats[:, 64 + 8 * tt:64 + 8 * tt + 8]
            ctx.w("dve", "act", ("ssq", gi))
            ctx.w("dve", "dve", ("mm2", gi))
            ctx.dve("scalar_tensor_tensor", st[:, 4:5], st[:, 1:2], 1.0 / C, st[:, 3:4],
                    ALU.mult, ALU.subtract, inc=True)
            ctx.mk(("var", gi), "dve")
        for tt in range(4):
            gi = 4 * w + tt
            st = stats[:, 64 + 8 * tt:64 + 8 * tt + 8]
            ctx.w("act", "dve", ("var", gi))
            if gi == 0:
                ctx.w("act", "gpc", ("p0gpc"))
            ctx.act(st[:, 5:6], st[:, 4:5], AF.Sqrt, bias=epsln, inc=True)
            ctx.mk(("sqv", gi), "act")
        for tt in range(4):
            gi = 4 * w + tt
            st = stats[:, 64 + 8 * tt:64 + 8 * tt + 8]
            ctx.w("dve", "act", ("sqv", gi))
            ctx.dve("reciprocal", st[:, 6:7], st[:, 5:6], inc=True)
            ctx.mk(("rstd", gi), "dve")
        for tt in range(4):
            gi = 4 * w + tt
            st = stats[:, 64 + 8 * tt:64 + 8 * tt + 8]
            xb = ic(XB[tt], 2048)
            xnb = ib(XN1[tt], 2048)
            ctx.w("dve", "dve", ("rstd", gi))
            if w > 0 and UPTO != 21:
                ctx.w("dve", "pe", ("tpk", w - 1, 15, tt))
            ctx.dve("tensor_scalar", xnb, xb, st[:, 2:3], st[:, 6:7],
                    ALU.subtract, ALU.mult, inc=True)
            ctx.mk(("xn", gi), "dve")
        pbF6 = pf[6][:, :].bitcast(BF16)
        for kc in range(16 if UPTO != 21 else 0):
            for tt in range(4):
                gi = 4 * w + tt
                g = tpg[0]; tpg[0] += 1
                grp = g // 8
                if UPTO == 23:
                    bankap = pb
                else:
                    bankap = pb if grp % 2 == 0 else pbF6
                sl = bankap[:, 128 * (g % 8):128 * (g % 8) + 128]
                ctx.w("pe", "dve", ("xn", gi))
                if UPTO == 23:
                    if grp >= 1 and g % 8 == 0:
                        ctx.w("pe", "dve", ("tpgv", grp - 1))
                elif grp >= 2 and g % 8 == 0:
                    ctx.w("pe", "dve", ("tpgv", grp - 2))
                ctx.tp(sl, ib(XN1[tt], 2048)[:, 128 * kc:128 * kc + 128], idn_b, inc=True)
                ctx.mk(("tpk", w, kc, tt), "pe")
                if g % 8 == 7:
                    ctx.mk(("tpgp", grp), "pe")
                dst = ib(STAGE1 + kc * 512 + tt * 128, 128)
                ctx.w("dve", "pe", ("tpgp", grp))
                if w > 0 and g % 64 < 1 and UPTO not in (21, 22, 23):
                    ctx.w("dve", "sxnw", ("xnwm", w - 1))
                ctx.dve("tensor_copy", dst, sl, inc=True)
                ctx.mk(("tpev", g), "dve")
                if g % 8 == 7:
                    ctx.mk(("tpgv", grp), "dve")
                ctx.mk(("ev", w, kc, tt), "dve")
        for kc in range(16 if UPTO not in (21, 22, 23) else 0):
            ctx.w("sync", "dve", ("ev", w, kc, 3))
            ctx.dma("sync", xnt_d[w * 16 + kc, :, :], ib(STAGE1 + kc * 512, 512), "sxnw")
        ctx.mk(("xnwm", w), "sxnw")
    ctx.mk("xnw_all", "sxnw")
    for i in range(4):
        ctx.mk(("xbt", i), "sxb%d" % i)

    ctx.w("sync", "sxnw", ("xnw_all"))
    for wl in range(OWNW if UPTO not in (11, 21, 22, 23) else 0):
        ctx.dma("sync",
                (lambda wl=wl: A[:, :].rearrange("p (kc t) -> p kc t", kc=16)
                 [:, :, wl * 512:wl * 512 + 512]),
                (lambda wl=wl: ctx.h["xnt_flat"][16 * wl:16 * NW + 1][
                    bass.ts((ctx.pidval % 4) * OWNW, 16), :, :]
                 .rearrange("k p c -> p k c")),
                "sA")
    ctx.mk("sA_all", "sA")

    if UPTO <= 1 or UPTO in (11, 21, 22, 23):
        _finish_stub(ctx, T)
        return
    for e in ["pe", "act", "dve"]:
        ctx.mk(("b12", e), e)
    for e in ["pe", "act", "dve"]:
        for o in ["pe", "act", "dve"]:
            if o != e:
                ctx.w(e, o, ("b12", o))
        for i in range(4):
            ctx.w(e, "sxb%d" % i, ("xbt", i))
        ctx.w(e, "sgin", ("p0gp"))
        ctx.w(e, "gpc", ("p0gpc"))

    for hh in range(HPC):
        for wi, (wsb, wd) in enumerate([(wq_sb, wqT_d), (wk_sb, wkT_d), (wv_sb, wvT_d)]):
            if hh > 0:
                ctx.w("gp", "pe", ("projdone", hh - 1))
            for half in range(2):
                ctx.dma("gp",
                        wsb[:, :].rearrange("p (kc d) -> p kc d", kc=16)[:, 8 * half:8 * half + 8],
                        wd[:, 128 * hh:128 * hh + 128]
                        .rearrange("(kc p) d -> p kc d", p=128)[:, 8 * half:8 * half + 8],
                        "swat")
        ctx.mk(("wat", hh), "swat")

        if hh == HPC - 1:
            ctx.w("gp", "pe", ("projdone", hh))
            ctx.dma("gp", wsgu_b[:, :, :, :], wsgu_d[:, :, :, :], "swbs")
            ctx.dma("gp", wsdn_b[:, :, :, :, :], wsdn_d[:, :, :, :, :], "swbs")
            ctx.mk("swb_sh", "swbs")
            for e_m in (0, 1):
                ctx.dma("gp", wgu_b[e_m], wgu_d[e_m], "swb%d" % e_m)
                ctx.dma("gp", wdn_b[e_m], wdn_d[e_m], "swb%d" % e_m)
                ctx.mk(("swb_e", e_m), "swb%d" % e_m)

        def chain(w):
            qb, kb, vb = pf[3 * (w % 2)], pf[3 * (w % 2) + 1], pf[3 * (w % 2) + 2]
            ctx.w("act", "pe", ("projmm", hh, w))
            if (hh, w) != (0, 0):
                pw = (hh, w - 1) if w > 0 else (hh - 1, NW - 1)
                ctx.w("act", "pe", ("ssmm",) + pw)
            ctx.act(icb(Q2, 512), qb[:, :], AF.Square, inc=True)
            ctx.act(icb(K2, 512), kb[:, :], AF.Square, inc=True)
            ctx.act(icb(V2, 512), vb[:, :], AF.Square, inc=True)
            ctx.mk(("sq", hh, w), "act")

            ctx.w("pe", "act", ("sq", hh, w))
            if (hh, w) != (0, 0):
                pw = (hh, w - 1) if w > 0 else (hh - 1, NW - 1)
                ctx.w("pe", "act", ("Rve",) + pw)
            ctx.mm(pf[6][0:1, :], ones_c, icb(Q2, 512), inc=True)
            ctx.mm(pf[6][32:33, :], ones_c, icb(K2, 512), inc=True)
            ctx.mm(pf[6][64:65, :], ones_c, icb(V2, 512), inc=True)
            ctx.mk(("ssmm", hh, w), "pe")

            ctx.w("act", "pe", ("ssmm", hh, w))
            if (hh, w) != (0, 0):
                pw = (hh, w - 1) if w > 0 else (hh - 1, NW - 1)
                ctx.w("act", "dve", ("rows",) + pw)
            ctx.act(Cf[0:1, RQ:RQ + 512], pf[6][0:1, :], AF.Sqrt, inc=True)
            ctx.act(Cf[0:1, RK:RK + 512], pf[6][32:33, :], AF.Sqrt, inc=True)
            ctx.act(Cf[0:1, RV:RV + 512], pf[6][64:65, :], AF.Sqrt, inc=True)
            ctx.mk(("rowsq", hh, w), "act")

            ctx.w("dve", "act", ("rowsq", hh, w))
            for off in (RQ, RK, RV):
                ctx.dve("tensor_scalar", Cf[0:1, off:off + 512], Cf[0:1, off:off + 512],
                        1.0e-8, None, ALU.add, inc=True)
            ctx.mk(("radd", hh, w), "dve")
            ctx.w("dve", "dve", ("radd", hh, w))
            for off in (RQ, RK, RV):
                ctx.dve("reciprocal", Cf[0:1, off:off + 512], Cf[0:1, off:off + 512], inc=True)
            ctx.mk(("rrec", hh, w), "dve")
            ctx.w("dve", "dve", ("rrec", hh, w))
            ctx.w("dve", "scst", ("p0sx"))
            ctx.dve("tensor_scalar", Cf[0:1, RQ:RQ + 512], Cf[0:1, RQ:RQ + 512],
                    tau11, None, ALU.mult, inc=True)
            ctx.mk(("rows", hh, w), "dve")

            ctx.w("pe", "dve", ("rows", hh, w))
            ctx.mm(pf[6][:, :], ones_r, Cf[0:1, RQ:RQ + 512], inc=True)
            ctx.mk(("Rq", hh, w), "pe")
            ctx.w("act", "pe", ("Rq", hh, w))
            if (hh, w) != (0, 0):
                pw = (hh, w - 1) if w > 0 else (hh - 1, NW - 1)
                ctx.w("act", "dve", ("qTev",) + pw)
            ctx.acopy(ic(RQS, 512), pf[6][:, :], inc=True)
            ctx.mk(("Rqe", hh, w), "act")
            ctx.w("pe", "act", ("Rqe", hh, w))
            ctx.mm(pf[6][:, :], ones_r, Cf[0:1, RV:RV + 512], inc=True)
            ctx.mk(("Rv", hh, w), "pe")
            ctx.w("act", "pe", ("Rv", hh, w))
            if (hh, w) != (0, 0):
                pw = (hh, w - 1) if w > 0 else (hh - 1, NW - 1)
                ctx.w("act", "dve", ("vsc",) + pw)
            ctx.acopy(ic(RVS, 512), pf[6][:, :], inc=True)
            ctx.mk(("Rve", hh, w), "act")

            ctx.w("dve", "act", ("Rqe", hh, w))
            if hh > 0 and w == 0:
                ctx.w("dve", "pe", ("flashdone", hh - 1))
            ctx.dve("tensor_mul", B[:, QT + 512 * w:QT + 512 * w + 512], qb[:, :],
                    ic(RQS, 512), inc=True)
            ctx.mk(("qTev", hh, w), "dve")
            ctx.w("act", "pe", ("ssmm", hh, w))
            if hh > 0 and w == 0:
                ctx.w("act", "pe", ("flashdone", hh - 1))
            ctx.acopy(B[:, KT + 512 * w:KT + 512 * w + 512], kb[:, :], inc=True)
            ctx.mk(("kTev", hh, w), "act")
            ctx.w("dve", "act", ("Rve", hh, w))
            ctx.dve("tensor_mul", icb(VTMP, 512), vb[:, :], ic(RVS, 512), inc=True)
            ctx.mk(("vsc", hh, w), "dve")

            for dd in range(4):
                ctx.w("pe", "dve", ("vsc", hh, w))
                if dd == 0:
                    if (hh, w) != (0, 0):
                        pw = (hh, w - 1) if w > 0 else (hh - 1, NW - 1)
                        ctx.w("pe", "act", ("rke",) + pw)
                        ctx.w("pe", "dve", ("vte",) + pw + (3,))
                    if hh > 0 and w == 0:
                        ctx.w("pe", "dve", ("crec0", hh - 1, NPAIRS - 1, 1))
                ctx.tp(pb[:, 128 * dd:128 * dd + 128], icb(VTMP + 64 * dd, 128), idn_b,
                       inc=True)
                ctx.mk(("vt", hh, w, dd), "pe")
            for dd in range(4):
                ctx.w("dve", "pe", ("vt", hh, w, 3))
                if hh > 0 and w == 0:
                    ctx.w("dve", "pe", ("flashdone", hh - 1))
                ctx.dve("tensor_copy",
                        B[:, VT + 512 * w + 128 * dd:VT + 512 * w + 128 * dd + 128],
                        pb[:, 128 * dd:128 * dd + 128], inc=True)
                ctx.mk(("vte", hh, w, dd), "dve")
            pbF2 = pb[:, :].bitcast(F32)
            for dd in range(4):
                ctx.w("pe", "dve", ("rows", hh, w))
                if dd == 0:
                    ctx.w("pe", "dve", ("vte", hh, w, 3))
                ctx.tp(pbF2[:, 384 + dd:384 + dd + 1],
                       Cf[0:1, RK + 128 * dd:RK + 128 * dd + 128],
                       ones_cf[0:1, 0:1], inc=True)
                ctx.mk(("rkt", hh, w, dd), "pe")
            ctx.w("act", "pe", ("rkt", hh, w, 3))
            if hh > 0 and w == 0:
                ctx.w("act", "pe", ("flashdone", hh - 1))
            for dd in range(4):
                ctx.acopy(rk_all[:, 4 * w + dd:4 * w + dd + 1],
                          pbF2[:, 384 + dd:384 + dd + 1], inc=(dd == 3))
            ctx.mk(("rke", hh, w), "act")

        for w in range(NW):
            ctx.w("sync", "sxnw", ("xnw_all"))
            if w >= 2:
                ctx.w("sync", "pe", ("projmm", hh, w - 2))
            elif hh > 0:
                ctx.w("sync", "pe", ("projmm", hh - 1, NW - 2 + w))
            ctx.dma("sync", ib(XW[w % 2], 8192).rearrange("p (kc c) -> p kc c", kc=16),
                    xnt_d[16 * w:16 * w + 16].rearrange("kc p c -> p kc c"),
                    "sxw%d" % (w % 2))
            ctx.mk(("xwl", hh, w), "sxw%d" % (w % 2))

            xw = ib(XW[w % 2], 8192)
            ctx.w("pe", "sxw%d" % (w % 2), ("xwl", hh, w))
            ctx.w("pe", "swat", ("wat", hh))
            if w >= 2:
                ctx.w("pe", "dve", ("qTev", hh, w - 2))
                ctx.w("pe", "act", ("kTev", hh, w - 2))
                ctx.w("pe", "dve", ("vsc", hh, w - 2))
            elif hh > 0:
                ctx.w("pe", "dve", ("qTev", hh - 1, NW - 2 + w))
                ctx.w("pe", "act", ("kTev", hh - 1, NW - 2 + w))
                ctx.w("pe", "dve", ("vsc", hh - 1, NW - 2 + w))
                for bk in range(2):
                    ctx.w("pe", "act", ("Cev", hh - 1, NPAIRS - 1, bk))
                    ctx.w("pe", "dve", ("attn", hh - 1, NPAIRS - 1, bk))
                    ctx.w("pe", "dve", ("crec", hh - 1, NPAIRS - 1, bk))
                    ctx.w("pe", "act", ("attnTe", hh - 1, NPAIRS - 1, bk))
            for (bi, wsb) in [(0, wq_sb), (1, wk_sb), (2, wv_sb)]:
                bank = pf[3 * (w % 2) + bi]
                for kc in range(16):
                    ctx.mm(bank[:, :], wsb[:, 128 * kc:128 * kc + 128],
                           xw[:, 512 * kc:512 * kc + 512],
                           start=(kc == 0), stop=(kc == 15), inc=(kc == 15))
            ctx.mk(("projmm", hh, w), "pe")
            if w > 0:
                chain(w - 1)
        chain(NW - 1)
        ctx.mk(("projdone", hh), "pe")

        st_banks = [[pf[0], pf[1]], [pf[2], pf[3]]]
        out_banks = [pf[4], pf[5]]
        pbF = pb[:, :].bitcast(F32)
        lrow = [pf[6][0:1, :], pbF[0:1, 0:512]]

        for P in range(NPAIRS):
            B0, B1 = 2 * P, 2 * P + 1
            JA = 4 * B0 + 4
            JB = 4 * B1 + 4
            blocks = [B0, B1]

            if P == 0:
                ctx.w("pe", "dve", ("vte", hh, NW - 1, 3))
                ctx.w("pe", "act", ("rke", hh, NW - 1))
                ctx.w("pe", "act", ("kTev", hh, NW - 1))
                ctx.w("pe", "dve", ("qTev", hh, NW - 1))
                ctx.w("pe", "act", ("a_nat"))
                if hh > 0:
                    for bk in range(2):
                        ctx.w("pe", "act", ("Cev", hh - 1, NPAIRS - 1, bk))
            else:
                for bk in range(2):
                    ctx.w("pe", "act", ("Cev", hh, P - 1, bk))
            prevfin = (hh, P - 1) if P > 0 else ((hh - 1, NPAIRS - 1) if hh > 0 else None)

            def active(bk, j):
                return j < (JA if bk == 0 else JB)

            def pt_ap(bk, j):
                idx = 2 * bk + (j % 2)
                return icb(PT[idx], 512)

            def emit_st(j):
                for bk in range(2):
                    if not active(bk, j):
                        continue
                    ctx.mm(st_banks[bk][j % 2][:, :],
                           B[:, KT + 128 * j:KT + 128 * j + 128],
                           B[:, QT + 512 * blocks[bk]:QT + 512 * blocks[bk] + 512],
                           inc=(bk == 1 or not active(1, j)))
                ctx.mk(("st", hh, P, j), "pe")

            def emit_exp(j):
                ctx.w("act", "pe", ("st", hh, P, j))
                if j >= 2:
                    ctx.w("act", "pe", ("pv", hh, P, j - 2))
                for bk in range(2):
                    if not active(bk, j):
                        continue
                    ctx.act(pt_ap(bk, j), st_banks[bk][j % 2][:, :], AF.Exp,
                            bias=natau, scale=rk_all[:, j:j + 1], inc=True)
                ctx.mk(("exp", hh, P, j), "act")
                for bk in range(2):
                    if not active(bk, j):
                        continue
                    dlt = j - 4 * blocks[bk]
                    if 0 <= dlt < 4:
                        ctx.w("dve", "act", ("exp", hh, P, j))
                        ctx.dve("tensor_mul", pt_ap(bk, j), pt_ap(bk, j),
                                maskb[:, 512 * dlt:512 * dlt + 512], inc=True)
                        ctx.mk(("msk", hh, P, j, bk), "dve")

            def emit_fin(bk):
                blkq = blocks[bk]
                crw = CRW + 512 * bk
                last = (JA - 1) if bk == 0 else (JB - 1)
                ctx.w("dve", "pe", ("pv", hh, P, last))
                if P > 0:
                    ctx.w("dve", "dve", ("crec", hh, P - 1, bk))
                elif hh > 0:
                    ctx.w("dve", "dve", ("crec", hh - 1, NPAIRS - 1, bk))
                ctx.dve("reciprocal", Cf[0:1, crw:crw + 512], lrow[bk], inc=True)
                ctx.mk(("crec0", hh, P, bk), "dve")
                ctx.w("dve", "dve", ("crec0", hh, P, bk))
                ctx.dve("tensor_scalar", Cf[0:1, crw:crw + 512], Cf[0:1, crw:crw + 512],
                        nu11, None, ALU.mult, inc=True)
                ctx.mk(("crec", hh, P, bk), "dve")
                ctx.w("pe", "dve", ("crec", hh, P, bk))
                ctx.mm(st_banks[bk][0][:, :], ones_r, Cf[0:1, crw:crw + 512], inc=True)
                ctx.mk(("Cbc", hh, P, bk), "pe")
                ctx.w("act", "pe", ("Cbc", hh, P, bk))
                if P > 0:
                    ctx.w("act", "dve", ("attn", hh, P - 1, bk))
                elif hh > 0:
                    ctx.w("act", "dve", ("attn", hh - 1, NPAIRS - 1, bk))
                ctx.acopy(ic(CSB[bk], 512), st_banks[bk][0][:, :], inc=True)
                ctx.mk(("Cev", hh, P, bk), "act")
                ctx.w("dve", "act", ("Cev", hh, P, bk))
                prevtp = ("attnT", hh, P - 1, bk) if P > 0 else (
                    ("attnT", hh - 1, NPAIRS - 1, bk) if hh > 0 else None)
                if prevtp is not None:
                    ctx.w("dve", "pe", prevtp)
                ctx.dve("tensor_mul", ic(ASTG[bk], 512), out_banks[bk][:, :],
                        ic(CSB[bk], 512), inc=True)
                ctx.mk(("attn", hh, P, bk), "dve")
                pfT = out_banks[bk]
                for cidx in range(4):
                    ctx.w("pe", "dve", ("attn", hh, P, bk))
                    ctx.tp(pfT[:, 128 * cidx:128 * cidx + 128],
                           ic(ASTG[bk] + 128 * cidx, 128), idn_f, inc=(cidx == 3))
                ctx.mk(("attnT", hh, P, bk), "pe")
                ctx.w("act", "pe", ("attnT", hh, P, bk))
                prevdma2 = ("agdma", hh, P - 1, bk) if P > 0 else (
                    ("agdma", hh - 1, NPAIRS - 1, bk) if hh > 0 else None)
                if prevdma2 is not None:
                    ctx.w("act", "sag%d" % bk, prevdma2)
                for cidx in range(4):
                    ctx.acopy(ic(AST2[bk] + 128 * cidx, 128),
                              pfT[:, 128 * cidx:128 * cidx + 128], inc=(cidx == 3))
                ctx.mk(("attnTe", hh, P, bk), "act")
                for cidx in range(4):
                    ctx.w("sync", "act", ("attnTe", hh, P, bk))
                    ctx.dma("sync",
                            agin_d[512 * blkq + 128 * cidx:512 * blkq + 128 * cidx + 128,
                                   128 * hh:128 * hh + 128],
                            ic(AST2[bk] + 128 * cidx, 128), "sag%d" % bk)
                ctx.mk(("agdma", hh, P, bk), "sag%d" % bk)

            def emit_pv(j):
                if j == 0 and prevfin is not None:
                    for bk in range(2):
                        ctx.w("pe", "dve", ("attn",) + prevfin + (bk,))
                        ctx.w("pe", "dve", ("crec",) + prevfin + (bk,))
                        ctx.w("pe", "act", ("attnTe",) + prevfin + (bk,))
                ctx.w("pe", "act", ("exp", hh, P, j))
                for bk in range(2):
                    if not active(bk, j):
                        continue
                    dlt = j - 4 * blocks[bk]
                    if 0 <= dlt < 4:
                        ctx.w("pe", "dve", ("msk", hh, P, j, bk))
                    last = (JA - 1) if bk == 0 else (JB - 1)
                    ctx.mm(out_banks[bk][:, :], B[:, VT + 128 * j:VT + 128 * j + 128],
                           pt_ap(bk, j), start=(j == 0), stop=(j == last))
                    ctx.mm(lrow[bk], ones_c, pt_ap(bk, j),
                           start=(j == 0), stop=(j == last), inc=True)
                ctx.mk(("pv", hh, P, j), "pe")
                for bk in range(2):
                    last = (JA - 1) if bk == 0 else (JB - 1)
                    if j == last:
                        emit_fin(bk)

            emit_st(0)
            if JB > 1:
                emit_st(1)
            for j in range(JB):
                emit_exp(j)
                emit_pv(j)
                if j + 2 < JB:
                    emit_st(j + 2)
        ctx.mk(("flashdone", hh), "pe")

    ctx.mk(("agT", 0), "sag0")
    ctx.mk(("agT", 1), "sag1")
    if UPTO <= 2:
        for e in ["pe", "act", "dve"]:
            ctx.mk(("b24", e), e)
        _finish_stub(ctx, T)
        return
    ctx.w("gp", "sag0", ("agT", 0))
    ctx.w("gp", "sag1", ("agT", 1))
    ctx.coll("AllGather", ALU.bypass, groups, [agin_d[:, :]], [agout_d[:, :]], "scc")

    for e in ["pe", "act", "dve"]:
        ctx.mk(("b24", e), e)
    if UPTO <= 3:
        ctx.w("sync", "scc", 1)
        _finish_stub(ctx, T)
        return
    for e in ["pe", "act", "dve", "gp", "sync"]:
        for o in ["pe", "act", "dve"]:
            if o != e:
                ctx.w(e, o, ("b24", o))

    for sub in range(NSUB):
        ctx.w("pe", "sgin", ("p0gp"))
        ctx.w("pe", "gpc", ("p0gpc"))
        ctx.w("pe", "sA", ("sA_all"))
        if sub > 0:
            ctx.w("pe", "act", ("lgev", sub - 1))
        for kc in range(16):
            ctx.mm(pf[4][0:8, :], gw_sb[:, 8 * kc:8 * kc + 8],
                   A[:, kc * OWN + 512 * sub:kc * OWN + 512 * sub + 512],
                   start=(kc == 0), stop=(kc == 15), inc=(kc == 15))
        ctx.mk(("lgmm", sub), "pe")
        ctx.w("act", "pe", ("lgmm", sub))
        ctx.acopy(Cf[0:8, LSB + 512 * sub:LSB + 512 * sub + 512], pf[4][0:8, :], inc=True)
        ctx.mk(("lgev", sub), "act")
    for tt in range(NT_OWN):
        ctx.w("pe", "act", ("lgev", (tt * 128) // 512))
        ctx.tp(pf[5][:, 8 * tt:8 * tt + 8], Cf[0:8, LSB + 128 * tt:LSB + 128 * tt + 128],
               idn_f[0:8, 0:8], inc=True)
        ctx.mk(("lgt", tt), "pe")
    ctx.w("dve", "pe", ("lgt", NT_OWN - 1))
    ctx.dve("tensor_copy", ic(LTOK, 8 * NT_OWN), pf[5][:, 0:8 * NT_OWN], inc=True)
    ctx.mk("ltok", "dve")
    sc = stats
    for tt in range(NT_OWN):
        ctx.w("dve", "dve", ("ltok"))
        ctx.dve("reduce_max", sc[:, 16 + tt:17 + tt], ic(LTOK + 8 * tt, 8),
                axis=AXX, negate=True, inc=True)
        ctx.mk(("nmx", tt), "dve")
    for tt in range(NT_OWN):
        ctx.w("act", "dve", ("nmx", tt))
        ctx.act(ic(STOK + 8 * tt, 8), ic(LTOK + 8 * tt, 8), AF.Exp,
                bias=sc[:, 16 + tt:17 + tt], accum=sc[:, 32 + tt:33 + tt], inc=True)
        ctx.mk(("sfe", tt), "act")
    for tt in range(NT_OWN):
        ctx.w("dve", "act", ("sfe", tt))
        ctx.dve("reciprocal", sc[:, 48 + tt:49 + tt], sc[:, 32 + tt:33 + tt], inc=True)
        ctx.mk(("srec", tt), "dve")
    for tt in range(NT_OWN):
        ctx.w("dve", "dve", ("srec", tt))
        ctx.dve("tensor_scalar", ic(STOK + 8 * tt, 8), ic(STOK + 8 * tt, 8),
                sc[:, 48 + tt:49 + tt], None, ALU.mult, inc=True)
        ctx.mk(("snrm", tt), "dve")
    for nm, args in [
        ("t2a", None), ("t2b", None), ("t2c", None), ("t2d", None),
        ("t2e", None), ("t2f", None), ("t2g", None)]:
        for tt in range(NT_OWN):
            if nm == "t2a":
                ctx.w("dve", "dve", ("snrm", tt))
                ctx.dve("reduce_max", sc[:, 16 + tt:17 + tt], ic(STOK + 8 * tt, 8),
                        axis=AXX, inc=True)
            elif nm == "t2b":
                ctx.w("dve", "dve", ("t2a", tt))
                ctx.dve("tensor_scalar", ic(CMBT + 8 * tt, 8), ic(STOK + 8 * tt, 8),
                        sc[:, 16 + tt:17 + tt], None, ALU.subtract, inc=True)
            elif nm == "t2c":
                ctx.w("dve", "dve", ("t2b", tt))
                ctx.dve("tensor_single_scalar", ic(CMBT + 8 * tt, 8), ic(CMBT + 8 * tt, 8),
                        0.0, ALU.is_equal, inc=True)
            elif nm == "t2d":
                ctx.w("dve", "dve", ("t2c", tt))
                ctx.dve("scalar_tensor_tensor", ic(CMBT + 8 * tt, 8), ic(CMBT + 8 * tt, 8),
                        -1.0e30, ic(STOK + 8 * tt, 8), ALU.mult, ALU.add, inc=True)
            elif nm == "t2e":
                ctx.w("dve", "dve", ("t2d", tt))
                ctx.dve("reduce_max", sc[:, 16 + tt:17 + tt], ic(CMBT + 8 * tt, 8),
                        axis=AXX, inc=True)
            elif nm == "t2f":
                ctx.w("dve", "dve", ("t2e", tt))
                ctx.dve("tensor_scalar", ic(CMBT + 8 * tt, 8), ic(STOK + 8 * tt, 8),
                        sc[:, 16 + tt:17 + tt], None, ALU.is_ge, inc=True)
            elif nm == "t2g":
                ctx.w("dve", "dve", ("t2f", tt))
                ctx.dve("tensor_mul", ic(CMBT + 8 * tt, 8), ic(CMBT + 8 * tt, 8),
                        ic(STOK + 8 * tt, 8), inc=True)
            ctx.mk((nm, tt), "dve")
    for ch in range(NT_OWN // 4):
        for tt4 in range(4):
            tt = 4 * ch + tt4
            ctx.w("pe", "dve", ("t2g", tt))
            if ch > 0 and tt4 == 0:
                ctx.w("pe", "dve", ("cmbev", ch - 1))
            ctx.tp(pf[6][0:8, 128 * tt4:128 * tt4 + 128], ic(CMBT + 8 * tt, 8),
                   idn_f, inc=True)
            ctx.mk(("cbt", tt), "pe")
        ctx.w("dve", "pe", ("cbt", 4 * ch + 3))
        ctx.dve("tensor_copy", Cf[0:8, CMBS + 512 * ch:CMBS + 512 * ch + 512],
                pf[6][0:8, 0:512], inc=True)
        ctx.mk(("cmbev", ch), "dve")
    ctx.mk("cmbev", "dve")
    ctx.w("sync", "dve", ("cmbev"))
    ctx.dma("sync", combd_d[:, :], Cf[0:8, CMBS:CMBS + OWN], "scmb")
    ctx.mk("cmbw", "scmb")

    ydma = [0]
    for u in range(10):
        is_sh = u < 2
        e = u - 2
        if 2 <= u <= 7:
            ctx.dma("gp", wgu_b[u], wgu_d[u], "swb%d" % u)
            ctx.dma("gp", wdn_b[u], wdn_d[u], "swb%d" % u)
            ctx.mk(("swb_e", u), "swb%d" % u)
        if not is_sh:
            ctx.w("sync", "scmb", ("cmbw"))
            if u > 2:
                ctx.w("sync", "pe", ("cbmm", u - 1))
            ctx.dma("sync", Cf[0:1, CROW2:CROW2 + OWN], combd_d[e:e + 1, :], "scmb")
            ctx.mk(("crl", u), "scmb")
            for sub in range(NSUB):
                ctx.w("pe", "scmb", ("crl", u))
                if sub > 0:
                    ctx.w("pe", "act", ("cbe", u, sub - 1))
                elif u > 2:
                    ctx.w("pe", "dve", ("dnall", u - 1))
                    ctx.w("pe", "act", ("cbe", u - 1, NSUB - 1))
                ctx.mm(pf[6][:, :], ones_r,
                       Cf[0:1, CROW2 + 512 * sub:CROW2 + 512 * sub + 512], inc=True)
                ctx.mk(("cbm", u, sub), "pe")
                ctx.w("act", "pe", ("cbm", u, sub))
                if u > 2:
                    ctx.w("act", "dve", ("dnall", u - 1))
                ctx.acopy(ic(CBW + 512 * sub, 512), pf[6][:, :], inc=True)
                ctx.mk(("cbe", u, sub), "act")
            ctx.mk(("cbmm", u), "pe")

        for p in range(11):
            buf = p % 2
            wg = ib(WGU[buf], 4096)
            if p >= 2:
                ctx.w("sync", "pe", ("guc", u, p - 2))
            elif u > 0:
                ctx.w("sync", "pe", ("guc", u - 1, 9 + p))
            else:
                ctx.w("sync", "pe", ("flashdone", HPC - 1))
            if is_sh:
                ctx.w("sync", "swbs", ("swb_sh"))
            else:
                ctx.w("sync", "swb%d" % e, ("swb_e", e))
            src = wsgu_b[11 * u + p] if is_sh else wgu_b[e, p]
            for half in range(2):
                ctx.dma("sync",
                        wg[:, :].rearrange("p (kc c) -> p kc c", kc=16)[:, 8 * half:8 * half + 8],
                        src[8 * half:8 * half + 8].rearrange("kc p c -> p kc c"),
                        "swf%d" % (p % 2))
            ctx.mk(("gul", u, p), "swf%d" % (p % 2))

            for sub in range(NSUB):
                gb, ub_ = pf[0 + 2 * (p % 2)], pf[1 + 2 * (p % 2)]
                ctx.w("pe", "swf%d" % (p % 2), ("gul", u, p))
                ctx.w("pe", "sA", ("sA_all"))
                if sub > 0:
                    ctx.w("pe", "act", ("silu", u, p, sub - 1))
                    ctx.w("pe", "dve", ("actm", u, p, sub - 1))
                elif p >= 2:
                    ctx.w("pe", "act", ("silu", u, p - 2, NSUB - 1))
                    ctx.w("pe", "dve", ("actm", u, p - 2, NSUB - 1))
                elif u > 0:
                    ctx.w("pe", "act", ("silu", u - 1, 9 + p, NSUB - 1))
                    ctx.w("pe", "dve", ("actm", u - 1, 9 + p, NSUB - 1))
                for kc in range(16):
                    ctx.mm(gb[:, :], wg[:, 256 * kc:256 * kc + 128],
                           A[:, kc * OWN + 512 * sub:kc * OWN + 512 * sub + 512],
                           start=(kc == 0), stop=(kc == 15), inc=(kc == 15))
                ctx.mk(("gumm", u, p, sub), "pe")
                for kc in range(16):
                    ctx.mm(ub_[:, :], wg[:, 256 * kc + 128:256 * kc + 256],
                           A[:, kc * OWN + 512 * sub:kc * OWN + 512 * sub + 512],
                           start=(kc == 0), stop=(kc == 15), inc=(kc == 15))
                ctx.mk(("upmm", u, p, sub), "pe")
                ctx.w("act", "pe", ("gumm", u, p, sub))
                if (u, p, sub) == (0, 0, 0):
                    ctx.w("act", "scmb", ("cmbw"))
                if sub > 0:
                    ctx.w("act", "dve", ("actm", u, p, sub - 1))
                elif p >= 2:
                    ctx.w("act", "dve", ("actm", u, p - 2, NSUB - 1))
                elif u > 0:
                    ctx.w("act", "dve", ("actm", u - 1, 9 + p, NSUB - 1))
                sgo = ic(SG[p % 2], 512)
                ctx.act(sgo, gb[:, :], AF.Silu, inc=True)
                ctx.mk(("silu", u, p, sub), "act")
                ctx.w("dve", "act", ("silu", u, p, sub))
                ctx.w("dve", "pe", ("upmm", u, p, sub))
                if u == 0 and p < 2:
                    ctx.w("dve", "pe", ("flashdone", HPC - 1))
                ctx.dve("tensor_mul",
                        B[:, ACT4 + p * OWN + 512 * sub:ACT4 + p * OWN + 512 * sub + 512],
                        sgo, ub_[:, :], inc=True)
                ctx.mk(("actm", u, p, sub), "dve")
            ctx.mk(("guc", u, p), "pe")
        ctx.mk(("actall", u), "dve")

        for q in range(4):
            buf = q % 2
            wd = ib(WDN[buf], 5632)
            if q >= 2:
                ctx.w("sync", "pe", ("dnc", u, q - 2))
            elif u > 0:
                ctx.w("sync", "pe", ("dnc", u - 1, 2 + q))
            else:
                ctx.w("sync", "pe", ("flashdone", HPC - 1))
            if is_sh:
                ctx.w("sync", "swbs", ("swb_sh"))
            else:
                ctx.w("sync", "swb%d" % e, ("swb_e", e))
            src = wsdn_b[u, q] if is_sh else wdn_b[e, q]
            ctx.dma("sync", wd[:, :].rearrange("p (fc c) -> p fc c", fc=11)[:, 0:6],
                    src[0:6].rearrange("fc p c -> p fc c"), "swd%d" % (q % 2))
            ctx.dma("sync", wd[:, :].rearrange("p (fc c) -> p fc c", fc=11)[:, 6:11],
                    src[6:11].rearrange("fc p c -> p fc c"), "swd%d" % (q % 2))
            ctx.mk(("dnl", u, q), "swd%d" % (q % 2))

            for hcl in range(4):
                hc = 4 * q + hcl
                for sub in range(NSUB):
                    pidx = hcl * NSUB + sub
                    bank = pf[4 + pidx % 2]
                    ctx.w("pe", "swd%d" % (q % 2), ("dnl", u, q))
                    ctx.w("pe", "dve", ("actall", u))
                    pv = None
                    if pidx >= 2:
                        pv = ("dnev", u, q, pidx - 2)
                    elif q > 0:
                        pv = ("dnev", u, q - 1, 4 * NSUB - 2 + pidx)
                    elif u > 0:
                        pv = ("dnev", u - 1, 3, 4 * NSUB - 2 + pidx)
                    if pv is not None:
                        ctx.w("pe", "dve", pv)
                    for fc in range(11):
                        ctx.mm(bank[:, :],
                               wd[:, 512 * fc + 128 * hcl:512 * fc + 128 * hcl + 128],
                               B[:, ACT4 + fc * OWN + 512 * sub:ACT4 + fc * OWN + 512 * sub + 512],
                               start=(fc == 0), stop=(fc == 10), inc=(fc == 10))
                    ctx.mk(("dnmm", u, q, pidx), "pe")
                    if pidx == 4 * NSUB - 1:
                        ctx.mk(("dnc", u, q), "pe")
                    stg = ic(STG[pidx % 4], 512)
                    di = ydma[0]
                    ssem = "syt%d" % (di % 4)
                    ctx.w("dve", "pe", ("dnmm", u, q, pidx))
                    if not is_sh:
                        ctx.w("dve", "act", ("cbe", u, sub))
                    if di >= 4:
                        ctx.w("dve", ssem, ("ytc", di - 4))
                    if di == 0:
                        ctx.w("dve", "scmb", ("cmbw"))
                    if is_sh:
                        ctx.dve("tensor_copy", stg, bank[:, :], inc=True)
                    else:
                        ctx.dve("tensor_mul", stg, bank[:, :], ic(CBW + 512 * sub, 512),
                                inc=True)
                    ctx.mk(("dnev", u, q, pidx), "dve")
                    ctx.w("gp", "dve", ("dnev", u, q, pidx))
                    if u > 0 and di == ctx.marks.get(("ytfu", u), -1):
                        for i in range(4):
                            ctx.w("gp", "syt%d" % i, ("ytallS", u - 1, i))
                    ctx.dma("gp", ytacc_d[128 * hc:128 * hc + 128, 512 * sub:512 * sub + 512],
                            stg, ssem, accum=None if u == 0 else ALU.add)
                    ctx.mk(("ytc", di), ssem)
                    ydma[0] += 1
        for i in range(4):
            ctx.mk(("ytallS", u, i), "syt%d" % i)
        if ctx.mode == "count":
            ctx.marks[("ytfu", u + 1)] = ydma[0]
        ctx.mk(("dnall", u), "dve")

    for i in range(4):
        ctx.mk(("ytT", i), "syt%d" % i)
    ctx.mk(("swfT", 0), "swf0")
    ctx.mk(("swfT", 1), "swf1")
    ctx.mk(("swdT", 0), "swd0")
    ctx.mk(("swdT", 1), "swd1")

    for e in ["pe", "act", "dve"]:
        ctx.mk(("b45", e), e)
    for e in ["pe", "act", "dve", "sync"]:
        for o in ["pe", "act", "dve"]:
            if o != e:
                ctx.w(e, o, ("b45", o))

    if UPTO <= 4:
        for i in range(4):
            ctx.w("sync", "syt%d" % i, ("ytT", i))
        _finish_stub(ctx, T)
        return
    Bf = B[:, :].bitcast(F32)
    EXB = [0, 2048]
    ATKB = [4096, 6144]
    EOB = [8192, 10240]
    YTB = [8192, 8704]
    TMPB = [9216, 9344]

    NTI = OWN // 128
    ctx.w("sync", "scc", 1)
    ctx.dma("sync", xloc_d[0:OWN, :],
            (lambda: x_d[:, :][bass.ts(ctx.pidval % 4, OWN), :]), "sloc")
    for rp in range(4):
        ctx.dma("sync", agloc_d[rp, 0:OWN, :],
                (lambda rp=rp: agout_d[rp * T:4 * T, :][
                    bass.ts(ctx.pidval % 4, OWN), :]), "sloc")
    ctx.mk("sloc_all", "sloc")
    ctx.w("sync", "sloc", ("sloc_all"))
    for i in range(4):
        ctx.w("sync", "syt%d" % i, ("ytT", i))
    ctx.w("sync", "swf0", ("swfT", 0))
    ctx.w("sync", "swf1", ("swfT", 1))
    ctx.w("sync", "swd0", ("swdT", 0))
    ctx.w("sync", "swd1", ("swdT", 1))
    for gi in range(NTI):
        if gi >= 2:
            ctx.w("sync", "dve", ("o2", gi - 2, 15))
        ctx.dma("sync", Bf[:, EXB[gi % 2]:EXB[gi % 2] + 2048],
                xloc_d[128 * gi:128 * gi + 128, :], "sepx%d" % (gi % 2))
        ctx.mk(("exl", gi), "sepx%d" % (gi % 2))
        for rp in range(4):
            ctx.dma("sync",
                    Bf[:, ATKB[gi % 2] + 512 * rp:ATKB[gi % 2] + 512 * rp + 512],
                    agloc_d[rp, 128 * gi:128 * gi + 128, :], "sepa%d" % (gi % 2))
        ctx.mk(("atk", gi), "sepa%d" % (gi % 2))
        for k4 in range(4):
            ki = 4 * gi + k4
            ybuf = ic(YTB[ki % 2], 512)
            if ki >= 2:
                ctx.w("sync", "pe", ("ytpl", ki - 2))
            ctx.dma("sync", ybuf.rearrange("p (k t) -> p k t", k=4),
                    ytacc_d[512 * k4:512 * k4 + 512, 128 * gi:128 * gi + 128]
                    .rearrange("(k p) t -> p k t", p=128), "sepy%d" % (ki % 2))
            ctx.mk(("ytl", ki), "sepy%d" % (ki % 2))
        def etp(kc):
            tg = 16 * gi + kc
            pslot = pf[tg % 2][:, 128 * ((tg // 2) % 4):128 * ((tg // 2) % 4) + 128]
            ctx.w("pe", "sepy%d" % ((4 * gi + kc // 4) % 2), ("ytl", 4 * gi + kc // 4))
            if tg >= 2:
                ctx.w("pe", "dve", ("o1", (tg - 2) // 16, (tg - 2) % 16))
            ctx.tp(pslot, ic(YTB[(4 * gi + kc // 4) % 2] + 128 * (kc % 4), 128),
                   idn_f, inc=True)
            ctx.mk(("ytp", tg), "pe")
            if kc % 4 == 3:
                ctx.mk(("ytpl", 4 * gi + kc // 4), "pe")
            return pslot

        slots = {}
        def eo1(kc):
            tg = 16 * gi + kc
            pslot = pf[tg % 2][:, 128 * ((tg // 2) % 4):128 * ((tg // 2) % 4) + 128]
            ctx.w("dve", "pe", ("ytp", tg))
            ctx.w("dve", "sepa%d" % (gi % 2), ("atk", gi))
            if kc >= 2:
                ctx.w("dve", "dve", ("o2", gi, kc - 2))
            ctx.dve("tensor_add", ic(TMPB[kc % 2], 128), pslot,
                    Bf[:, ATKB[gi % 2] + 128 * kc:ATKB[gi % 2] + 128 * kc + 128], inc=True)
            ctx.mk(("o1", gi, kc), "dve")

        def eo2(kc):
            ctx.w("dve", "dve", ("o1", gi, kc))
            ctx.w("dve", "sepx%d" % (gi % 2), ("exl", gi))
            if kc == 0 and gi >= 2:
                ctx.w("dve", "sout%d" % (gi % 2), ("outd", gi - 2))
            ctx.dve("tensor_add", Bf[:, EOB[gi % 2] + 128 * kc:EOB[gi % 2] + 128 * kc + 128],
                    ic(TMPB[kc % 2], 128),
                    Bf[:, EXB[gi % 2] + 128 * kc:EXB[gi % 2] + 128 * kc + 128], inc=True)
            ctx.mk(("o2", gi, kc), "dve")

        for kc in range(16):
            etp(kc)
            if kc >= 1:
                eo1(kc - 1)
            if kc >= 2:
                eo2(kc - 2)
        eo1(15)
        eo2(14)
        eo2(15)
        ctx.w("sync", "dve", ("o2", gi, 15))
        ctx.dma("sync", out_d[128 * gi:128 * gi + 128, :],
                Bf[:, EOB[gi % 2]:EOB[gi % 2] + 2048], "sout%d" % (gi % 2))
        ctx.mk(("outd", gi), "sout%d" % (gi % 2))
    ctx.w("sync", "sout0", ctx.cnt["sout0"])
    ctx.w("sync", "sout1", ctx.cnt["sout1"])


def build_program(T):
    OWN = T // 4
    NW = T // 512
    nc = bass.Bass("TRN2", target_bir_lowering=False, debug=False)

    h = {"groups": [[0, 1, 2, 3], [4, 5, 6, 7]]}

    def din(name, shp):
        return nc.dram_tensor(name, shp, F32, kind="ExternalInput").ap()

    h["x_d"] = din("x", [T, C])
    h["wqT_d"] = din("wqT", [C, 512])
    h["wkT_d"] = din("wkT", [C, 512])
    h["wvT_d"] = din("wvT", [C, 512])
    h["tau_d"] = din("tau", [1])
    h["nu_d"] = din("nu", [1])
    h["gw_d"] = din("gw", [16, 128, 8])
    h["wgu_d"] = din("wgu", [NE, 11, 16, 128, 256])
    h["wdn_d"] = din("wdn", [NE, 4, 11, 128, 512])
    h["wsgu_d"] = din("wsgu", [22, 16, 128, 256])
    h["wsdn_d"] = din("wsdn", [2, 4, 11, 128, 512])
    h["cmask_d"] = din("cmask", [128, 4, 512])
    h["idn_d"] = din("idn", [128, 128])
    h["out_d"] = nc.dram_tensor("out", [OWN, C], F32, kind="ExternalOutput").ap()

    xnt = nc.dram_tensor("xnt", [NW * 16 + 1, 128, 512], BF16, kind="Internal")
    h["xnt_d"] = xnt.ap()
    h["xnt_flat"] = xnt.ap()
    h["agin_d"] = nc.dram_tensor("agin", [T, 512], F32, kind="Internal").ap()
    h["agout_d"] = nc.dram_tensor("agout", [4 * T, 512], F32, kind="Internal").ap()
    h["combd_d"] = nc.dram_tensor("combd", [NE, OWN], F32, kind="Internal").ap()
    h["ytacc_d"] = nc.dram_tensor("ytacc", [C, OWN], F32, kind="Internal").ap()
    h["agloc_d"] = nc.dram_tensor("agloc", [4, OWN + 128, 512], F32, kind="Internal").ap()
    h["wgu_b"] = nc.dram_tensor("wgu_b", [NE, 11, 16, 128, 256], BF16, kind="Internal").ap()
    h["wdn_b"] = nc.dram_tensor("wdn_b", [NE, 4, 11, 128, 512], BF16, kind="Internal").ap()
    h["wsgu_b"] = nc.dram_tensor("wsgu_b", [22, 16, 128, 256], BF16, kind="Internal").ap()
    h["wsdn_b"] = nc.dram_tensor("wsdn_b", [2, 4, 11, 128, 512], BF16, kind="Internal").ap()
    h["xloc_d"] = nc.dram_tensor("xloc", [OWN + 128, C], F32, kind="Internal").ap()

    BCOLS = max(3 * T + 16384, 8192 + 2 * 5632 + 11 * OWN)
    CCOLS = 10240

    es = ExitStack()
    with es:
        h["A"] = es.enter_context(nc.sbuf_tensor("arnA", [128, 16 * OWN], BF16))[:, :]
        h["B"] = es.enter_context(nc.sbuf_tensor("arnB", [128, BCOLS], BF16))[:, :]
        h["C"] = es.enter_context(nc.sbuf_tensor("arnC", [128, CCOLS], F32))[:, :]
        for nm, shp, dt in [("idn_f", [128, 128], F32), ("idn_b", [128, 128], BF16),
                            ("ones_r", [1, 128], F32), ("ones_c", [128, 1], BF16),
                            ("ones_cf", [128, 1], F32),
                            ("stats", [128, 96], F32), ("rk_all", [128, 64], F32),
                            ("maskb", [128, 2048], BF16), ("natau", [128, 1], F32),
                            ("epsln", [128, 1], F32), ("tau11", [1, 1], F32),
                            ("nu11", [1, 1], F32), ("gw_sb", [128, 128], BF16),
                            ("wq_sb", [128, 2048], BF16), ("wk_sb", [128, 2048], BF16),
                            ("wv_sb", [128, 2048], BF16)]:
            h[nm] = es.enter_context(nc.sbuf_tensor(nm, shp, dt))[:, :]
        pf = []
        for i in range(7):
            pf.append(es.enter_context(nc.psum_tensor("pf%d" % i, [128, 512], F32))[:, :])
        h["pf"] = pf
        h["pb"] = es.enter_context(nc.psum_tensor("pb", [128, 1024], BF16))[:, :]
        for s in SEMS:
            h[s] = es.enter_context(nc.semaphore(s))

        marks = {}
        cc = Ctx(nc, "count", h, marks)
        emit_all(cc, T)
        final = dict(cc.cnt)

        blk = es.enter_context(nc.Block())

        def run(mode, pid=None):
            c2 = Ctx(nc, mode, h, marks)
            if pid is not None:
                c2.pidval = pid
            emit_all(c2, T)
            for k in SEMS:
                assert c2.cnt[k] == final[k], (mode, k, c2.cnt[k], final[k])

        @blk.tensor
        def _(t):
            run("pe")

        @blk.scalar
        def _(s):
            run("act")

        @blk.vector
        def _(v):
            run("dve")

        @blk.sync
        def _(s):
            pid = nc.partition_id(engines=[mybir.EngineType.SP])
            run("sync", pid=pid)

        @blk.gpsimd
        def _(g):
            run("gp")

    return nc


def kernel(L0_attn_x, L0_attn_wq, L0_attn_wk, L0_attn_wv, L0_attn_tau, L0_attn_nu,
           L0_ffn_gate_weight, L0_ffn_gate_up_weights, L0_ffn_down_weights,
           L0_ffn_shared_gate_up, L0_ffn_shared_down):
    x = np.asarray(L0_attn_x)
    Bn, T, _ = x.shape
    OWN = T // 4

    wq = np.asarray(L0_attn_wq)
    wk = np.asarray(L0_attn_wk)
    wv = np.asarray(L0_attn_wv)
    tau = np.asarray(L0_attn_tau).reshape(1).astype(np.float32)
    nu = np.asarray(L0_attn_nu).reshape(1).astype(np.float32)
    gwt = np.ascontiguousarray(np.asarray(L0_ffn_gate_weight).T).reshape(16, 128, 8)
    guw = np.asarray(L0_ffn_gate_up_weights)
    dw = np.asarray(L0_ffn_down_weights)
    sgu = np.asarray(L0_ffn_shared_gate_up)
    sdn = np.asarray(L0_ffn_shared_down)

    Ag = guw.transpose(0, 2, 1).reshape(NE, 16, 128, 22, 128)
    wgu = np.ascontiguousarray(
        np.stack([Ag[:, :, :, 0:11, :], Ag[:, :, :, 11:22, :]], axis=4)
        .transpose(0, 3, 1, 2, 4, 5).reshape(NE, 11, 16, 128, 256))
    Dd = dw.transpose(0, 2, 1).reshape(NE, 11, 128, 4, 512)
    wdn = np.ascontiguousarray(Dd.transpose(0, 3, 1, 2, 4))
    Sg = np.ascontiguousarray(sgu.T).reshape(16, 128, 44, 128)
    wsgu = np.ascontiguousarray(
        np.stack([Sg[:, :, 0:22, :], Sg[:, :, 22:44, :]], axis=3)
        .transpose(2, 0, 1, 3, 4).reshape(22, 16, 128, 256))
    Sd = np.ascontiguousarray(sdn.T).reshape(2, 11, 128, 4, 512)
    wsdn = np.ascontiguousarray(Sd.transpose(0, 3, 1, 2, 4))

    p = np.arange(128)[:, None]
    f = np.arange(512)[None, :]
    cmask = np.stack([(128 * d + p <= f).astype(np.float32) for d in range(4)], axis=1)
    cmask = np.ascontiguousarray(cmask)
    idn = np.eye(128, dtype=np.float32)

    _enable_ldw_opt()
    nc = build_program(T)

    in_maps = []
    for c in range(NCORES):
        b = c // 4
        r = c % 4
        in_maps.append(dict(
            x=np.ascontiguousarray(x[b]),
            wqT=np.ascontiguousarray(wq.T[:, 512 * r:512 * r + 512]),
            wkT=np.ascontiguousarray(wk.T[:, 512 * r:512 * r + 512]),
            wvT=np.ascontiguousarray(wv.T[:, 512 * r:512 * r + 512]),
            tau=tau, nu=nu, gw=gwt, wgu=wgu, wdn=wdn, wsgu=wsgu, wsdn=wsdn,
            cmask=cmask, idn=idn,
        ))
    res = run_bass_kernel_spmd(nc, in_maps, core_ids=list(range(NCORES)))
    out = np.empty((Bn, T, C), dtype=np.float32)
    for c in range(NCORES):
        b, r = c // 4, c % 4
        out[b, r * OWN:(r + 1) * OWN, :] = res.results[c]["out"]
    return out


# version 11 (final)
# speedup vs baseline: 1.0781x; 1.0572x over previous
import numpy as np
from contextlib import ExitStack

import concourse.bass as bass
import concourse.mybir as mybir
from concourse.bass_utils import run_bass_kernel_spmd


def _enable_ldw_opt():
    try:
        from concourse.compiler_utils import get_compiler_flags, set_compiler_flags
        flags = get_compiler_flags()
        newflags = [f.replace("--enable-ldw-opt=false", "--enable-ldw-opt=true")
                    for f in flags]
        set_compiler_flags(newflags)
    except Exception:
        pass

F32 = mybir.dt.float32
BF16 = mybir.dt.bfloat16
AF = mybir.ActivationFunctionType
ALU = mybir.AluOpType
AXX = mybir.AxisListType.X

C = 2048
D = 128
HPC = 4
NE = 8
NCORES = 8

SEMS = (["pe", "act", "dve", "scst", "sxnw", "sA", "swat", "scmb", "scc",
         "sgin", "gpc"]
        + ["sxb%d" % i for i in range(4)] + ["sxw%d" % i for i in range(2)]
        + ["swf%d" % i for i in range(2)] + ["swd%d" % i for i in range(2)]
        + ["sag%d" % i for i in range(2)] + ["syt%d" % i for i in range(4)]
        + ["sepx%d" % i for i in range(2)] + ["sepa%d" % i for i in range(2)]
        + ["sepy%d" % i for i in range(2)] + ["sout%d" % i for i in range(2)]
        + ["sloc", "swbs"] + ["swb%d" % i for i in range(8)])


class Ctx:

    def __init__(self, nc, mode, h, marks):
        self.nc = nc
        self.mode = mode
        self.h = h
        self.marks = marks
        self.cnt = {k: 0 for k in SEMS}
        self.pidval = 0

    def mk(self, key, sem):
        v = self.cnt[sem]
        if self.mode == "count":
            self.marks[key] = v
        else:
            assert self.marks[key] == v, (key, self.marks[key], v, self.mode)
        return v

    def m(self, key):
        return self.marks[key]

    def w(self, eng, sem, val):
        if self.mode != eng:
            return
        if isinstance(val, (tuple, str)):
            val = self.marks[val]
        if val is None or val <= 0:
            return
        self._eng(eng).wait_ge(self.h[sem], val)

    def _eng(self, eng):
        nc = self.nc
        return {"pe": nc.tensor, "act": nc.scalar, "dve": nc.vector,
                "sync": nc.sync, "gp": nc.gpsimd}[eng]

    def _fin(self, inst, sem, amount):
        self.cnt[sem] += amount
        if inst is not None:
            inst.then_inc(self.h[sem], amount)

    def mm(self, out, lhsT, rhs, start=True, stop=True, inc=False):
        inst = None
        if self.mode == "pe":
            inst = self.nc.tensor.matmul(out, lhsT, rhs, start=start, stop=stop)
        if inc:
            self._fin(inst, "pe", 1)

    def tp(self, out, in_, idn, inc=False):
        inst = None
        if self.mode == "pe":
            inst = self.nc.tensor.transpose(out, in_, idn)
        if inc:
            self._fin(inst, "pe", 1)

    def act(self, out, in_, func, bias=0.0, scale=1.0, accum=None, inc=False):
        inst = None
        if self.mode == "act":
            kw = {}
            if accum is not None:
                kw["accum_out"] = accum
            inst = self.nc.scalar.activation(out, in_, func, bias=bias, scale=scale, **kw)
        if inc:
            self._fin(inst, "act", 1)

    def acopy(self, out, in_, inc=False):
        inst = None
        if self.mode == "act":
            inst = self.nc.scalar.copy(out, in_)
        if inc:
            self._fin(inst, "act", 1)

    def amul(self, out, in_, mulv, inc=False):
        inst = None
        if self.mode == "act":
            inst = self.nc.scalar.mul(out, in_, mulv)
        if inc:
            self._fin(inst, "act", 1)

    def dve(self, meth, *args, inc=False, **kw):
        inst = None
        if self.mode == "dve":
            inst = getattr(self.nc.vector, meth)(*args, **kw)
        if inc:
            self._fin(inst, "dve", 1)

    def dma(self, queue, out, in_, sem, accum=None):
        inst = None
        if self.mode == queue:
            if callable(out):
                out = out()
            if callable(in_):
                in_ = in_()
            eng = self._eng(queue)
            try:
                if accum is None:
                    inst = eng.dma_start(out=out, in_=in_)
                else:
                    inst = eng.dma_start(out=out, in_=in_, accum_op=accum)
            except Exception as e:
                import traceback
                print("DMA FAIL: out=", out, " in=", in_)
                tb = e.__traceback__
                while tb is not None:
                    print("  frame:", tb.tb_frame.f_code.co_filename, tb.tb_lineno,
                          tb.tb_frame.f_code.co_name)
                    tb = tb.tb_next
                raise
        self._fin(inst, sem, 16)

    def coll(self, kind, op, groups, ins, outs, sem):
        inst = None
        if self.mode == "gp":
            inst = self.nc.gpsimd.collective_compute(kind, op, replica_groups=groups,
                                                     ins=ins, outs=outs)
        self.cnt[sem] += 1
        if inst is not None:
            inst.then_inc(self.h[sem])

    def gmemset(self, ap, val, sem=None):
        inst = None
        if self.mode == "gp":
            inst = self.nc.gpsimd.memset(ap, val)
        if sem is not None:
            self._fin(inst, sem, 1)


UPTO = 9


def _finish_stub(ctx, T):
    OWN = T // 4
    out_d = ctx.h["out_d"]
    B = ctx.h["B"]
    for e in ["pe", "act", "dve"]:
        for o in ["pe", "act", "dve"]:
            if o != e:
                ctx.w(e, o, ctx.cnt[o])
    for i in range(OWN // 128):
        ctx.dma("sync", out_d[128 * i:128 * i + 128, :],
                B[:, 0:4096].bitcast(mybir.dt.float32), "sout0")
    ctx.w("sync", "sout0", ctx.cnt["sout0"])


def emit_all(ctx, T):
    h = ctx.h
    OWN = T // 4
    NW = T // 512
    OWNW = OWN // 512
    NSUB = OWN // 512
    NQB = T // 512
    NPAIRS = NQB // 2
    NEPI = OWN // 256
    NT_OWN = OWN // 128

    A = h["A"]; B = h["B"]; Cf = h["C"]
    Cb = Cf[:, :].bitcast(BF16)
    pf = h["pf"]; pb = h["pb"]
    idn_f = h["idn_f"]; idn_b = h["idn_b"]
    ones_r = h["ones_r"]; ones_c = h["ones_c"]; ones_cf = h["ones_cf"]
    stats = h["stats"]; rk_all = h["rk_all"]; maskb = h["maskb"]
    natau = h["natau"]; epsln = h["epsln"]
    tau11 = h["tau11"]; nu11 = h["nu11"]
    gw_sb = h["gw_sb"]; wq_sb = h["wq_sb"]; wk_sb = h["wk_sb"]; wv_sb = h["wv_sb"]

    x_d = h["x_d"]; wqT_d = h["wqT_d"]; wkT_d = h["wkT_d"]; wvT_d = h["wvT_d"]
    tau_d = h["tau_d"]; nu_d = h["nu_d"]; gw_d = h["gw_d"]
    wgu_d = h["wgu_d"]; wdn_d = h["wdn_d"]; wsgu_d = h["wsgu_d"]; wsdn_d = h["wsdn_d"]
    cmask_d = h["cmask_d"]; idn_d = h["idn_d"]
    xnt_d = h["xnt_d"]; agin_d = h["agin_d"]; agout_d = h["agout_d"]
    combd_d = h["combd_d"]; ytacc_d = h["ytacc_d"]; out_d = h["out_d"]
    agloc_d = h["agloc_d"]; xloc_d = h["xloc_d"]
    wgu_b = h["wgu_b"]; wdn_b = h["wdn_b"]; wsgu_b = h["wsgu_b"]; wsdn_b = h["wsdn_b"]
    groups = h["groups"]

    ib = lambda off, n: B[:, off:off + n]
    ic = lambda off, n: Cf[:, off:off + n]
    icb = lambda f32off, n: Cb[:, 2 * f32off:2 * f32off + n]

    XB = [0, 2048, 4096, 6144]; TRASH = 8192
    PT = [0, 256, 512, 768]
    CSB = [1024, 1536]
    ASTG = [2048, 2560]
    RQ, RV, RK, CRW = 3072, 3584, 4096, 4608
    Q2, K2, V2, VTMP = 5632, 5888, 6144, 6400
    RQS, RVS = 6656, 7168
    AST2 = [7680, 8192]
    SG = [0, 512]
    STG = [1024, 1536, 2048, 2560]
    CBW = 3072
    LTOK, STOK, CMBT = 5120, 5248, 5376
    LSB = 5504
    CROW2 = 5504
    CMBS = 0
    EX = [0, 2048]; EO = [4096, 6144]
    EAG = [8192, 8448]; EYT = [8704, 8960]; ES1 = [9216, 9472]

    QT, KT, VT = 0, T, 2 * T
    XW = [3 * T, 3 * T + 8192]
    STAGE1 = 0
    XN1 = [8192, 10240, 12288, 14336]
    WGU = [0, 4096]
    WDN = [8192, 8192 + 5632]
    ACT4 = 8192 + 2 * 5632

    ctx.dma("sync", h["idn_f"], idn_d[:, :], "scst")
    ctx.dma("sync", h["tau11"], tau_d[:].unsqueeze(0), "scst")
    ctx.dma("sync", h["nu11"], nu_d[:].unsqueeze(0), "scst")
    ctx.mk(("p0sx"), "scst")
    ctx.dma("gp", h["idn_b"], idn_d[:, :], "sgin")
    ctx.dma("gp", maskb[:, :].rearrange("p (d c) -> p d c", d=4), cmask_d[:, :, :], "sgin")
    ctx.dma("gp", gw_sb[:, :].rearrange("p (kc e) -> p kc e", kc=16),
            gw_d[:, :, :].rearrange("kc p e -> p kc e"), "sgin")
    ctx.mk("p0gp", "sgin")
    ctx.gmemset(ones_r, 1.0, sem="gpc")
    ctx.gmemset(ones_c, 1.0, sem="gpc")
    ctx.gmemset(ones_cf, 1.0, sem="gpc")
    ctx.gmemset(epsln, 1.0e-5, sem="gpc")
    ctx.mk("p0gpc", "gpc")

    ctx.w("act", "scst", ("p0sx"))
    ctx.act(stats[0:1, 60:61], tau11, AF.Abs, inc=True)
    ctx.mk("a_abs", "act")
    ctx.w("act", "act", ("a_abs"))
    ctx.amul(stats[0:1, 61:62], stats[0:1, 60:61], -1.0, inc=True)
    ctx.mk("a_neg", "act")
    ctx.w("pe", "act", ("a_neg"))
    ctx.w("pe", "sgin", ("p0gp"))
    ctx.w("pe", "gpc", ("p0gpc"))
    ctx.mm(pf[0][:, 0:1], ones_r[0:1, :], stats[0:1, 61:62], inc=True)
    ctx.mk("p_nat", "pe")
    ctx.w("act", "pe", ("p_nat"))
    ctx.acopy(natau, pf[0][:, 0:1], inc=True)
    ctx.mk("a_nat", "act")

    if UPTO <= 0:
        _finish_stub(ctx, T)
        return

    tpg = [0]

    for w in range(NW):
        for tt in range(4):
            gi = 4 * w + tt
            xb = ic(XB[tt], 2048)
            if gi >= 4:
                ctx.w("sync", "dve", ("xn", gi - 4))
            ctx.dma("sync", xb, x_d[512 * w + 128 * tt:512 * w + 128 * tt + 128, :],
                    "sxb%d" % tt)
            ctx.mk(("xld", gi), "sxb%d" % tt)
        for tt in range(4):
            gi = 4 * w + tt
            xb = ic(XB[tt], 2048)
            st = stats[:, 64 + 8 * tt:64 + 8 * tt + 8]
            ctx.w("dve", "sxb%d" % tt, ("xld", gi))
            if gi >= 4:
                ctx.w("dve", "dve", ("xn", gi - 4))
            ctx.dve("reduce_sum", st[:, 0:1], xb, axis=AXX, inc=True)
            ctx.mk(("sum", gi), "dve")
            ctx.w("act", "sxb%d" % tt, ("xld", gi))
            if gi >= 4:
                ctx.w("act", "dve", ("var", gi - 4))
            if gi >= 1:
                ctx.w("act", "act", ("ssq", gi - 1))
            ctx.act(ic(TRASH, 2048), xb, AF.Square, accum=st[:, 1:2], inc=True)
            ctx.mk(("ssq", gi), "act")
        for tt in range(4):
            gi = 4 * w + tt
            st = stats[:, 64 + 8 * tt:64 + 8 * tt + 8]
            ctx.w("dve", "dve", ("sum", gi))
            ctx.dve("tensor_scalar", st[:, 2:3], st[:, 0:1], 1.0 / C, None, ALU.mult, inc=True)
            ctx.mk(("mean", gi), "dve")
        for tt in range(4):
            gi = 4 * w + tt
            st = stats[:, 64 + 8 * tt:64 + 8 * tt + 8]
            ctx.w("dve", "dve", ("mean", gi))
            ctx.dve("tensor_mul", st[:, 3:4], st[:, 2:3], st[:, 2:3], inc=True)
            ctx.mk(("mm2", gi), "dve")
        for tt in range(4):
            gi = 4 * w + tt
            st = stats[:, 64 + 8 * tt:64 + 8 * tt + 8]
            ctx.w("dve", "act", ("ssq", gi))
            ctx.w("dve", "dve", ("mm2", gi))
            ctx.dve("scalar_tensor_tensor", st[:, 4:5], st[:, 1:2], 1.0 / C, st[:, 3:4],
                    ALU.mult, ALU.subtract, inc=True)
            ctx.mk(("var", gi), "dve")
        for tt in range(4):
            gi = 4 * w + tt
            st = stats[:, 64 + 8 * tt:64 + 8 * tt + 8]
            ctx.w("act", "dve", ("var", gi))
            if gi == 0:
                ctx.w("act", "gpc", ("p0gpc"))
            ctx.act(st[:, 5:6], st[:, 4:5], AF.Sqrt, bias=epsln, inc=True)
            ctx.mk(("sqv", gi), "act")
        for tt in range(4):
            gi = 4 * w + tt
            st = stats[:, 64 + 8 * tt:64 + 8 * tt + 8]
            ctx.w("dve", "act", ("sqv", gi))
            ctx.dve("reciprocal", st[:, 6:7], st[:, 5:6], inc=True)
            ctx.mk(("rstd", gi), "dve")
        for tt in range(4):
            gi = 4 * w + tt
            st = stats[:, 64 + 8 * tt:64 + 8 * tt + 8]
            xb = ic(XB[tt], 2048)
            xnb = ib(XN1[tt], 2048)
            ctx.w("dve", "dve", ("rstd", gi))
            if w > 0 and UPTO != 21:
                ctx.w("dve", "pe", ("tpk", w - 1, 15, tt))
            ctx.dve("tensor_scalar", xnb, xb, st[:, 2:3], st[:, 6:7],
                    ALU.subtract, ALU.mult, inc=True)
            ctx.mk(("xn", gi), "dve")
        pbF6 = pf[6][:, :].bitcast(BF16)
        for kc in range(16 if UPTO != 21 else 0):
            for tt in range(4):
                gi = 4 * w + tt
                g = tpg[0]; tpg[0] += 1
                grp = g // 8
                if UPTO == 23:
                    bankap = pb
                else:
                    bankap = pb if grp % 2 == 0 else pbF6
                sl = bankap[:, 128 * (g % 8):128 * (g % 8) + 128]
                ctx.w("pe", "dve", ("xn", gi))
                if UPTO == 23:
                    if grp >= 1 and g % 8 == 0:
                        ctx.w("pe", "dve", ("tpgv", grp - 1))
                elif grp >= 2 and g % 8 == 0:
                    ctx.w("pe", "dve", ("tpgv", grp - 2))
                ctx.tp(sl, ib(XN1[tt], 2048)[:, 128 * kc:128 * kc + 128], idn_b, inc=True)
                ctx.mk(("tpk", w, kc, tt), "pe")
                if g % 8 == 7:
                    ctx.mk(("tpgp", grp), "pe")
                dst = ib(STAGE1 + kc * 512 + tt * 128, 128)
                ctx.w("dve", "pe", ("tpgp", grp))
                if w > 0 and g % 64 < 1 and UPTO not in (21, 22, 23):
                    ctx.w("dve", "sxnw", ("xnwm", w - 1))
                ctx.dve("tensor_copy", dst, sl, inc=True)
                ctx.mk(("tpev", g), "dve")
                if g % 8 == 7:
                    ctx.mk(("tpgv", grp), "dve")
                ctx.mk(("ev", w, kc, tt), "dve")
        for kc in range(16 if UPTO not in (21, 22, 23) else 0):
            ctx.w("sync", "dve", ("ev", w, kc, 3))
            ctx.dma("sync", xnt_d[w * 16 + kc, :, :], ib(STAGE1 + kc * 512, 512), "sxnw")
        ctx.mk(("xnwm", w), "sxnw")
    ctx.mk("xnw_all", "sxnw")
    for i in range(4):
        ctx.mk(("xbt", i), "sxb%d" % i)

    ctx.w("sync", "sxnw", ("xnw_all"))
    for wl in range(OWNW if UPTO not in (11, 21, 22, 23) else 0):
        ctx.dma("sync",
                (lambda wl=wl: A[:, :].rearrange("p (kc t) -> p kc t", kc=16)
                 [:, :, wl * 512:wl * 512 + 512]),
                (lambda wl=wl: ctx.h["xnt_flat"][16 * wl:16 * NW + 1][
                    bass.ts((ctx.pidval % 4) * OWNW, 16), :, :]
                 .rearrange("k p c -> p k c")),
                "sA")
    ctx.mk("sA_all", "sA")

    if UPTO <= 1 or UPTO in (11, 21, 22, 23):
        _finish_stub(ctx, T)
        return
    for e in ["pe", "act", "dve"]:
        ctx.mk(("b12", e), e)
    for e in ["pe", "act", "dve"]:
        for o in ["pe", "act", "dve"]:
            if o != e:
                ctx.w(e, o, ("b12", o))
        for i in range(4):
            ctx.w(e, "sxb%d" % i, ("xbt", i))
        ctx.w(e, "sgin", ("p0gp"))
        ctx.w(e, "gpc", ("p0gpc"))

    for hh in range(HPC):
        for wi, (wsb, wd) in enumerate([(wq_sb, wqT_d), (wk_sb, wkT_d), (wv_sb, wvT_d)]):
            if hh > 0:
                ctx.w("gp", "pe", ("projdone", hh - 1))
            for half in range(2):
                ctx.dma("gp",
                        wsb[:, :].rearrange("p (kc d) -> p kc d", kc=16)[:, 8 * half:8 * half + 8],
                        wd[:, 128 * hh:128 * hh + 128]
                        .rearrange("(kc p) d -> p kc d", p=128)[:, 8 * half:8 * half + 8],
                        "swat")
        ctx.mk(("wat", hh), "swat")


        def chain(w):
            qb, kb, vb = pf[3 * (w % 2)], pf[3 * (w % 2) + 1], pf[3 * (w % 2) + 2]
            ctx.w("act", "pe", ("projmm", hh, w))
            if (hh, w) != (0, 0):
                pw = (hh, w - 1) if w > 0 else (hh - 1, NW - 1)
                ctx.w("act", "pe", ("ssmm",) + pw)
            ctx.act(icb(Q2, 512), qb[:, :], AF.Square, inc=True)
            ctx.act(icb(K2, 512), kb[:, :], AF.Square, inc=True)
            ctx.act(icb(V2, 512), vb[:, :], AF.Square, inc=True)
            ctx.mk(("sq", hh, w), "act")

            ctx.w("pe", "act", ("sq", hh, w))
            if (hh, w) != (0, 0):
                pw = (hh, w - 1) if w > 0 else (hh - 1, NW - 1)
                ctx.w("pe", "act", ("Rve",) + pw)
            ctx.mm(pf[6][0:1, :], ones_c, icb(Q2, 512), inc=True)
            ctx.mm(pf[6][32:33, :], ones_c, icb(K2, 512), inc=True)
            ctx.mm(pf[6][64:65, :], ones_c, icb(V2, 512), inc=True)
            ctx.mk(("ssmm", hh, w), "pe")

            ctx.w("act", "pe", ("ssmm", hh, w))
            if (hh, w) != (0, 0):
                pw = (hh, w - 1) if w > 0 else (hh - 1, NW - 1)
                ctx.w("act", "dve", ("rows",) + pw)
            for rp_ in (0, 32, 64):
                ctx.act(Cf[rp_:rp_ + 1, RQ:RQ + 512], pf[6][rp_:rp_ + 1, :], AF.Ln,
                        inc=(rp_ == 64))
            ctx.mk(("rln", hh, w), "act")
            ctx.w("act", "act", ("rln", hh, w))
            for rp_ in (0, 32, 64):
                ctx.act(Cf[rp_:rp_ + 1, RQ:RQ + 512], Cf[rp_:rp_ + 1, RQ:RQ + 512],
                        AF.Exp, scale=-0.5, inc=(rp_ == 64))
            ctx.mk(("rxp", hh, w), "act")
            ctx.w("dve", "act", ("rxp", hh, w))
            ctx.w("dve", "scst", ("p0sx"))
            ctx.dve("tensor_scalar", Cf[0:1, RQ:RQ + 512], Cf[0:1, RQ:RQ + 512],
                    tau11, None, ALU.mult, inc=True)
            ctx.mk(("rows", hh, w), "dve")

            ctx.w("pe", "dve", ("rows", hh, w))
            ctx.mm(pf[6][:, :], ones_r[0:1, :], Cf[0:1, RQ:RQ + 512], inc=True)
            ctx.mk(("Rq", hh, w), "pe")
            ctx.w("act", "pe", ("Rq", hh, w))
            if (hh, w) != (0, 0):
                pw = (hh, w - 1) if w > 0 else (hh - 1, NW - 1)
                ctx.w("act", "dve", ("qTev",) + pw)
            ctx.acopy(ic(RQS, 512), pf[6][:, :], inc=True)
            ctx.mk(("Rqe", hh, w), "act")
            ctx.w("pe", "act", ("Rqe", hh, w))
            ctx.mm(pf[6][:, :], ones_r[64:65, :], Cf[64:65, RQ:RQ + 512], inc=True)
            ctx.mk(("Rv", hh, w), "pe")
            ctx.w("act", "pe", ("Rv", hh, w))
            if (hh, w) != (0, 0):
                pw = (hh, w - 1) if w > 0 else (hh - 1, NW - 1)
                ctx.w("act", "dve", ("vsc",) + pw)
            ctx.acopy(ic(RVS, 512), pf[6][:, :], inc=True)
            ctx.mk(("Rve", hh, w), "act")

            ctx.w("dve", "act", ("Rqe", hh, w))
            if hh > 0 and w == 0:
                ctx.w("dve", "pe", ("flashdone", hh - 1))
            ctx.dve("tensor_mul", B[:, QT + 512 * w:QT + 512 * w + 512], qb[:, :],
                    ic(RQS, 512), inc=True)
            ctx.mk(("qTev", hh, w), "dve")
            ctx.w("act", "pe", ("ssmm", hh, w))
            if hh > 0 and w == 0:
                ctx.w("act", "pe", ("flashdone", hh - 1))
            ctx.acopy(B[:, KT + 512 * w:KT + 512 * w + 512], kb[:, :], inc=True)
            ctx.mk(("kTev", hh, w), "act")
            ctx.w("dve", "act", ("Rve", hh, w))
            ctx.dve("tensor_mul", icb(VTMP, 512), vb[:, :], ic(RVS, 512), inc=True)
            ctx.mk(("vsc", hh, w), "dve")

            for dd in range(4):
                ctx.w("pe", "dve", ("vsc", hh, w))
                if dd == 0:
                    if (hh, w) != (0, 0):
                        pw = (hh, w - 1) if w > 0 else (hh - 1, NW - 1)
                        ctx.w("pe", "act", ("rke",) + pw)
                        ctx.w("pe", "dve", ("vte",) + pw + (3,))
                    if hh > 0 and w == 0:
                        ctx.w("pe", "dve", ("crec0", hh - 1, NPAIRS - 1, 1))
                ctx.tp(pb[:, 128 * dd:128 * dd + 128], icb(VTMP + 64 * dd, 128), idn_b,
                       inc=True)
                ctx.mk(("vt", hh, w, dd), "pe")
            for dd in range(4):
                ctx.w("dve", "pe", ("vt", hh, w, 3))
                if hh > 0 and w == 0:
                    ctx.w("dve", "pe", ("flashdone", hh - 1))
                ctx.dve("tensor_copy",
                        B[:, VT + 512 * w + 128 * dd:VT + 512 * w + 128 * dd + 128],
                        pb[:, 128 * dd:128 * dd + 128], inc=True)
                ctx.mk(("vte", hh, w, dd), "dve")
            pbF2 = pb[:, :].bitcast(F32)
            for dd in range(4):
                ctx.w("pe", "dve", ("rows", hh, w))
                if dd == 0:
                    ctx.w("pe", "dve", ("vte", hh, w, 3))
                ctx.tp(pbF2[:, 384 + dd:384 + dd + 1],
                       Cf[32:33, RQ + 128 * dd:RQ + 128 * dd + 128],
                       ones_r[32:33, 0:1], inc=True)
                ctx.mk(("rkt", hh, w, dd), "pe")
            ctx.w("act", "pe", ("rkt", hh, w, 3))
            if hh > 0 and w == 0:
                ctx.w("act", "pe", ("flashdone", hh - 1))
            for dd in range(4):
                ctx.acopy(rk_all[:, 4 * w + dd:4 * w + dd + 1],
                          pbF2[:, 384 + dd:384 + dd + 1], inc=(dd == 3))
            ctx.mk(("rke", hh, w), "act")

        for w in range(NW):
            ctx.w("sync", "sxnw", ("xnw_all"))
            if w >= 2:
                ctx.w("sync", "pe", ("projmm", hh, w - 2))
            elif hh > 0:
                ctx.w("sync", "pe", ("projmm", hh - 1, NW - 2 + w))
            ctx.dma("sync", ib(XW[w % 2], 8192).rearrange("p (kc c) -> p kc c", kc=16),
                    xnt_d[16 * w:16 * w + 16].rearrange("kc p c -> p kc c"),
                    "sxw%d" % (w % 2))
            ctx.mk(("xwl", hh, w), "sxw%d" % (w % 2))

            xw = ib(XW[w % 2], 8192)
            ctx.w("pe", "sxw%d" % (w % 2), ("xwl", hh, w))
            ctx.w("pe", "swat", ("wat", hh))
            if w >= 2:
                ctx.w("pe", "dve", ("qTev", hh, w - 2))
                ctx.w("pe", "act", ("kTev", hh, w - 2))
                ctx.w("pe", "dve", ("vsc", hh, w - 2))
            elif hh > 0:
                ctx.w("pe", "dve", ("qTev", hh - 1, NW - 2 + w))
                ctx.w("pe", "act", ("kTev", hh - 1, NW - 2 + w))
                ctx.w("pe", "dve", ("vsc", hh - 1, NW - 2 + w))
                for bk in range(2):
                    ctx.w("pe", "act", ("Cev", hh - 1, NPAIRS - 1, bk))
                    ctx.w("pe", "dve", ("attn", hh - 1, NPAIRS - 1, bk))
                    ctx.w("pe", "dve", ("crec", hh - 1, NPAIRS - 1, bk))
                    ctx.w("pe", "act", ("attnTe", hh - 1, NPAIRS - 1, bk))
            for (bi, wsb) in [(0, wq_sb), (1, wk_sb), (2, wv_sb)]:
                bank = pf[3 * (w % 2) + bi]
                for kc in range(16):
                    ctx.mm(bank[:, :], wsb[:, 128 * kc:128 * kc + 128],
                           xw[:, 512 * kc:512 * kc + 512],
                           start=(kc == 0), stop=(kc == 15), inc=(kc == 15))
            ctx.mk(("projmm", hh, w), "pe")
            if w > 0:
                chain(w - 1)
        chain(NW - 1)
        ctx.mk(("projdone", hh), "pe")

        st_banks = [[pf[0], pf[1]], [pf[2], pf[3]]]
        out_banks = [pf[4], pf[5]]
        pbF = pb[:, :].bitcast(F32)
        lrow = [pf[6][0:1, :], pbF[0:1, 0:512]]

        for P in range(NPAIRS):
            B0, B1 = 2 * P, 2 * P + 1
            JA = 4 * B0 + 4
            JB = 4 * B1 + 4
            blocks = [B0, B1]

            if P == 0:
                ctx.w("pe", "dve", ("vte", hh, NW - 1, 3))
                ctx.w("pe", "act", ("rke", hh, NW - 1))
                ctx.w("pe", "act", ("kTev", hh, NW - 1))
                ctx.w("pe", "dve", ("qTev", hh, NW - 1))
                ctx.w("pe", "act", ("a_nat"))
                if hh > 0:
                    for bk in range(2):
                        ctx.w("pe", "act", ("Cev", hh - 1, NPAIRS - 1, bk))
            else:
                for bk in range(2):
                    ctx.w("pe", "act", ("Cev", hh, P - 1, bk))
            prevfin = (hh, P - 1) if P > 0 else ((hh - 1, NPAIRS - 1) if hh > 0 else None)

            def active(bk, j):
                return j < (JA if bk == 0 else JB)

            def pt_ap(bk, j):
                idx = 2 * bk + (j % 2)
                return icb(PT[idx], 512)

            def emit_st(j):
                for bk in range(2):
                    if not active(bk, j):
                        continue
                    ctx.mm(st_banks[bk][j % 2][:, :],
                           B[:, KT + 128 * j:KT + 128 * j + 128],
                           B[:, QT + 512 * blocks[bk]:QT + 512 * blocks[bk] + 512],
                           inc=(bk == 1 or not active(1, j)))
                ctx.mk(("st", hh, P, j), "pe")

            def emit_exp(j):
                ctx.w("act", "pe", ("st", hh, P, j))
                if j >= 2:
                    ctx.w("act", "pe", ("pv", hh, P, j - 2))
                for bk in range(2):
                    if not active(bk, j):
                        continue
                    ctx.act(pt_ap(bk, j), st_banks[bk][j % 2][:, :], AF.Exp,
                            bias=natau, scale=rk_all[:, j:j + 1], inc=True)
                ctx.mk(("exp", hh, P, j), "act")
                for bk in range(2):
                    if not active(bk, j):
                        continue
                    dlt = j - 4 * blocks[bk]
                    if 0 <= dlt < 4:
                        ctx.w("dve", "act", ("exp", hh, P, j))
                        ctx.dve("tensor_mul", pt_ap(bk, j), pt_ap(bk, j),
                                maskb[:, 512 * dlt:512 * dlt + 512], inc=True)
                        ctx.mk(("msk", hh, P, j, bk), "dve")

            def emit_fin(bk):
                blkq = blocks[bk]
                crw = CRW + 512 * bk
                last = (JA - 1) if bk == 0 else (JB - 1)
                ctx.w("dve", "pe", ("pv", hh, P, last))
                if P > 0:
                    ctx.w("dve", "dve", ("crec", hh, P - 1, bk))
                elif hh > 0:
                    ctx.w("dve", "dve", ("crec", hh - 1, NPAIRS - 1, bk))
                ctx.dve("reciprocal", Cf[0:1, crw:crw + 512], lrow[bk], inc=True)
                ctx.mk(("crec0", hh, P, bk), "dve")
                ctx.w("dve", "dve", ("crec0", hh, P, bk))
                ctx.dve("tensor_scalar", Cf[0:1, crw:crw + 512], Cf[0:1, crw:crw + 512],
                        nu11, None, ALU.mult, inc=True)
                ctx.mk(("crec", hh, P, bk), "dve")
                ctx.w("pe", "dve", ("crec", hh, P, bk))
                ctx.mm(st_banks[bk][0][:, :], ones_r[0:1, :], Cf[0:1, crw:crw + 512], inc=True)
                ctx.mk(("Cbc", hh, P, bk), "pe")
                ctx.w("act", "pe", ("Cbc", hh, P, bk))
                if P > 0:
                    ctx.w("act", "dve", ("attn", hh, P - 1, bk))
                elif hh > 0:
                    ctx.w("act", "dve", ("attn", hh - 1, NPAIRS - 1, bk))
                ctx.acopy(ic(CSB[bk], 512), st_banks[bk][0][:, :], inc=True)
                ctx.mk(("Cev", hh, P, bk), "act")
                ctx.w("dve", "act", ("Cev", hh, P, bk))
                prevtp = ("attnT", hh, P - 1, bk) if P > 0 else (
                    ("attnT", hh - 1, NPAIRS - 1, bk) if hh > 0 else None)
                if prevtp is not None:
                    ctx.w("dve", "pe", prevtp)
                ctx.dve("tensor_mul", ic(ASTG[bk], 512), out_banks[bk][:, :],
                        ic(CSB[bk], 512), inc=True)
                ctx.mk(("attn", hh, P, bk), "dve")
                pfT = out_banks[bk]
                for cidx in range(4):
                    ctx.w("pe", "dve", ("attn", hh, P, bk))
                    ctx.tp(pfT[:, 128 * cidx:128 * cidx + 128],
                           ic(ASTG[bk] + 128 * cidx, 128), idn_f, inc=(cidx == 3))
                ctx.mk(("attnT", hh, P, bk), "pe")
                ctx.w("act", "pe", ("attnT", hh, P, bk))
                prevdma2 = ("agdma", hh, P - 1, bk) if P > 0 else (
                    ("agdma", hh - 1, NPAIRS - 1, bk) if hh > 0 else None)
                if prevdma2 is not None:
                    ctx.w("act", "sag%d" % bk, prevdma2)
                for cidx in range(4):
                    ctx.acopy(ic(AST2[bk] + 128 * cidx, 128),
                              pfT[:, 128 * cidx:128 * cidx + 128], inc=(cidx == 3))
                ctx.mk(("attnTe", hh, P, bk), "act")
                for cidx in range(4):
                    ctx.w("sync", "act", ("attnTe", hh, P, bk))
                    ctx.dma("sync",
                            agin_d[512 * blkq + 128 * cidx:512 * blkq + 128 * cidx + 128,
                                   128 * hh:128 * hh + 128],
                            ic(AST2[bk] + 128 * cidx, 128), "sag%d" % bk)
                ctx.mk(("agdma", hh, P, bk), "sag%d" % bk)

            def emit_pv(j):
                if j == 0 and prevfin is not None:
                    for bk in range(2):
                        ctx.w("pe", "dve", ("attn",) + prevfin + (bk,))
                        ctx.w("pe", "dve", ("crec",) + prevfin + (bk,))
                        ctx.w("pe", "act", ("attnTe",) + prevfin + (bk,))
                ctx.w("pe", "act", ("exp", hh, P, j))
                for bk in range(2):
                    if not active(bk, j):
                        continue
                    dlt = j - 4 * blocks[bk]
                    if 0 <= dlt < 4:
                        ctx.w("pe", "dve", ("msk", hh, P, j, bk))
                    last = (JA - 1) if bk == 0 else (JB - 1)
                    ctx.mm(out_banks[bk][:, :], B[:, VT + 128 * j:VT + 128 * j + 128],
                           pt_ap(bk, j), start=(j == 0), stop=(j == last))
                    ctx.mm(lrow[bk], ones_c, pt_ap(bk, j),
                           start=(j == 0), stop=(j == last), inc=True)
                ctx.mk(("pv", hh, P, j), "pe")
                for bk in range(2):
                    last = (JA - 1) if bk == 0 else (JB - 1)
                    if j == last:
                        emit_fin(bk)

            emit_st(0)
            if JB > 1:
                emit_st(1)
            for j in range(JB):
                emit_exp(j)
                emit_pv(j)
                if j + 2 < JB:
                    emit_st(j + 2)
        if hh == HPC - 1:
            ctx.w("gp", "pe", ("st", hh, 0, 0))
            ctx.dma("gp", wsgu_b[:, :, :, :], wsgu_d[:, :, :, :], "swbs")
            ctx.dma("gp", wsdn_b[:, :, :, :, :], wsdn_d[:, :, :, :, :], "swbs")
            ctx.mk("swb_sh", "swbs")
            for e_m in (0, 1):
                ctx.dma("gp", wgu_b[e_m], wgu_d[e_m], "swb%d" % e_m)
                ctx.dma("gp", wdn_b[e_m], wdn_d[e_m], "swb%d" % e_m)
                ctx.mk(("swb_e", e_m), "swb%d" % e_m)
        ctx.mk(("flashdone", hh), "pe")

    ctx.mk(("agT", 0), "sag0")
    ctx.mk(("agT", 1), "sag1")
    if UPTO <= 2:
        for e in ["pe", "act", "dve"]:
            ctx.mk(("b24", e), e)
        _finish_stub(ctx, T)
        return
    ctx.w("gp", "sag0", ("agT", 0))
    ctx.w("gp", "sag1", ("agT", 1))
    ctx.coll("AllGather", ALU.bypass, groups, [agin_d[:, :]], [agout_d[:, :]], "scc")

    for e in ["pe", "act", "dve"]:
        ctx.mk(("b24", e), e)
    if UPTO <= 3:
        ctx.w("sync", "scc", 1)
        _finish_stub(ctx, T)
        return
    for e in ["pe", "act", "dve", "gp", "sync"]:
        for o in ["pe", "act", "dve"]:
            if o != e:
                ctx.w(e, o, ("b24", o))

    for sub in range(NSUB):
        ctx.w("pe", "sgin", ("p0gp"))
        ctx.w("pe", "gpc", ("p0gpc"))
        ctx.w("pe", "sA", ("sA_all"))
        if sub > 0:
            ctx.w("pe", "act", ("lgev", sub - 1))
        for kc in range(16):
            ctx.mm(pf[4][0:8, :], gw_sb[:, 8 * kc:8 * kc + 8],
                   A[:, kc * OWN + 512 * sub:kc * OWN + 512 * sub + 512],
                   start=(kc == 0), stop=(kc == 15), inc=(kc == 15))
        ctx.mk(("lgmm", sub), "pe")
        ctx.w("act", "pe", ("lgmm", sub))
        ctx.acopy(Cf[0:8, LSB + 512 * sub:LSB + 512 * sub + 512], pf[4][0:8, :], inc=True)
        ctx.mk(("lgev", sub), "act")
    for tt in range(NT_OWN):
        ctx.w("pe", "act", ("lgev", (tt * 128) // 512))
        ctx.tp(pf[5][:, 8 * tt:8 * tt + 8], Cf[0:8, LSB + 128 * tt:LSB + 128 * tt + 128],
               idn_f[0:8, 0:8], inc=True)
        ctx.mk(("lgt", tt), "pe")
    ctx.w("dve", "pe", ("lgt", NT_OWN - 1))
    ctx.dve("tensor_copy", ic(LTOK, 8 * NT_OWN), pf[5][:, 0:8 * NT_OWN], inc=True)
    ctx.mk("ltok", "dve")
    sc = stats
    for tt in range(NT_OWN):
        ctx.w("dve", "dve", ("ltok"))
        ctx.dve("reduce_max", sc[:, 16 + tt:17 + tt], ic(LTOK + 8 * tt, 8),
                axis=AXX, negate=True, inc=True)
        ctx.mk(("nmx", tt), "dve")
    for tt in range(NT_OWN):
        ctx.w("act", "dve", ("nmx", tt))
        ctx.act(ic(STOK + 8 * tt, 8), ic(LTOK + 8 * tt, 8), AF.Exp,
                bias=sc[:, 16 + tt:17 + tt], accum=sc[:, 32 + tt:33 + tt], inc=True)
        ctx.mk(("sfe", tt), "act")
    for tt in range(NT_OWN):
        ctx.w("dve", "act", ("sfe", tt))
        ctx.dve("reciprocal", sc[:, 48 + tt:49 + tt], sc[:, 32 + tt:33 + tt], inc=True)
        ctx.mk(("srec", tt), "dve")
    for tt in range(NT_OWN):
        ctx.w("dve", "dve", ("srec", tt))
        ctx.dve("tensor_scalar", ic(STOK + 8 * tt, 8), ic(STOK + 8 * tt, 8),
                sc[:, 48 + tt:49 + tt], None, ALU.mult, inc=True)
        ctx.mk(("snrm", tt), "dve")
    for nm, args in [
        ("t2a", None), ("t2b", None), ("t2c", None), ("t2d", None),
        ("t2e", None), ("t2f", None), ("t2g", None)]:
        for tt in range(NT_OWN):
            if nm == "t2a":
                ctx.w("dve", "dve", ("snrm", tt))
                ctx.dve("reduce_max", sc[:, 16 + tt:17 + tt], ic(STOK + 8 * tt, 8),
                        axis=AXX, inc=True)
            elif nm == "t2b":
                ctx.w("dve", "dve", ("t2a", tt))
                ctx.dve("tensor_scalar", ic(CMBT + 8 * tt, 8), ic(STOK + 8 * tt, 8),
                        sc[:, 16 + tt:17 + tt], None, ALU.subtract, inc=True)
            elif nm == "t2c":
                ctx.w("dve", "dve", ("t2b", tt))
                ctx.dve("tensor_single_scalar", ic(CMBT + 8 * tt, 8), ic(CMBT + 8 * tt, 8),
                        0.0, ALU.is_equal, inc=True)
            elif nm == "t2d":
                ctx.w("dve", "dve", ("t2c", tt))
                ctx.dve("scalar_tensor_tensor", ic(CMBT + 8 * tt, 8), ic(CMBT + 8 * tt, 8),
                        -1.0e30, ic(STOK + 8 * tt, 8), ALU.mult, ALU.add, inc=True)
            elif nm == "t2e":
                ctx.w("dve", "dve", ("t2d", tt))
                ctx.dve("reduce_max", sc[:, 16 + tt:17 + tt], ic(CMBT + 8 * tt, 8),
                        axis=AXX, inc=True)
            elif nm == "t2f":
                ctx.w("dve", "dve", ("t2e", tt))
                ctx.dve("tensor_scalar", ic(CMBT + 8 * tt, 8), ic(STOK + 8 * tt, 8),
                        sc[:, 16 + tt:17 + tt], None, ALU.is_ge, inc=True)
            elif nm == "t2g":
                ctx.w("dve", "dve", ("t2f", tt))
                ctx.dve("tensor_mul", ic(CMBT + 8 * tt, 8), ic(CMBT + 8 * tt, 8),
                        ic(STOK + 8 * tt, 8), inc=True)
            ctx.mk((nm, tt), "dve")
    for ch in range(NT_OWN // 4):
        for tt4 in range(4):
            tt = 4 * ch + tt4
            ctx.w("pe", "dve", ("t2g", tt))
            if ch > 0 and tt4 == 0:
                ctx.w("pe", "dve", ("cmbev", ch - 1))
            ctx.tp(pf[6][0:8, 128 * tt4:128 * tt4 + 128], ic(CMBT + 8 * tt, 8),
                   idn_f, inc=True)
            ctx.mk(("cbt", tt), "pe")
        ctx.w("dve", "pe", ("cbt", 4 * ch + 3))
        ctx.dve("tensor_copy", Cf[0:8, CMBS + 512 * ch:CMBS + 512 * ch + 512],
                pf[6][0:8, 0:512], inc=True)
        ctx.mk(("cmbev", ch), "dve")
    ctx.mk("cmbev", "dve")
    ctx.w("sync", "dve", ("cmbev"))
    ctx.dma("sync", combd_d[:, :], Cf[0:8, CMBS:CMBS + OWN], "scmb")
    ctx.mk("cmbw", "scmb")

    ydma = [0]
    for u in range(10):
        is_sh = u < 2
        e = u - 2
        if 2 <= u <= 7:
            ctx.dma("gp", wgu_b[u], wgu_d[u], "swb%d" % u)
            ctx.dma("gp", wdn_b[u], wdn_d[u], "swb%d" % u)
            ctx.mk(("swb_e", u), "swb%d" % u)
        if not is_sh:
            ctx.w("sync", "scmb", ("cmbw"))
            if u > 2:
                ctx.w("sync", "pe", ("cbmm", u - 1))
            ctx.dma("sync", Cf[0:1, CROW2:CROW2 + OWN], combd_d[e:e + 1, :], "scmb")
            ctx.mk(("crl", u), "scmb")
            for sub in range(NSUB):
                ctx.w("pe", "scmb", ("crl", u))
                if sub > 0:
                    ctx.w("pe", "act", ("cbe", u, sub - 1))
                elif u > 2:
                    ctx.w("pe", "dve", ("dnall", u - 1))
                    ctx.w("pe", "act", ("cbe", u - 1, NSUB - 1))
                ctx.mm(pf[6][:, :], ones_r[0:1, :],
                       Cf[0:1, CROW2 + 512 * sub:CROW2 + 512 * sub + 512], inc=True)
                ctx.mk(("cbm", u, sub), "pe")
                ctx.w("act", "pe", ("cbm", u, sub))
                if u > 2:
                    ctx.w("act", "dve", ("dnall", u - 1))
                ctx.acopy(ic(CBW + 512 * sub, 512), pf[6][:, :], inc=True)
                ctx.mk(("cbe", u, sub), "act")
            ctx.mk(("cbmm", u), "pe")

        for p in range(11):
            buf = p % 2
            wg = ib(WGU[buf], 4096)
            if p >= 2:
                ctx.w("sync", "pe", ("guc", u, p - 2))
            elif u > 0:
                ctx.w("sync", "pe", ("guc", u - 1, 9 + p))
            else:
                ctx.w("sync", "pe", ("flashdone", HPC - 1))
            if is_sh:
                ctx.w("sync", "swbs", ("swb_sh"))
            else:
                ctx.w("sync", "swb%d" % e, ("swb_e", e))
            src = wsgu_b[11 * u + p] if is_sh else wgu_b[e, p]
            for half in range(2):
                ctx.dma("sync",
                        wg[:, :].rearrange("p (kc c) -> p kc c", kc=16)[:, 8 * half:8 * half + 8],
                        src[8 * half:8 * half + 8].rearrange("kc p c -> p kc c"),
                        "swf%d" % (p % 2))
            ctx.mk(("gul", u, p), "swf%d" % (p % 2))

            for sub in range(NSUB):
                gb, ub_ = pf[0 + 2 * (p % 2)], pf[1 + 2 * (p % 2)]
                ctx.w("pe", "swf%d" % (p % 2), ("gul", u, p))
                ctx.w("pe", "sA", ("sA_all"))
                if sub > 0:
                    ctx.w("pe", "act", ("silu", u, p, sub - 1))
                    ctx.w("pe", "dve", ("actm", u, p, sub - 1))
                elif p >= 2:
                    ctx.w("pe", "act", ("silu", u, p - 2, NSUB - 1))
                    ctx.w("pe", "dve", ("actm", u, p - 2, NSUB - 1))
                elif u > 0:
                    ctx.w("pe", "act", ("silu", u - 1, 9 + p, NSUB - 1))
                    ctx.w("pe", "dve", ("actm", u - 1, 9 + p, NSUB - 1))
                for kc in range(16):
                    ctx.mm(gb[:, :], wg[:, 256 * kc:256 * kc + 128],
                           A[:, kc * OWN + 512 * sub:kc * OWN + 512 * sub + 512],
                           start=(kc == 0), stop=(kc == 15), inc=(kc == 15))
                ctx.mk(("gumm", u, p, sub), "pe")
                for kc in range(16):
                    ctx.mm(ub_[:, :], wg[:, 256 * kc + 128:256 * kc + 256],
                           A[:, kc * OWN + 512 * sub:kc * OWN + 512 * sub + 512],
                           start=(kc == 0), stop=(kc == 15), inc=(kc == 15))
                ctx.mk(("upmm", u, p, sub), "pe")
                ctx.w("act", "pe", ("gumm", u, p, sub))
                if (u, p, sub) == (0, 0, 0):
                    ctx.w("act", "scmb", ("cmbw"))
                if sub > 0:
                    ctx.w("act", "dve", ("actm", u, p, sub - 1))
                elif p >= 2:
                    ctx.w("act", "dve", ("actm", u, p - 2, NSUB - 1))
                elif u > 0:
                    ctx.w("act", "dve", ("actm", u - 1, 9 + p, NSUB - 1))
                sgo = ic(SG[p % 2], 512)
                ctx.act(sgo, gb[:, :], AF.Silu, inc=True)
                ctx.mk(("silu", u, p, sub), "act")
                ctx.w("dve", "act", ("silu", u, p, sub))
                ctx.w("dve", "pe", ("upmm", u, p, sub))
                if u == 0 and p < 2:
                    ctx.w("dve", "pe", ("flashdone", HPC - 1))
                ctx.dve("tensor_mul",
                        B[:, ACT4 + p * OWN + 512 * sub:ACT4 + p * OWN + 512 * sub + 512],
                        sgo, ub_[:, :], inc=True)
                ctx.mk(("actm", u, p, sub), "dve")
            ctx.mk(("guc", u, p), "pe")
        ctx.mk(("actall", u), "dve")

        for q in range(4):
            buf = q % 2
            wd = ib(WDN[buf], 5632)
            if q >= 2:
                ctx.w("sync", "pe", ("dnc", u, q - 2))
            elif u > 0:
                ctx.w("sync", "pe", ("dnc", u - 1, 2 + q))
            else:
                ctx.w("sync", "pe", ("flashdone", HPC - 1))
            if is_sh:
                ctx.w("sync", "swbs", ("swb_sh"))
            else:
                ctx.w("sync", "swb%d" % e, ("swb_e", e))
            src = wsdn_b[u, q] if is_sh else wdn_b[e, q]
            ctx.dma("sync", wd[:, :].rearrange("p (fc c) -> p fc c", fc=11)[:, 0:6],
                    src[0:6].rearrange("fc p c -> p fc c"), "swd%d" % (q % 2))
            ctx.dma("sync", wd[:, :].rearrange("p (fc c) -> p fc c", fc=11)[:, 6:11],
                    src[6:11].rearrange("fc p c -> p fc c"), "swd%d" % (q % 2))
            ctx.mk(("dnl", u, q), "swd%d" % (q % 2))

            for hcl in range(4):
                hc = 4 * q + hcl
                for sub in range(NSUB):
                    pidx = hcl * NSUB + sub
                    bank = pf[4 + pidx % 2]
                    ctx.w("pe", "swd%d" % (q % 2), ("dnl", u, q))
                    ctx.w("pe", "dve", ("actall", u))
                    pv = None
                    if pidx >= 2:
                        pv = ("dnev", u, q, pidx - 2)
                    elif q > 0:
                        pv = ("dnev", u, q - 1, 4 * NSUB - 2 + pidx)
                    elif u > 0:
                        pv = ("dnev", u - 1, 3, 4 * NSUB - 2 + pidx)
                    if pv is not None:
                        ctx.w("pe", "dve", pv)
                    for fc in range(11):
                        ctx.mm(bank[:, :],
                               wd[:, 512 * fc + 128 * hcl:512 * fc + 128 * hcl + 128],
                               B[:, ACT4 + fc * OWN + 512 * sub:ACT4 + fc * OWN + 512 * sub + 512],
                               start=(fc == 0), stop=(fc == 10), inc=(fc == 10))
                    ctx.mk(("dnmm", u, q, pidx), "pe")
                    if pidx == 4 * NSUB - 1:
                        ctx.mk(("dnc", u, q), "pe")
                    stg = ic(STG[pidx % 4], 512)
                    di = ydma[0]
                    ssem = "syt%d" % (di % 4)
                    ctx.w("dve", "pe", ("dnmm", u, q, pidx))
                    if not is_sh:
                        ctx.w("dve", "act", ("cbe", u, sub))
                    if di >= 4:
                        ctx.w("dve", ssem, ("ytc", di - 4))
                    if di == 0:
                        ctx.w("dve", "scmb", ("cmbw"))
                    if is_sh:
                        ctx.dve("tensor_copy", stg, bank[:, :], inc=True)
                    else:
                        ctx.dve("tensor_mul", stg, bank[:, :], ic(CBW + 512 * sub, 512),
                                inc=True)
                    ctx.mk(("dnev", u, q, pidx), "dve")
                    ctx.w("gp", "dve", ("dnev", u, q, pidx))
                    if u > 0 and di == ctx.marks.get(("ytfu", u), -1):
                        for i in range(4):
                            ctx.w("gp", "syt%d" % i, ("ytallS", u - 1, i))
                    ctx.dma("gp", ytacc_d[128 * hc:128 * hc + 128, 512 * sub:512 * sub + 512],
                            stg, ssem, accum=None if u == 0 else ALU.add)
                    ctx.mk(("ytc", di), ssem)
                    ydma[0] += 1
        for i in range(4):
            ctx.mk(("ytallS", u, i), "syt%d" % i)
        if ctx.mode == "count":
            ctx.marks[("ytfu", u + 1)] = ydma[0]
        ctx.mk(("dnall", u), "dve")

    for i in range(4):
        ctx.mk(("ytT", i), "syt%d" % i)
    ctx.mk(("swfT", 0), "swf0")
    ctx.mk(("swfT", 1), "swf1")
    ctx.mk(("swdT", 0), "swd0")
    ctx.mk(("swdT", 1), "swd1")

    for e in ["pe", "act", "dve"]:
        ctx.mk(("b45", e), e)
    for e in ["pe", "act", "dve", "sync"]:
        for o in ["pe", "act", "dve"]:
            if o != e:
                ctx.w(e, o, ("b45", o))

    if UPTO <= 4:
        for i in range(4):
            ctx.w("sync", "syt%d" % i, ("ytT", i))
        _finish_stub(ctx, T)
        return
    Bf = B[:, :].bitcast(F32)
    EXB = [0, 2048]
    ATKB = [4096, 6144]
    EOB = [8192, 10240]
    YTB = [8192, 8704]
    TMPB = [9216, 9344]

    NTI = OWN // 128
    ctx.w("sync", "scc", 1)
    ctx.dma("sync", xloc_d[0:OWN, :],
            (lambda: x_d[:, :][bass.ts(ctx.pidval % 4, OWN), :]), "sloc")
    for rp in range(4):
        ctx.dma("sync", agloc_d[rp, 0:OWN, :],
                (lambda rp=rp: agout_d[rp * T:4 * T, :][
                    bass.ts(ctx.pidval % 4, OWN), :]), "sloc")
    ctx.mk("sloc_all", "sloc")
    ctx.w("sync", "sloc", ("sloc_all"))
    for i in range(4):
        ctx.w("sync", "syt%d" % i, ("ytT", i))
    ctx.w("sync", "swf0", ("swfT", 0))
    ctx.w("sync", "swf1", ("swfT", 1))
    ctx.w("sync", "swd0", ("swdT", 0))
    ctx.w("sync", "swd1", ("swdT", 1))
    for gi in range(NTI):
        if gi >= 2:
            ctx.w("sync", "dve", ("o2", gi - 2, 15))
        ctx.dma("sync", Bf[:, EXB[gi % 2]:EXB[gi % 2] + 2048],
                xloc_d[128 * gi:128 * gi + 128, :], "sepx%d" % (gi % 2))
        ctx.mk(("exl", gi), "sepx%d" % (gi % 2))
        for rp in range(4):
            ctx.dma("sync",
                    Bf[:, ATKB[gi % 2] + 512 * rp:ATKB[gi % 2] + 512 * rp + 512],
                    agloc_d[rp, 128 * gi:128 * gi + 128, :], "sepa%d" % (gi % 2))
        ctx.mk(("atk", gi), "sepa%d" % (gi % 2))
        for k4 in range(4):
            ki = 4 * gi + k4
            ybuf = ic(YTB[ki % 2], 512)
            if ki >= 2:
                ctx.w("sync", "pe", ("ytpl", ki - 2))
            ctx.dma("sync", ybuf.rearrange("p (k t) -> p k t", k=4),
                    ytacc_d[512 * k4:512 * k4 + 512, 128 * gi:128 * gi + 128]
                    .rearrange("(k p) t -> p k t", p=128), "sepy%d" % (ki % 2))
            ctx.mk(("ytl", ki), "sepy%d" % (ki % 2))
        def etp(kc):
            tg = 16 * gi + kc
            pslot = pf[tg % 2][:, 128 * ((tg // 2) % 4):128 * ((tg // 2) % 4) + 128]
            ctx.w("pe", "sepy%d" % ((4 * gi + kc // 4) % 2), ("ytl", 4 * gi + kc // 4))
            if tg >= 2:
                ctx.w("pe", "dve", ("o1", (tg - 2) // 16, (tg - 2) % 16))
            ctx.tp(pslot, ic(YTB[(4 * gi + kc // 4) % 2] + 128 * (kc % 4), 128),
                   idn_f, inc=True)
            ctx.mk(("ytp", tg), "pe")
            if kc % 4 == 3:
                ctx.mk(("ytpl", 4 * gi + kc // 4), "pe")
            return pslot

        slots = {}
        def eo1(kc):
            tg = 16 * gi + kc
            pslot = pf[tg % 2][:, 128 * ((tg // 2) % 4):128 * ((tg // 2) % 4) + 128]
            ctx.w("dve", "pe", ("ytp", tg))
            ctx.w("dve", "sepa%d" % (gi % 2), ("atk", gi))
            if kc >= 2:
                ctx.w("dve", "dve", ("o2", gi, kc - 2))
            ctx.dve("tensor_add", ic(TMPB[kc % 2], 128), pslot,
                    Bf[:, ATKB[gi % 2] + 128 * kc:ATKB[gi % 2] + 128 * kc + 128], inc=True)
            ctx.mk(("o1", gi, kc), "dve")

        def eo2(kc):
            ctx.w("dve", "dve", ("o1", gi, kc))
            ctx.w("dve", "sepx%d" % (gi % 2), ("exl", gi))
            if kc == 0 and gi >= 2:
                ctx.w("dve", "sout%d" % (gi % 2), ("outd", gi - 2))
            ctx.dve("tensor_add", Bf[:, EOB[gi % 2] + 128 * kc:EOB[gi % 2] + 128 * kc + 128],
                    ic(TMPB[kc % 2], 128),
                    Bf[:, EXB[gi % 2] + 128 * kc:EXB[gi % 2] + 128 * kc + 128], inc=True)
            ctx.mk(("o2", gi, kc), "dve")

        for kc in range(16):
            etp(kc)
            if kc >= 1:
                eo1(kc - 1)
            if kc >= 2:
                eo2(kc - 2)
        eo1(15)
        eo2(14)
        eo2(15)
        ctx.w("sync", "dve", ("o2", gi, 15))
        ctx.dma("sync", out_d[128 * gi:128 * gi + 128, :],
                Bf[:, EOB[gi % 2]:EOB[gi % 2] + 2048], "sout%d" % (gi % 2))
        ctx.mk(("outd", gi), "sout%d" % (gi % 2))
    ctx.w("sync", "sout0", ctx.cnt["sout0"])
    ctx.w("sync", "sout1", ctx.cnt["sout1"])


def build_program(T):
    OWN = T // 4
    NW = T // 512
    nc = bass.Bass("TRN2", target_bir_lowering=False, debug=False)

    h = {"groups": [[0, 1, 2, 3], [4, 5, 6, 7]]}

    def din(name, shp):
        return nc.dram_tensor(name, shp, F32, kind="ExternalInput").ap()

    h["x_d"] = din("x", [T, C])
    h["wqT_d"] = din("wqT", [C, 512])
    h["wkT_d"] = din("wkT", [C, 512])
    h["wvT_d"] = din("wvT", [C, 512])
    h["tau_d"] = din("tau", [1])
    h["nu_d"] = din("nu", [1])
    h["gw_d"] = din("gw", [16, 128, 8])
    h["wgu_d"] = din("wgu", [NE, 11, 16, 128, 256])
    h["wdn_d"] = din("wdn", [NE, 4, 11, 128, 512])
    h["wsgu_d"] = din("wsgu", [22, 16, 128, 256])
    h["wsdn_d"] = din("wsdn", [2, 4, 11, 128, 512])
    h["cmask_d"] = din("cmask", [128, 4, 512])
    h["idn_d"] = din("idn", [128, 128])
    h["out_d"] = nc.dram_tensor("out", [OWN, C], F32, kind="ExternalOutput").ap()

    xnt = nc.dram_tensor("xnt", [NW * 16 + 1, 128, 512], BF16, kind="Internal")
    h["xnt_d"] = xnt.ap()
    h["xnt_flat"] = xnt.ap()
    h["agin_d"] = nc.dram_tensor("agin", [T, 512], F32, kind="Internal").ap()
    h["agout_d"] = nc.dram_tensor("agout", [4 * T, 512], F32, kind="Internal").ap()
    h["combd_d"] = nc.dram_tensor("combd", [NE, OWN], F32, kind="Internal").ap()
    h["ytacc_d"] = nc.dram_tensor("ytacc", [C, OWN], F32, kind="Internal").ap()
    h["agloc_d"] = nc.dram_tensor("agloc", [4, OWN + 128, 512], F32, kind="Internal").ap()
    h["wgu_b"] = nc.dram_tensor("wgu_b", [NE, 11, 16, 128, 256], BF16, kind="Internal").ap()
    h["wdn_b"] = nc.dram_tensor("wdn_b", [NE, 4, 11, 128, 512], BF16, kind="Internal").ap()
    h["wsgu_b"] = nc.dram_tensor("wsgu_b", [22, 16, 128, 256], BF16, kind="Internal").ap()
    h["wsdn_b"] = nc.dram_tensor("wsdn_b", [2, 4, 11, 128, 512], BF16, kind="Internal").ap()
    h["xloc_d"] = nc.dram_tensor("xloc", [OWN + 128, C], F32, kind="Internal").ap()

    BCOLS = max(3 * T + 16384, 8192 + 2 * 5632 + 11 * OWN)
    CCOLS = 10240

    es = ExitStack()
    with es:
        h["A"] = es.enter_context(nc.sbuf_tensor("arnA", [128, 16 * OWN], BF16))[:, :]
        h["B"] = es.enter_context(nc.sbuf_tensor("arnB", [128, BCOLS], BF16))[:, :]
        h["C"] = es.enter_context(nc.sbuf_tensor("arnC", [128, CCOLS], F32))[:, :]
        for nm, shp, dt in [("idn_f", [128, 128], F32), ("idn_b", [128, 128], BF16),
                            ("ones_r", [128, 128], F32), ("ones_c", [128, 1], BF16),
                            ("ones_cf", [128, 1], F32),
                            ("stats", [128, 96], F32), ("rk_all", [128, 64], F32),
                            ("maskb", [128, 2048], BF16), ("natau", [128, 1], F32),
                            ("epsln", [128, 1], F32), ("tau11", [1, 1], F32),
                            ("nu11", [1, 1], F32), ("gw_sb", [128, 128], BF16),
                            ("wq_sb", [128, 2048], BF16), ("wk_sb", [128, 2048], BF16),
                            ("wv_sb", [128, 2048], BF16)]:
            h[nm] = es.enter_context(nc.sbuf_tensor(nm, shp, dt))[:, :]
        pf = []
        for i in range(7):
            pf.append(es.enter_context(nc.psum_tensor("pf%d" % i, [128, 512], F32))[:, :])
        h["pf"] = pf
        h["pb"] = es.enter_context(nc.psum_tensor("pb", [128, 1024], BF16))[:, :]
        for s in SEMS:
            h[s] = es.enter_context(nc.semaphore(s))

        marks = {}
        cc = Ctx(nc, "count", h, marks)
        emit_all(cc, T)
        final = dict(cc.cnt)

        blk = es.enter_context(nc.Block())

        def run(mode, pid=None):
            c2 = Ctx(nc, mode, h, marks)
            if pid is not None:
                c2.pidval = pid
            emit_all(c2, T)
            for k in SEMS:
                assert c2.cnt[k] == final[k], (mode, k, c2.cnt[k], final[k])

        @blk.tensor
        def _(t):
            run("pe")

        @blk.scalar
        def _(s):
            run("act")

        @blk.vector
        def _(v):
            run("dve")

        @blk.sync
        def _(s):
            pid = nc.partition_id(engines=[mybir.EngineType.SP])
            run("sync", pid=pid)

        @blk.gpsimd
        def _(g):
            run("gp")

    return nc


def kernel(L0_attn_x, L0_attn_wq, L0_attn_wk, L0_attn_wv, L0_attn_tau, L0_attn_nu,
           L0_ffn_gate_weight, L0_ffn_gate_up_weights, L0_ffn_down_weights,
           L0_ffn_shared_gate_up, L0_ffn_shared_down):
    x = np.asarray(L0_attn_x)
    Bn, T, _ = x.shape
    OWN = T // 4

    wq = np.asarray(L0_attn_wq)
    wk = np.asarray(L0_attn_wk)
    wv = np.asarray(L0_attn_wv)
    tau = np.asarray(L0_attn_tau).reshape(1).astype(np.float32)
    nu = np.asarray(L0_attn_nu).reshape(1).astype(np.float32)
    gwt = np.ascontiguousarray(np.asarray(L0_ffn_gate_weight).T).reshape(16, 128, 8)
    guw = np.asarray(L0_ffn_gate_up_weights)
    dw = np.asarray(L0_ffn_down_weights)
    sgu = np.asarray(L0_ffn_shared_gate_up)
    sdn = np.asarray(L0_ffn_shared_down)

    Ag = guw.transpose(0, 2, 1).reshape(NE, 16, 128, 22, 128)
    wgu = np.ascontiguousarray(
        np.stack([Ag[:, :, :, 0:11, :], Ag[:, :, :, 11:22, :]], axis=4)
        .transpose(0, 3, 1, 2, 4, 5).reshape(NE, 11, 16, 128, 256))
    Dd = dw.transpose(0, 2, 1).reshape(NE, 11, 128, 4, 512)
    wdn = np.ascontiguousarray(Dd.transpose(0, 3, 1, 2, 4))
    Sg = np.ascontiguousarray(sgu.T).reshape(16, 128, 44, 128)
    wsgu = np.ascontiguousarray(
        np.stack([Sg[:, :, 0:22, :], Sg[:, :, 22:44, :]], axis=3)
        .transpose(2, 0, 1, 3, 4).reshape(22, 16, 128, 256))
    Sd = np.ascontiguousarray(sdn.T).reshape(2, 11, 128, 4, 512)
    wsdn = np.ascontiguousarray(Sd.transpose(0, 3, 1, 2, 4))

    p = np.arange(128)[:, None]
    f = np.arange(512)[None, :]
    cmask = np.stack([(128 * d + p <= f).astype(np.float32) for d in range(4)], axis=1)
    cmask = np.ascontiguousarray(cmask)
    idn = np.eye(128, dtype=np.float32)

    _enable_ldw_opt()
    nc = build_program(T)

    in_maps = []
    for c in range(NCORES):
        b = c // 4
        r = c % 4
        in_maps.append(dict(
            x=np.ascontiguousarray(x[b]),
            wqT=np.ascontiguousarray(wq.T[:, 512 * r:512 * r + 512]),
            wkT=np.ascontiguousarray(wk.T[:, 512 * r:512 * r + 512]),
            wvT=np.ascontiguousarray(wv.T[:, 512 * r:512 * r + 512]),
            tau=tau, nu=nu, gw=gwt, wgu=wgu, wdn=wdn, wsgu=wsgu, wsdn=wsdn,
            cmask=cmask, idn=idn,
        ))
    res = run_bass_kernel_spmd(nc, in_maps, core_ids=list(range(NCORES)))
    out = np.empty((Bn, T, C), dtype=np.float32)
    for c in range(NCORES):
        b, r = c // 4, c % 4
        out[b, r * OWN:(r + 1) * OWN, :] = res.results[c]["out"]
    return out
